# version 04
# baseline (speedup 1.0000x reference)
import numpy as np
from contextlib import ExitStack
import concourse.bass as bass
import concourse.mybir as mybir
from concourse.bass_utils import run_bass_kernel_spmd
import ml_dtypes

F32 = mybir.dt.float32
BF16 = mybir.dt.bfloat16
I8 = mybir.dt.int8
AF = mybir.ActivationFunctionType
ALU = mybir.AluOpType
AX = mybir.AxisListType
BF = ml_dtypes.bfloat16

D = 2048
S = 4096
NT = D // 128
ASCALE = float(1.0 / np.sqrt(128.0))

ENGINES = ("tensor", "vector", "scalar", "gpsimd", "sync")
DMA_QUEUES = ("sync", "gpsimd", "scalar", "vector")


class Step:
    __slots__ = ("engine", "fn", "deps", "sem", "val", "is_dma", "n_dma", "tag")

    def __init__(self, engine, fn, deps, is_dma=False, n_dma=1, tag=""):
        self.engine = engine
        self.fn = fn
        self.deps = deps
        self.is_dma = is_dma
        self.n_dma = n_dma
        self.sem = None
        self.val = None
        self.tag = tag


class Buf:
    __slots__ = ("name", "last_w", "readers")

    def __init__(self, name):
        self.name = name
        self.last_w = None
        self.readers = []


class Prog:
    def __init__(self, nc: bass.Bass):
        self.nc = nc
        self.steps = []
        self.es = ExitStack()
        self._scopes = []
        self._sems = {}
        self._counts = {}
        self._barrier = None
        self._nbuf = 0

    def push_scope(self):
        self._scopes.append(ExitStack())

    def pop_scope(self):
        self._scopes.pop().close()

    def _stack(self):
        return self._scopes[-1] if self._scopes else self.es

    def sbuf(self, name, shape, dtype):
        return self._stack().enter_context(self.nc.sbuf_tensor(name, shape, dtype))

    def psum(self, name, shape, dtype=F32):
        return self._stack().enter_context(self.nc.psum_tensor(name, shape, dtype))

    def buf(self, name=None):
        self._nbuf += 1
        return Buf(name or f"b{self._nbuf}")

    SERIAL = False

    def _mkdeps(self, reads, writes, deps):
        dd = set(d for d in deps if d is not None)
        if Prog.SERIAL and self.steps:
            dd.add(self.steps[-1])
        if self._barrier is not None:
            dd.add(self._barrier)
        for b in reads:
            if b.last_w is not None:
                dd.add(b.last_w)
        for b in writes:
            if b.last_w is not None:
                dd.add(b.last_w)
            dd.update(b.readers)
        return dd

    def _post(self, s, reads, writes):
        for b in reads:
            b.readers.append(s)
        for b in writes:
            b.last_w = s
            b.readers = []
        return s

    def step(self, engine, fn, reads=(), writes=(), deps=(), tag=""):
        s = Step(engine, fn, tuple(self._mkdeps(reads, writes, deps)), tag=tag)
        self.steps.append(s)
        return self._post(s, reads, writes)

    def dma(self, queue, fn, reads=(), writes=(), deps=(), n_dma=1, tag=""):
        s = Step(queue, fn, tuple(self._mkdeps(reads, writes, deps)), is_dma=True, n_dma=n_dma, tag=tag)
        self.steps.append(s)
        return self._post(s, reads, writes)

    def barrier(self, tag="bar"):
        lasts = {}
        for s in self.steps:
            lasts[(s.is_dma, s.engine)] = s
        j = Step("sync", lambda e: e.nop(), tuple(set(lasts.values())), tag=tag)
        self.steps.append(j)
        self._barrier = j
        return j

    def emit(self):
        nc = self.nc
        for e in ENGINES:
            self._sems[("c", e)] = self.es.enter_context(nc.semaphore(f"sem_{e}"))
            self._counts[("c", e)] = 0
        for q in DMA_QUEUES:
            self._sems[("d", q)] = self.es.enter_context(nc.semaphore(f"dsem_{q}"))
            self._counts[("d", q)] = 0
        for s in self.steps:
            key = ("d", s.engine) if s.is_dma else ("c", s.engine)
            s.sem = self._sems[key]
            inc = 16 * s.n_dma if s.is_dma else 1
            self._counts[key] += inc
            s.val = self._counts[key]
        block = self.es.enter_context(nc.Block())
        by_engine = {e: [] for e in ENGINES}
        for s in self.steps:
            by_engine[s.engine].append(s)

        def make_body(steps):
            def body(eng):
                done = {}
                for s in steps:
                    waits = {}
                    if s.is_dma:
                        prev = s.val - 16 * s.n_dma
                        k = id(s.sem)
                        if prev > 0 and done.get(k, -1) < prev:
                            waits[k] = (s.sem, prev)
                    for d in s.deps:
                        k = id(d.sem)
                        if done.get(k, -1) >= d.val:
                            continue
                        if k not in waits or waits[k][1] < d.val:
                            waits[k] = (d.sem, d.val)
                    for (sem, val) in waits.values():
                        eng.wait_ge(sem, val)
                        done[id(sem)] = val
                    if s.is_dma:
                        s.fn(eng, s.sem)
                    else:
                        inst = s.fn(eng)
                        assert inst is not None, f"step {s.tag} returned no instruction"
                        inst.then_inc(s.sem, 1)
            return body

        for e in ENGINES:
            if by_engine[e]:
                getattr(block, e)(make_body(by_engine[e]))



def make_consts():
    c = {}
    c["IDF"] = np.eye(128, dtype=np.float32)
    c["ONESCOL"] = np.ones((128, 1), dtype=np.float32).astype(BF)
    c["ONESROW"] = np.ones((1, 128), dtype=np.float32)
    tri = np.arange(128)
    c["TRI_LO"] = (tri[None, :] <= tri[:, None]).astype(np.float32).astype(BF)
    c["TRI_HI"] = (tri[None, :] >= tri[:, None]).astype(np.float32).astype(BF)
    i32 = np.arange(128)
    blk = i32 // 32
    same = blk[:, None] == blk[None, :]
    c["GLA_L"] = (same & (i32[:, None] <= i32[None, :])).astype(np.float32)
    c["GLA_R"] = (same & (i32[:, None] > i32[None, :])).astype(np.float32)
    t32 = np.arange(32)
    c["GLA_PMASK"] = (t32[:, None] <= np.tile(t32, 4)[None, :]).astype(np.float32)
    return c


CONSTS = make_consts()


def declare_consts(nc, p, names):
    out = {}
    loads = []
    for n in names:
        arr = CONSTS[n]
        dt = {np.dtype(np.float32): F32, np.dtype(BF): BF16}[arr.dtype]
        ap = nc.dram_tensor("c_" + n, list(arr.shape), dt, kind="ExternalInput").ap()
        sb = p.sbuf("cs_" + n, list(arr.shape), dt)
        out[n] = sb
        loads.append((sb, ap))

    def fn(e, s):
        for sb, ap in loads:
            e.dma_start(out=sb[:], in_=ap).then_inc(s, 16)
    out["_step"] = p.dma("sync", fn, n_dma=len(loads), tag="consts")
    return out


def const_inputs(names):
    return {"c_" + n: np.ascontiguousarray(CONSTS[n]) for n in names}


def fm_view(hbm_ap):
    return hbm_ap.rearrange("(j p) t -> p j t", p=128)



def emit_in_transpose(p, nc, x_in, xfm, cst):
    p.push_scope()
    xt = [p.sbuf(f"it_x{i}", [128, D], F32) for i in range(2)]
    ot = [p.sbuf(f"it_o{i}", [128, D], F32) for i in range(2)]
    pst = [p.psum(f"it_ps{i}", [128, 512]) for i in range(2)]
    xt_b = [p.buf() for _ in range(2)]
    ot_b = [p.buf() for _ in range(2)]
    ps_b = [p.buf() for _ in range(2)]
    for i in range(S // 128):
        bi = i % 2
        p.dma("sync", (lambda xsb, src: lambda e, s: e.dma_start(out=xsb[:], in_=src).then_inc(s, 16))(
            xt[bi], x_in[i * 128:(i + 1) * 128, :]), writes=[xt_b[bi]], deps=[cst["_step"]], tag=f"it_ld{i}")
        for q in range(4):
            bq = q % 2
            def trf(e, xsb=xt[bi], ps=pst[bq], q=q, idf=cst["IDF"]):
                for j in range(4):
                    et = 4 * q + j
                    inst = e.transpose(ps[:, 128 * j:128 * (j + 1)], xsb[:, 128 * et:128 * et + 128], idf[:])
                return inst
            p.step("tensor", trf, reads=[xt_b[bi]], writes=[ps_b[bq]], tag=f"it_tr{i}_{q}")
            if q % 2 == 0:
                evf = (lambda ps, osb, q: lambda e: e.copy(osb[:, 512 * q:512 * (q + 1)], ps[:]))(pst[bq], ot[bi], q)
                p.step("scalar", evf, reads=[ps_b[bq]], writes=[ot_b[bi]], tag=f"it_ev{i}_{q}")
            else:
                evf = (lambda ps, osb, q: lambda e: e.tensor_copy(osb[:, 512 * q:512 * (q + 1)], ps[:]))(pst[bq], ot[bi], q)
                p.step("vector", evf, reads=[ps_b[bq]], writes=[ot_b[bi]], tag=f"it_ev{i}_{q}")
        p.dma("sync", (lambda osb, i: lambda e, s: e.dma_start(
            out=fm_view(xfm)[:, :, 128 * i:128 * (i + 1)],
            in_=osb[:].rearrange("p (j t) -> p j t", t=128)).then_inc(s, 16))(ot[bi], i),
            reads=[ot_b[bi]], tag=f"it_st{i}")
    p.pop_scope()
    return p.barrier("it_end")



def emit_ln(p, nc, xfm, xn_dst, cst, tag, nw_col=None, nw_col_buf=None):
    CH = 512
    nch = S // CH
    p.push_scope()
    xc = [p.sbuf(f"{tag}_x{i}", [128, NT * CH], F32) for i in range(2)]
    xbB = [p.sbuf(f"{tag}_xb{i}", [128, NT * CH], BF16) for i in range(2)]
    xqB = [p.sbuf(f"{tag}_xq{i}", [128, NT * CH], BF16) for i in range(2)]
    xnc = [p.sbuf(f"{tag}_xn{i}", [128, NT * CH], BF16) for i in range(2)]
    tmp = [p.sbuf(f"{tag}_tmp{i}", [128, CH], F32) for i in range(4)]
    rowB = [p.sbuf(f"{tag}_rows{i}", [1, 6 * CH], F32) for i in range(2)]
    b1sb = [p.sbuf(f"{tag}_b1{i}", [128, CH], F32) for i in range(2)]
    b2sb = [p.sbuf(f"{tag}_b2{i}", [128, CH], F32) for i in range(2)]
    ps_stB = [p.psum(f"{tag}_ps{i}", [1, 2 * CH]) for i in range(2)]
    ps_bc = [p.psum(f"{tag}_pbc{i}", [128, CH]) for i in range(2)]
    B = dict(xc=[p.buf(), p.buf()], xb=[p.buf(), p.buf()], xq=[p.buf(), p.buf()],
             xn=[p.buf(), p.buf()], tmp=[p.buf() for _ in range(4)],
             rows=[p.buf(), p.buf()], ps=[p.buf(), p.buf()],
             pbc=[p.buf(), p.buf()], b1=[p.buf(), p.buf()], b2=[p.buf(), p.buf()])
    for c in range(nch):
        bi = c % 2
        xb = xbB[bi]; xq = xqB[bi]; rowbuf = rowB[bi]; ps_st = ps_stB[bi]
        r_m = rowbuf[:, 0:CH]; r_sq = rowbuf[:, CH:2 * CH]; r_var = rowbuf[:, 2 * CH:3 * CH]
        r_rs = rowbuf[:, 3 * CH:4 * CH]; r_mrs = rowbuf[:, 4 * CH:5 * CH]; r_t = rowbuf[:, 5 * CH:6 * CH]
        Brows = B["rows"][bi]; Bps = B["ps"][bi]; Bxb = B["xb"][bi]; Bxq = B["xq"][bi]
        p.dma("sync", (lambda dst, c: lambda e, s: e.dma_start(
            out=dst[:].rearrange("p (j t) -> p j t", t=CH),
            in_=fm_view(xfm)[:, :, CH * c:CH * (c + 1)]).then_inc(s, 16))(xc[bi], c),
            writes=[B["xc"][bi]], tag=f"{tag}_ld{c}")
        p.step("vector", (lambda src, xb=xb: lambda e: e.tensor_copy(xb[:], src[:]))(xc[bi]),
               reads=[B["xc"][bi]], writes=[Bxb], tag=f"{tag}_cast{c}")
        p.step("scalar", (lambda xb=xb, xq=xq: lambda e: e.activation(xq[:], xb[:], AF.Square))(),
               reads=[Bxb], writes=[Bxq], tag=f"{tag}_sq{c}")

        def stats(e, xb=xb, xq=xq, ps_st=ps_st):
            for j in range(NT):
                e.matmul(ps_st[:, 0:CH], cst["ONESCOL"][:], xb[:, CH * j:CH * (j + 1)],
                         start=(j == 0), stop=(j == NT - 1))
            for j in range(NT):
                inst = e.matmul(ps_st[:, CH:2 * CH], cst["ONESCOL"][:], xq[:, CH * j:CH * (j + 1)],
                                start=(j == 0), stop=(j == NT - 1))
            return inst
        p.step("tensor", stats, reads=[Bxb, Bxq], writes=[Bps], tag=f"{tag}_stat{c}")
        p.step("vector", (lambda r_m=r_m, ps_st=ps_st: lambda e: e.tensor_scalar_mul(r_m, ps_st[:, 0:CH], 1.0 / D))(),
               reads=[Bps], writes=[Brows])
        p.step("vector", (lambda r_sq=r_sq, ps_st=ps_st: lambda e: e.tensor_scalar_mul(r_sq, ps_st[:, CH:2 * CH], 1.0 / D))(),
               reads=[Bps], writes=[Brows])
        p.step("vector", (lambda r_t=r_t, r_m=r_m: lambda e: e.tensor_mul(r_t, r_m, r_m))(),
               reads=[Brows], writes=[Brows])
        p.step("vector", (lambda r_var=r_var, r_sq=r_sq, r_t=r_t: lambda e: e.tensor_sub(r_var, r_sq, r_t))(),
               reads=[Brows], writes=[Brows])
        p.step("vector", (lambda r_t=r_t, r_var=r_var: lambda e: e.tensor_scalar_add(r_t, r_var, 1e-5))(),
               reads=[Brows], writes=[Brows])
        p.step("scalar", (lambda r_t=r_t: lambda e: e.activation(r_t, r_t, AF.Ln))(),
               reads=[Brows], writes=[Brows])
        if nw_col is None:
            p.step("scalar", (lambda r_rs=r_rs, r_t=r_t: lambda e: e.activation(r_rs, r_t, AF.Exp, scale=-0.5))(),
                   reads=[Brows], writes=[Brows])
        else:
            p.step("scalar", (lambda r_mrs=r_mrs, r_t=r_t: lambda e: e.activation(r_mrs, r_t, AF.Exp, scale=-1.0))(),
                   reads=[Brows], writes=[Brows])
            p.step("vector", (lambda r_mrs=r_mrs, r_var=r_var: lambda e: e.tensor_mul(r_mrs, r_var, r_mrs))(),
                   reads=[Brows], writes=[Brows])
            p.step("vector", (lambda r_mrs=r_mrs: lambda e: e.tensor_scalar_add(r_mrs, r_mrs, 1.1920929e-07))(),
                   reads=[Brows], writes=[Brows])
            p.step("scalar", (lambda r_mrs=r_mrs: lambda e: e.activation(r_mrs, r_mrs, AF.Ln))(),
                   reads=[Brows], writes=[Brows])
            p.step("vector", (lambda r_t=r_t, r_mrs=r_mrs: lambda e: e.tensor_add(r_t, r_t, r_mrs))(),
                   reads=[Brows], writes=[Brows])
            p.step("scalar", (lambda r_rs=r_rs, r_t=r_t: lambda e: e.activation(r_rs, r_t, AF.Exp, scale=-0.5))(),
                   reads=[Brows], writes=[Brows])
        p.step("vector", (lambda r_mrs=r_mrs, r_m=r_m, r_rs=r_rs: lambda e: e.tensor_mul(r_mrs, r_m, r_rs))(),
               reads=[Brows], writes=[Brows])
        bcA = p.step("tensor", (lambda bi, r_rs=r_rs: lambda e: e.matmul(
            ps_bc[bi][:], cst["ONESROW"][0:1, :], r_rs, start=True, stop=True))(bi),
            reads=[Brows], writes=[B["pbc"][bi]], tag=f"{tag}_bcA{c}")
        cpA = p.step("vector", (lambda bi: lambda e: e.tensor_copy(b1sb[bi][:], ps_bc[bi][:]))(bi),
                     reads=[B["pbc"][bi]], writes=[B["b1"][bi]])
        bcB = p.step("tensor", (lambda bi, r_mrs=r_mrs: lambda e: e.matmul(
            ps_bc[bi][:], cst["ONESROW"][0:1, :], r_mrs, start=True, stop=True))(bi),
            reads=[Brows], writes=[B["pbc"][bi]], deps=[cpA], tag=f"{tag}_bcB{c}")
        cpB = p.step("vector", (lambda bi: lambda e: e.tensor_copy(b2sb[bi][:], ps_bc[bi][:]))(bi),
                     reads=[B["pbc"][bi]], writes=[B["b2"][bi]])
        for j in range(NT):
            tb = j % 4
            p.step("gpsimd", (lambda bi, j, tb: lambda e: e.tensor_mul(
                tmp[tb][:], xc[bi][:, CH * j:CH * (j + 1)], b1sb[bi][:]))(bi, j, tb),
                reads=[B["xc"][bi], B["b1"][bi]], writes=[B["tmp"][tb]], tag=f"{tag}_t1_{c}_{j}")
            if nw_col is None:
                p.step("vector", (lambda bi, j, tb: lambda e: e.tensor_sub(
                    xnc[bi][:, CH * j:CH * (j + 1)], tmp[tb][:], b2sb[bi][:]))(bi, j, tb),
                    reads=[B["tmp"][tb], B["b2"][bi]], writes=[B["xn"][bi]], tag=f"{tag}_t2_{c}_{j}")
            else:
                p.step("vector", (lambda bi, j, tb: lambda e: e.tensor_sub(
                    tmp[tb][:], tmp[tb][:], b2sb[bi][:]))(bi, j, tb),
                    reads=[B["tmp"][tb], B["b2"][bi]], writes=[B["tmp"][tb]])
                p.step("vector", (lambda bi, j, tb: lambda e: e.tensor_scalar_mul(
                    xnc[bi][:, CH * j:CH * (j + 1)], tmp[tb][:], nw_col[:, j:j + 1]))(bi, j, tb),
                    reads=[B["tmp"][tb], nw_col_buf], writes=[B["xn"][bi]], tag=f"{tag}_t3_{c}_{j}")
        p.dma("sync", (lambda src, c: lambda e, s: e.dma_start(
            out=fm_view(xn_dst)[:, :, CH * c:CH * (c + 1)],
            in_=src[:].rearrange("p (j t) -> p j t", t=CH)).then_inc(s, 16))(xnc[bi], c),
            reads=[B["xn"][bi]], tag=f"{tag}_st{c}")
    p.pop_scope()
    return p.barrier(f"{tag}_end")



def band_sched(chunk0):
    out = []
    if chunk0:
        for k in range(4):
            out.append((k, 128 * k, 512, k, None))
    else:
        for k in range(8):
            qlo = max(0, 128 * (k - 4))
            qhi = min(512, 128 * (k + 1))
            lo = (k - 4) if k >= 4 else None
            hi = k if k < 4 else None
            out.append((k, qlo, qhi, lo, hi))
    return out


def emit_l0_attn(p, nc, xn_hbm, wqkv_b, attnT_hbm, cst):
    CH = 512
    nch = S // CH
    p.push_scope()
    wq = [p.sbuf(f"a_wq{i}", [128, NT * CH], BF16) for i in range(2)]
    wk = [p.sbuf(f"a_wk{i}", [128, NT * CH], BF16) for i in range(2)]
    wv = [p.sbuf(f"a_wv{i}", [128, NT * CH], BF16) for i in range(2)]
    wq_b = [p.buf(), p.buf()]
    wk_b = [p.buf(), p.buf()]
    wv_b = [p.buf(), p.buf()]
    xnc = [p.sbuf(f"a_xn{i}", [128, NT * CH], BF16) for i in range(2)]
    xnc_b = [p.buf(), p.buf()]
    qT = p.sbuf("a_qT", [128, 4 * CH], BF16)
    kT = [p.sbuf(f"a_kT{i}", [128, 4 * CH], BF16) for i in range(2)]
    vv = [p.sbuf(f"a_v{i}", [128, 4 * CH], BF16) for i in range(2)]
    qT_b = p.buf()
    kT_b = [p.buf(), p.buf()]
    vv_b = [p.buf(), p.buf()]
    pT = p.sbuf("a_pT", [128, 8 * CH], BF16)
    pT_b = p.buf()
    rz = p.sbuf("a_rz", [1, CH], F32)
    rzb = p.sbuf("a_rzb", [128, CH], F32)
    rz_b, rzb_b = p.buf(), p.buf()
    att = p.sbuf("a_att", [128, CH], BF16)
    att_b = p.buf()
    ps_q = p.psum("a_psq", [128, 4 * CH])
    ps_s = p.psum("a_pss", [128, 2 * CH])
    ps_q_b = p.buf()
    ps_s_b = [p.buf(), p.buf()]
    ps_pv = p.psum("a_pspv", [128, CH])
    ps_pv_b = p.buf()
    ps_z = p.psum("a_psz", [1, CH])
    ps_z_b = p.buf()
    ps_bc = ps_pv
    ps_bc_b = ps_pv_b

    for g in range(4):
        gw = g % 2
        def wload(e, s, g=g, gw=gw):
            for et in range(NT):
                e.dma_start(out=wq[gw][:, CH * et:CH * (et + 1)],
                            in_=wqkv_b[512 * g:512 * (g + 1), 128 * et:128 * (et + 1)], transpose=True).then_inc(s, 16)
                e.dma_start(out=wk[gw][:, CH * et:CH * (et + 1)],
                            in_=wqkv_b[2048 + 512 * g:2048 + 512 * (g + 1), 128 * et:128 * (et + 1)], transpose=True).then_inc(s, 16)
                e.dma_start(out=wv[gw][:, CH * et:CH * (et + 1)],
                            in_=wqkv_b[4096 + 512 * g:4096 + 512 * (g + 1), 128 * et:128 * (et + 1)], transpose=True).then_inc(s, 16)
        p.dma("sync", wload, writes=[wq_b[gw], wk_b[gw], wv_b[gw]], n_dma=3 * NT, tag=f"a_wl{g}")
        for c in range(nch):
            bi = c % 2
            cur, prv = c % 2, 1 - c % 2
            p.dma("gpsimd", (lambda dst, c: lambda e, s: e.dma_start(
                out=dst[:].rearrange("p (j t) -> p j t", t=CH),
                in_=fm_view(xn_hbm)[:, :, CH * c:CH * (c + 1)]).then_inc(s, 16))(xnc[bi], c),
                writes=[xnc_b[bi]], tag=f"a_ldx{g}_{c}")

            def projmm(wsb, trans_out):
                def fn(e, wsb=wsb, bi=bi):
                    for ft in range(4):
                        for et in range(NT):
                            inst = e.matmul(ps_q[:, CH * ft:CH * (ft + 1)],
                                            wsb[:, CH * et + 128 * ft:CH * et + 128 * (ft + 1)],
                                            xnc[bi][:, CH * et:CH * (et + 1)],
                                            start=(et == 0), stop=(et == NT - 1))
                    return inst
                return fn
            q_mm = p.step("tensor", projmm(wq[gw], True), reads=[wq_b[gw], xnc_b[bi]], writes=[ps_q_b], tag=f"a_qmm{g}_{c}")
            q_ev = p.step("scalar", lambda e: e.mul(qT[:], ps_q[:], ASCALE),
                          reads=[ps_q_b], writes=[qT_b], tag=f"a_qev{g}_{c}")
            k_mm = p.step("tensor", projmm(wk[gw], True), reads=[wk_b[gw], xnc_b[bi]], writes=[ps_q_b], tag=f"a_kmm{g}_{c}")
            k_ev = p.step("scalar", (lambda cur: lambda e: e.copy(kT[cur][:], ps_q[:]))(cur),
                          reads=[ps_q_b], writes=[kT_b[cur]], tag=f"a_kev{g}_{c}")

            def vmm(e, bi=bi, gw=gw):
                for tt in range(4):
                    for et in range(NT):
                        inst = e.matmul(ps_q[:, CH * tt:CH * (tt + 1)],
                                        xnc[bi][:, CH * et + 128 * tt:CH * et + 128 * (tt + 1)],
                                        wv[gw][:, CH * et:CH * (et + 1)],
                                        start=(et == 0), stop=(et == NT - 1))
                return inst
            v_mm = p.step("tensor", vmm, reads=[wv_b[gw], xnc_b[bi]], writes=[ps_q_b], tag=f"a_vmm{g}_{c}")
            v_ev = p.step("scalar", (lambda cur: lambda e: e.copy(vv[cur][:], ps_q[:]))(cur),
                          reads=[ps_q_b], writes=[vv_b[cur]], tag=f"a_vev{g}_{c}")
            sched = band_sched(c == 0)
            kread = [kT_b[cur]] if c == 0 else [kT_b[cur], kT_b[prv]]
            vread = [vv_b[cur]] if c == 0 else [vv_b[cur], vv_b[prv]]
            for h in range(4):
                qs = qT[:, CH * h:CH * (h + 1)]
                for it, (k, qlo, qhi, lo, hi) in enumerate(sched):
                    pb = it % 2
                    if c == 0:
                        ksrc, ktile = kT[cur], k
                    else:
                        ksrc, ktile = (kT[prv], k) if k < 4 else (kT[cur], k - 4)
                    stat = ksrc[:, CH * h + 128 * ktile:CH * h + 128 * (ktile + 1)]
                    p.step("tensor", (lambda stat, qs, pb, qlo, qhi: lambda e: e.matmul(
                        ps_s[:, CH * pb + qlo:CH * pb + qhi], stat, qs[:, qlo:qhi], start=True, stop=True))(stat, qs, pb, qlo, qhi),
                        reads=kread + [qT_b], writes=[ps_s_b[pb]], tag=f"a_smm{g}_{c}_{h}_{it}")
                    p.step("scalar", (lambda pb, it, qlo, qhi: lambda e: e.activation(
                        pT[:, CH * it + qlo:CH * it + qhi], ps_s[:, CH * pb + qlo:CH * pb + qhi], AF.Exp))(pb, it, qlo, qhi),
                        reads=[ps_s_b[pb]], writes=[pT_b], tag=f"a_sev{g}_{c}_{h}_{it}")
                    if lo is not None:
                        p.step("vector", (lambda it, lo: lambda e: e.tensor_mul(
                            pT[:, CH * it + 128 * lo:CH * it + 128 * (lo + 1)],
                            pT[:, CH * it + 128 * lo:CH * it + 128 * (lo + 1)], cst["TRI_HI"][:]))(it, lo),
                            reads=[pT_b], writes=[pT_b], tag=f"a_mlo{g}_{c}_{h}_{it}")
                    if hi is not None:
                        p.step("vector", (lambda it, hi: lambda e: e.tensor_mul(
                            pT[:, CH * it + 128 * hi:CH * it + 128 * (hi + 1)],
                            pT[:, CH * it + 128 * hi:CH * it + 128 * (hi + 1)], cst["TRI_LO"][:]))(it, hi),
                            reads=[pT_b], writes=[pT_b], tag=f"a_mhi{g}_{c}_{h}_{it}")

                def zmm(e, sched=sched):
                    for it, (k, qlo, qhi, lo, hi) in enumerate(sched):
                        inst = e.matmul(ps_z[0:1, qlo:qhi], cst["ONESCOL"][:],
                                        pT[:, CH * it + qlo:CH * it + qhi],
                                        start=(it == 0), stop=(it == len(sched) - 1))
                    return inst
                p.step("tensor", zmm, reads=[pT_b], writes=[ps_z_b], tag=f"a_zmm{g}_{c}_{h}")
                p.step("vector", lambda e: e.reciprocal(rz[:], ps_z[:]), reads=[ps_z_b], writes=[rz_b])
                p.step("tensor", lambda e: e.matmul(ps_bc[:], cst["ONESROW"][0:1, :], rz[:], start=True, stop=True),
                       reads=[rz_b], writes=[ps_bc_b])
                p.step("vector", lambda e: e.tensor_copy(rzb[:], ps_bc[:]), reads=[ps_bc_b], writes=[rzb_b])

                def pvmm(e, sched=sched, h=h, cur=cur, prv=prv, c=c):
                    for it, (k, qlo, qhi, lo, hi) in enumerate(sched):
                        if c == 0:
                            vsrc, vtile = vv[cur], k
                        else:
                            vsrc, vtile = (vv[prv], k) if k < 4 else (vv[cur], k - 4)
                        stat = vsrc[:, CH * vtile + 128 * h:CH * vtile + 128 * (h + 1)]
                        inst = e.matmul(ps_pv[:, qlo:qhi], stat,
                                        pT[:, CH * it + qlo:CH * it + qhi],
                                        start=(it == 0), stop=(it == len(sched) - 1))
                    return inst
                p.step("tensor", pvmm, reads=[pT_b] + vread, writes=[ps_pv_b], tag=f"a_pv{g}_{c}_{h}")
                p.step("vector", lambda e: e.tensor_mul(att[:], ps_pv[:], rzb[:]),
                       reads=[ps_pv_b, rzb_b], writes=[att_b], tag=f"a_nrm{g}_{c}_{h}")
                hg = 4 * g + h
                p.dma("sync", (lambda hg, c: lambda e, s: e.dma_start(
                    out=attnT_hbm[128 * hg:128 * (hg + 1), CH * c:CH * (c + 1)], in_=att[:]).then_inc(s, 16))(hg, c),
                    reads=[att_b], tag=f"a_stA{g}_{c}_{h}")
    p.pop_scope()
    return p.barrier("a_end")



def emit_fm_proj_residual(p, nc, src_hbm, w_b, resid_hbm, out_hbm, cst, tag):
    CH = 512
    nch = S // CH
    p.push_scope()
    wo = p.sbuf(f"{tag}_wo", [128, NT * D], BF16)
    wo_b = p.buf()
    atc = [p.sbuf(f"{tag}_at{i}", [128, NT * CH], BF16) for i in range(2)]
    atc_b = [p.buf(), p.buf()]
    xc = [p.sbuf(f"{tag}_x{i}", [128, NT * CH], F32) for i in range(2)]
    xc_b = [p.buf(), p.buf()]
    oc = [p.sbuf(f"{tag}_o{i}", [128, CH], F32) for i in range(4)]
    oc_b = [p.buf() for _ in range(4)]
    ps = p.psum(f"{tag}_ps", [128, 2 * CH])
    ps_b = [p.buf(), p.buf()]

    def wload(e, s):
        for et in range(NT):
            e.dma_start(out=wo[:, D * et:D * (et + 1)],
                        in_=w_b[:, 128 * et:128 * (et + 1)], transpose=True).then_inc(s, 16)
    p.dma("sync", wload, writes=[wo_b], n_dma=NT, tag=f"{tag}_wl")
    for c in range(nch):
        bi = c % 2
        p.dma("gpsimd", (lambda dst, c: lambda e, s: e.dma_start(
            out=dst[:].rearrange("p (j t) -> p j t", t=CH),
            in_=fm_view(src_hbm)[:, :, CH * c:CH * (c + 1)]).then_inc(s, 16))(atc[bi], c),
            writes=[atc_b[bi]], tag=f"{tag}_lda{c}")
        p.dma("sync", (lambda dst, c: lambda e, s: e.dma_start(
            out=dst[:].rearrange("p (j t) -> p j t", t=CH),
            in_=fm_view(resid_hbm)[:, :, CH * c:CH * (c + 1)]).then_inc(s, 16))(xc[bi], c),
            writes=[xc_b[bi]], tag=f"{tag}_ldx{c}")
        for ft in range(NT):
            pb = ft % 2
            ob = ft % 4

            def omm(e, bi=bi, ft=ft, pb=pb):
                for et in range(NT):
                    inst = e.matmul(ps[:, CH * pb:CH * (pb + 1)],
                                    wo[:, D * et + 128 * ft:D * et + 128 * (ft + 1)],
                                    atc[bi][:, CH * et:CH * (et + 1)],
                                    start=(et == 0), stop=(et == NT - 1))
                return inst
            p.step("tensor", omm, reads=[wo_b, atc_b[bi]], writes=[ps_b[pb]], tag=f"{tag}_mm{c}_{ft}")
            p.step("vector", (lambda bi, ft, pb, ob: lambda e: e.tensor_add(
                oc[ob][:], ps[:, CH * pb:CH * (pb + 1)], xc[bi][:, CH * ft:CH * (ft + 1)]))(bi, ft, pb, ob),
                reads=[ps_b[pb], xc_b[bi]], writes=[oc_b[ob]], tag=f"{tag}_ev{c}_{ft}")
            p.dma("sync", (lambda ft, c, ob: lambda e, s: e.dma_start(
                out=out_hbm[128 * ft:128 * (ft + 1), CH * c:CH * (c + 1)], in_=oc[ob][:]).then_inc(s, 16))(ft, c, ob),
                reads=[oc_b[ob]], tag=f"{tag}_st{c}_{ft}")
    p.pop_scope()
    return p.barrier(f"{tag}_end")



L1_CONSTS = ["IDF", "ONESCOL", "ONESROW", "TRI_LO", "TRI_HI"]


def build_launch1(dbg=False):
    nc = bass.Bass("TRN2", target_bir_lowering=False, debug=False)
    x_in = nc.dram_tensor("x", [S, D], F32, kind="ExternalInput").ap()
    wqkv = nc.dram_tensor("w_qkv", [3 * D, D], F32, kind="ExternalInput").ap()
    wout = nc.dram_tensor("w_out", [D, D], F32, kind="ExternalInput").ap()
    xfm_out = nc.dram_tensor("xfm", [D, S], F32, kind="ExternalOutput").ap()
    x1fm = nc.dram_tensor("x1fm", [D, S], F32, kind="ExternalOutput").ap()
    knd = "ExternalOutput" if dbg else "Internal"
    xn_hbm = nc.dram_tensor("xn_s", [D, S], BF16, kind=knd).ap()
    wqkv_b = nc.dram_tensor("wqkv_b", [3 * D, D], BF16, kind="Internal").ap()
    wout_b = nc.dram_tensor("wout_b", [D, D], BF16, kind="Internal").ap()
    attnT = nc.dram_tensor("attnT_s", [D, S], BF16, kind=knd).ap()

    p = Prog(nc)
    cst = declare_consts(nc, p, L1_CONSTS)
    p.dma("gpsimd", lambda e, s: e.dma_start(out=wqkv_b, in_=wqkv).then_inc(s, 16), n_dma=1, tag="wc1")
    p.dma("gpsimd", lambda e, s: e.dma_start(out=wout_b, in_=wout).then_inc(s, 16), n_dma=1, tag="wc2")
    emit_in_transpose(p, nc, x_in, xfm_out, cst)
    emit_ln(p, nc, xfm_out, xn_hbm, cst, "ln0a")
    emit_l0_attn(p, nc, xn_hbm, wqkv_b, attnT, cst)
    emit_fm_proj_residual(p, nc, attnT, wout_b, xfm_out, x1fm, cst, "o")
    p.emit()
    return nc




def emit_scale_cols(p, nc, vec_ap, n, tag, cst, pst, pst_b, col):
    tmp = p.sbuf(f"{tag}_r", [max(n, 1), 128], F32)
    tb, cb = p.buf(), p.buf()
    p.dma("sync", lambda e, s: e.dma_start(out=tmp[:], in_=vec_ap.rearrange("(i p) -> i p", p=128)).then_inc(s, 16),
          writes=[tb], tag=f"{tag}_ld")
    p.step("tensor", lambda e: e.transpose(pst[:, 0:n], tmp[:], cst["IDF"][0:max(n, 1), 0:max(n, 1)]),
           reads=[tb], writes=[pst_b], tag=f"{tag}_tr")
    p.step("vector", lambda e: e.tensor_copy(col[:], pst[:, 0:n]), reads=[pst_b], writes=[cb], tag=f"{tag}_cp")
    return cb


def emit_l0_ffn(p, nc, x1fm, xn_hbm, wg8, wu8, wd8, sg_ap, su_ap, sd_ap, fuse_hbm, x2fm, cst, gate_dbg=None):
    INTER = 8192
    NO = INTER // 128
    p.push_scope()
    sg_col = p.sbuf("sg_c", [128, 64], F32)
    su_col = p.sbuf("su_c", [128, 64], F32)
    sd_col = p.sbuf("sd_c", [128, 16], F32)
    p.push_scope()
    pst_sc = p.psum("sc_ps", [128, 128])
    pst_sc_b = p.buf()
    sg_b = emit_scale_cols(p, nc, sg_ap, 64, "sg", cst, pst_sc, pst_sc_b, sg_col)
    su_b = emit_scale_cols(p, nc, su_ap, 64, "su", cst, pst_sc, pst_sc_b, su_col)
    sd_b = emit_scale_cols(p, nc, sd_ap, 16, "sd", cst, pst_sc, pst_sc_b, sd_col)
    p.pop_scope()
    p.barrier("sc_end")

    p.push_scope()
    xn = p.sbuf("f_xn", [128, NT * S], BF16)
    xn_b = p.buf()
    p.dma("sync", lambda e, s: e.dma_start(
        out=xn[:].rearrange("p (j t) -> p j t", t=S), in_=fm_view(xn_hbm)).then_inc(s, 16),
        writes=[xn_b], tag="f_ldxn")
    wg = [p.sbuf("f_wg0", [128, NT * 512], BF16)] * 2
    wu = [p.sbuf("f_wu0", [128, NT * 512], BF16)] * 2
    _wgb = p.buf(); _wub = p.buf()
    wg_b = [_wgb, _wgb]
    wu_b = [_wub, _wub]
    gsb = p.sbuf("f_gate", [128, S], F32)
    gsb_b = p.buf()
    fsb = [p.sbuf(f"f_fuse{i}", [128, S], BF16) for i in range(2)]
    fsb_b = [p.buf(), p.buf()]
    ps = [p.psum(f"f_ps{i}", [128, 2048]) for i in range(2)]
    ps_b = [p.buf(), p.buf()]
    for og in range(16):
        gi = og % 2
        def wload(e, s, og=og, gi=gi):
            e.dma_start(out=wg[gi][:].rearrange("p (j o) -> p j o", o=512),
                        in_=wg8.rearrange("(j p) o -> p j o", p=128)[:, :, 512 * og:512 * (og + 1)]).then_inc(s, 16)
            e.dma_start(out=wu[gi][:].rearrange("p (j o) -> p j o", o=512),
                        in_=wu8.rearrange("(j p) o -> p j o", p=128)[:, :, 512 * og:512 * (og + 1)]).then_inc(s, 16)
        p.dma("gpsimd", wload, writes=[wg_b[gi], wu_b[gi]], n_dma=2, tag=f"f_wl{og}")
        for ot in range(4):
            o = 4 * og + ot
            for half in range(2):
                pb = half
                def gmm(e, gi=gi, ot=ot, half=half, pb=pb):
                    for h in range(NT):
                        for q in range(4):
                            inst = e.matmul(ps[pb][:, 512 * q:512 * (q + 1)],
                                            wg[gi][:, 512 * h + 128 * ot:512 * h + 128 * (ot + 1)],
                                            xn[:, S * h + 2048 * half + 512 * q:S * h + 2048 * half + 512 * (q + 1)],
                                            start=(h == 0), stop=(h == NT - 1))
                    return inst
                p.step("tensor", gmm, reads=[wg_b[gi], xn_b], writes=[ps_b[pb]], tag=f"f_gmm{o}_{half}")
                p.step("scalar", (lambda o, half, pb: lambda e: e.activation(
                    gsb[:, 2048 * half:2048 * (half + 1)], ps[pb][:], AF.Silu, scale=sg_col[:, o:o + 1]))(o, half, pb),
                    reads=[ps_b[pb], sg_b], writes=[gsb_b], tag=f"f_gev{o}_{half}")
            for half in range(2):
                pb = half
                def umm(e, gi=gi, ot=ot, half=half, pb=pb):
                    for h in range(NT):
                        for q in range(4):
                            inst = e.matmul(ps[pb][:, 512 * q:512 * (q + 1)],
                                            wu[gi][:, 512 * h + 128 * ot:512 * h + 128 * (ot + 1)],
                                            xn[:, S * h + 2048 * half + 512 * q:S * h + 2048 * half + 512 * (q + 1)],
                                            start=(h == 0), stop=(h == NT - 1))
                    return inst
                p.step("tensor", umm, reads=[wu_b[gi], xn_b], writes=[ps_b[pb]], tag=f"f_umm{o}_{half}")
                p.step("vector", (lambda o, half, pb, fi: lambda e: e.scalar_tensor_tensor(
                    fsb[fi][:, 2048 * half:2048 * (half + 1)], ps[pb][:], su_col[:, o:o + 1],
                    gsb[:, 2048 * half:2048 * (half + 1)], ALU.mult, ALU.mult))(o, half, pb, o % 2),
                    reads=[ps_b[pb], su_b, gsb_b], writes=[fsb_b[o % 2]], tag=f"f_uev{o}_{half}")
            if gate_dbg is not None:
                p.dma("scalar", (lambda o: lambda e, s: e.dma_start(
                    out=gate_dbg[128 * o:128 * (o + 1), :], in_=gsb[:]).then_inc(s, 16))(o),
                    reads=[gsb_b], tag=f"f_gdb{o}")
            p.dma("sync", (lambda o, fi: lambda e, s: e.dma_start(
                out=fuse_hbm[128 * o:128 * (o + 1), :], in_=fsb[fi][:]).then_inc(s, 16))(o, o % 2),
                reads=[fsb_b[o % 2]], tag=f"f_fst{o}")
    p.pop_scope()
    p.barrier("f_gu_end")

    p.push_scope()
    TC = 256
    ntc = S // TC
    wd = p.sbuf("f_wd", [128, NO * 1024], BF16)
    wd_b = p.buf()
    fc = [p.sbuf(f"f_fc{i}", [128, NO * TC], BF16) for i in range(2)]
    fc_b = [p.buf(), p.buf()]
    xr = [p.sbuf("f_xr0", [128, 8 * TC], F32)] * 2
    _xrb = p.buf()
    xr_b = [_xrb, _xrb]
    ob = [p.sbuf(f"f_ob{i}", [128, TC], F32) for i in range(2)]
    ob_b = [p.buf() for _ in range(2)]
    psd = [p.psum(f"f_psd{i}", [128, 512]) for i in range(2)]
    psd_b = [p.buf(), p.buf()]
    for h2h in range(2):
        def wdload(e, s, h2h=h2h):
            e.dma_start(out=wd[:].rearrange("p (j o) -> p j o", o=1024),
                        in_=wd8.rearrange("(j p) o -> p j o", p=128)[:, :, 1024 * h2h:1024 * (h2h + 1)]).then_inc(s, 16)
        p.dma("gpsimd", wdload, writes=[wd_b], n_dma=1, tag=f"f_wdl{h2h}")
        for tc in range(ntc):
            bi = tc % 2
            p.dma("sync", (lambda bi, tc: lambda e, s: e.dma_start(
                out=fc[bi][:].rearrange("p (j t) -> p j t", t=TC),
                in_=fuse_hbm.rearrange("(j p) t -> p j t", p=128)[:, :, TC * tc:TC * (tc + 1)]).then_inc(s, 16))(bi, tc),
                writes=[fc_b[bi]], tag=f"f_fld{h2h}_{tc}")
            p.dma("sync", (lambda bi, tc, h2h: lambda e, s: e.dma_start(
                out=xr[bi][:].rearrange("p (j t) -> p j t", t=TC),
                in_=fm_view(x1fm)[:, 8 * h2h:8 * (h2h + 1), TC * tc:TC * (tc + 1)]).then_inc(s, 16))(bi, tc, h2h),
                writes=[xr_b[bi]], tag=f"f_xld{h2h}_{tc}")
            for h2t in range(8):
                pb = h2t % 2
                obi = h2t % 2
                def dmm(e, h2t=h2t, bi=bi, pb=pb):
                    for ot in range(NO):
                        inst = e.matmul(psd[pb][:, 0:TC],
                                        wd[:, 1024 * ot + 128 * h2t:1024 * ot + 128 * (h2t + 1)],
                                        fc[bi][:, TC * ot:TC * (ot + 1)],
                                        start=(ot == 0), stop=(ot == NO - 1))
                    return inst
                p.step("tensor", dmm, reads=[wd_b, fc_b[bi]], writes=[psd_b[pb]], tag=f"f_dmm{h2h}_{tc}_{h2t}")
                h2 = 8 * h2h + h2t
                p.step("vector", (lambda h2, h2t, bi, pb, obi: lambda e: e.scalar_tensor_tensor(
                    ob[obi][:], psd[pb][:, 0:TC], sd_col[:, h2:h2 + 1],
                    xr[bi][:, TC * h2t:TC * (h2t + 1)], ALU.mult, ALU.add))(h2, h2t, bi, pb, obi),
                    reads=[psd_b[pb], sd_b, xr_b[bi]], writes=[ob_b[obi]], tag=f"f_dev{h2h}_{tc}_{h2t}")
                p.dma("sync", (lambda h2, tc, obi: lambda e, s: e.dma_start(
                    out=x2fm[128 * h2:128 * (h2 + 1), TC * tc:TC * (tc + 1)], in_=ob[obi][:]).then_inc(s, 16))(h2, tc, obi),
                    reads=[ob_b[obi]], tag=f"f_dst{h2h}_{tc}_{h2t}")
    p.pop_scope()
    p.pop_scope()
    return p.barrier("f_end")


L2_CONSTS = ["IDF", "ONESCOL", "ONESROW"]


def build_launch2(dbg=False):
    nc = bass.Bass("TRN2", target_bir_lowering=False, debug=False)
    x1fm = nc.dram_tensor("x1fm", [D, S], F32, kind="ExternalInput").ap()
    wg8 = nc.dram_tensor("w_gate", [D, 8192], I8, kind="ExternalInput").ap()
    wu8 = nc.dram_tensor("w_up", [D, 8192], I8, kind="ExternalInput").ap()
    wd8 = nc.dram_tensor("w_down", [8192, D], I8, kind="ExternalInput").ap()
    sg = nc.dram_tensor("s_gate", [8192], F32, kind="ExternalInput").ap()
    su = nc.dram_tensor("s_up", [8192], F32, kind="ExternalInput").ap()
    sd = nc.dram_tensor("s_down", [D], F32, kind="ExternalInput").ap()
    x2fm = nc.dram_tensor("x2fm", [D, S], F32, kind="ExternalOutput").ap()
    knd = "ExternalOutput" if dbg else "Internal"
    xn_hbm = nc.dram_tensor("xn_s", [D, S], BF16, kind=knd).ap()
    fuse_hbm = nc.dram_tensor("fuse_s", [8192, S], BF16, kind=knd).ap()

    p = Prog(nc)
    cst = declare_consts(nc, p, L2_CONSTS)
    gate_dbg = nc.dram_tensor("gate_s", [8192, S], F32, kind="ExternalOutput").ap() if dbg else None
    emit_ln(p, nc, x1fm, xn_hbm, cst, "ln0f")
    emit_l0_ffn(p, nc, x1fm, xn_hbm, wg8, wu8, wd8, sg, su, sd, fuse_hbm, x2fm, cst, gate_dbg)
    p.emit()
    return nc




def emit_l1_attn(p, nc, x2fm, xn_hbm, wqb, wkb, wvb, bq_ap, bk_ap, bv_ap, x3fm, cst, dbg_ap=None):
    PC = 512
    npc = S // PC
    p.push_scope()
    bqs_col = p.sbuf("g_bqs", [128, NT], F32)
    bkr = p.sbuf("g_bkr", [1, 1024], F32)
    bkr_b = p.buf()
    bkb = p.sbuf("g_bkb", [128, 1024], F32)
    bkb_b = p.buf()
    p.push_scope()
    pst_sc = p.psum("g_scps", [128, 128])
    pst_sc_b = p.buf()
    bqs_b = emit_scale_cols(p, nc, bq_ap, NT, "gbq", cst, pst_sc, pst_sc_b, bqs_col)
    sc = p.step("vector", lambda e: e.tensor_scalar_mul(bqs_col[:], bqs_col[:], ASCALE),
                reads=[bqs_b], writes=[bqs_b], tag="g_bqs_s")
    p.dma("sync", lambda e, s: [e.dma_start(out=bkr[0:1, 0:512], in_=bk_ap[None, :]).then_inc(s, 16),
                                e.dma_start(out=bkr[0:1, 512:1024], in_=bv_ap[None, :]).then_inc(s, 16)][-1],
          writes=[bkr_b], n_dma=2, tag="g_bkr")
    for i in range(8):
        p.step("tensor", (lambda i: lambda e: e.matmul(pst_sc[:], cst["ONESROW"][0:1, :], bkr[0:1, 128 * i:128 * (i + 1)],
                                                       start=True, stop=True))(i),
               reads=[bkr_b], writes=[pst_sc_b], tag=f"g_bkb_mm{i}")
        p.step("vector", (lambda i: lambda e: e.tensor_copy(bkb[:, 128 * i:128 * (i + 1)], pst_sc[:]))(i),
               reads=[pst_sc_b], writes=[bkb_b], tag=f"g_bkb_cp{i}")
    p.pop_scope()
    p.barrier("g_pre")

    wqT = p.sbuf("g_wqT", [128, NT * D], BF16)
    wkT = p.sbuf("g_wkT", [128, NT * 512], BF16)
    wvT = p.sbuf("g_wvT", [128, NT * 512], BF16)
    wqT_b, wkT_b, wvT_b = p.buf(), p.buf(), p.buf()

    def wload(e, s):
        for et in range(NT):
            e.dma_start(out=wqT[:, D * et:D * (et + 1)],
                        in_=wqb[:, 128 * et:128 * (et + 1)], transpose=True).then_inc(s, 16)
            e.dma_start(out=wkT[:, 512 * et:512 * (et + 1)],
                        in_=wkb[:, 128 * et:128 * (et + 1)], transpose=True).then_inc(s, 16)
            e.dma_start(out=wvT[:, 512 * et:512 * (et + 1)],
                        in_=wvb[:, 128 * et:128 * (et + 1)], transpose=True).then_inc(s, 16)
    p.dma("sync", wload, writes=[wqT_b, wkT_b, wvT_b], n_dma=3 * NT, tag="g_wl")

    Sst = p.sbuf("g_S", [128, 4 * 128], F32)
    Sbf = p.sbuf("g_Sbf", [128, 4 * 128], BF16)
    S_b = [p.buf() for _ in range(4)]
    Sbf_b = [p.buf() for _ in range(4)]
    p.step("vector", lambda e: e.memset(Sst[:], 0.0), writes=S_b, tag="g_S0")
    p.step("gpsimd", lambda e: e.memset(Sbf[:], 0.0), writes=Sbf_b, tag="g_Sbf0")

    xnc = p.sbuf("g_xn", [128, NT * PC], BF16)
    xnc_b = p.buf()
    qT = p.sbuf("g_qT", [128, NT * PC], BF16)
    qT_b = p.buf()
    kpre = p.sbuf("g_kpre", [128, 512], F32)
    vF = p.sbuf("g_vF", [128, 512], BF16)
    vH = p.sbuf("g_vH", [64, 512], BF16)
    vQ1 = p.sbuf("g_vQ1", [32, 512], BF16)
    vQ3 = p.sbuf("g_vQ3", [32, 512], BF16)
    knF = p.sbuf("g_knF", [128, 512], BF16)
    knH = p.sbuf("g_knH", [64, 512], BF16)
    knQ1 = p.sbuf("g_knQ1", [32, 512], BF16)
    knQ3 = p.sbuf("g_knQ3", [32, 512], BF16)
    Ee = p.sbuf("g_E", [128, 512], F32)
    Rr = p.sbuf("g_R", [128, 512], F32)
    kf = p.sbuf("g_kf", [128, 512], F32)
    uu = p.sbuf("g_u", [128, 512], F32)
    gg = p.sbuf("g_g", [128, 512], F32)
    er = p.sbuf("g_er", [128, 512], F32)
    cT_sb = p.sbuf("g_cT", [128, 512], F32)
    cs = p.sbuf("g_cs", [128, 512], F32)
    eqc = p.sbuf("g_eqc", [128, 512], BF16)
    emc = p.sbuf("g_emc", [128, 512], F32)
    khatT = p.sbuf("g_khatT", [128, 512], BF16)
    qst = p.sbuf("g_qst", [128, 512], BF16)
    ect = p.sbuf("g_ect", [128, 16], F32)
    ecm = p.sbuf("g_ecm", [128, 16], F32)
    pT_sb = p.sbuf("g_pT", [32, 128], BF16)
    x2c = [p.sbuf(f"g_x2c{i}", [128, NT * 128], F32) for i in range(2)]
    x3c = [p.sbuf(f"g_x3c{i}", [128, NT * 128], F32) for i in range(2)]
    kpre_b = p.buf(); vAB_b = p.buf(); knAB_b = p.buf()
    E_b = p.buf(); R_b = p.buf(); kf_b = p.buf(); u_b = p.buf(); g_b = p.buf(); er_b = p.buf()
    cT_b = p.buf(); cs_b = p.buf(); eqc_b = p.buf(); emc_b = p.buf(); khatT_b = p.buf()
    qst_b = p.buf(); ect_b = p.buf(); ecm_b = p.buf(); pT_b = p.buf()
    x2c_b = [p.buf(), p.buf()]; x3c_b = [p.buf(), p.buf()]

    ps_q = p.psum("g_psq", [128, 1024])
    ps_q_b = [p.buf(), p.buf()]
    ps_c = p.psum("g_psc", [128, 512])
    ps_c_b = p.buf()
    ps_r = p.psum("g_psr", [128, 512])
    ps_r_b = p.buf()
    ps_cT = p.psum("g_pscT", [128, 512])
    ps_cT_b = p.buf()
    ps_kT = p.psum("g_pskT", [128, 512])
    ps_kT_b = p.buf()
    ps_o = p.psum("g_pso", [128, 512])
    ps_o_b = p.buf()
    ps_su = p.psum("g_pssu", [128, 128])
    ps_su_b = p.buf()

    for pc in range(npc):
        p.dma("gpsimd", (lambda pc: lambda e, s: e.dma_start(
            out=xnc[:].rearrange("p (j t) -> p j t", t=PC),
            in_=fm_view(xn_hbm)[:, :, PC * pc:PC * (pc + 1)]).then_inc(s, 16))(pc),
            writes=[xnc_b], tag=f"g_ldx{pc}")
        for ft in range(NT):
            pb = ft % 2
            def qmm(e, ft=ft, pb=pb):
                for et in range(NT):
                    inst = e.matmul(ps_q[:, 512 * pb:512 * (pb + 1)],
                                    wqT[:, D * et + 128 * ft:D * et + 128 * (ft + 1)],
                                    xnc[:, PC * et:PC * (et + 1)],
                                    start=(et == 0), stop=(et == NT - 1))
                return inst
            p.step("tensor", qmm, reads=[wqT_b, xnc_b], writes=[ps_q_b[pb]], tag=f"g_qmm{pc}_{ft}")
            p.step("scalar", (lambda ft, pb: lambda e: e.activation(
                qT[:, PC * ft:PC * (ft + 1)], ps_q[:, 512 * pb:512 * (pb + 1)], AF.Relu,
                scale=ASCALE, bias=bqs_col[:, ft:ft + 1]))(ft, pb),
                reads=[ps_q_b[pb], bqs_b], writes=[qT_b], tag=f"g_qev{pc}_{ft}")
        for gb in range(PC // 128):
            t0 = 128 * gb
            def kvmm(e, gb=gb, t0=t0):
                for et in range(NT):
                    e.matmul(ps_q[:, 0:512],
                             xnc[:, PC * et + t0:PC * et + t0 + 128],
                             wkT[:, 512 * et:512 * (et + 1)],
                             start=(et == 0), stop=(et == NT - 1))
                for et in range(NT):
                    inst = e.matmul(ps_q[:, 512:1024],
                                    xnc[:, PC * et + t0:PC * et + t0 + 128],
                                    wvT[:, 512 * et:512 * (et + 1)],
                                    start=(et == 0), stop=(et == NT - 1))
                return inst
            p.step("tensor", kvmm, reads=[wkT_b, wvT_b, xnc_b], writes=[ps_q_b[0], ps_q_b[1]], tag=f"g_kv{pc}_{gb}")
            p.step("vector", lambda e: e.tensor_add(kpre[:], ps_q[:, 0:512], bkb[:, 0:512]),
                   reads=[ps_q_b[0], bkb_b], writes=[kpre_b], tag=f"g_kev{pc}_{gb}")
            p.step("vector", lambda e: e.tensor_add(vF[:], ps_q[:, 512:1024], bkb[:, 512:1024]),
                   reads=[ps_q_b[1], bkb_b], writes=[vAB_b], tag=f"g_vev{pc}_{gb}")
            p.step("gpsimd", lambda e: e.tensor_copy(vQ1[:], vF[32:64, :]), reads=[vAB_b], writes=[vAB_b])
            p.step("gpsimd", lambda e: e.tensor_copy(vH[:], vF[64:128, :]), reads=[vAB_b], writes=[vAB_b])
            p.step("gpsimd", lambda e: e.tensor_copy(vQ3[:], vH[32:64, :]), reads=[vAB_b], writes=[vAB_b])
            p.step("scalar", lambda e: e.activation(Ee[:], kpre[:], AF.Exp, scale=-1.0),
                   reads=[kpre_b], writes=[E_b], tag=f"g_E{pc}_{gb}")
            p.step("vector", lambda e: e.tensor_scalar_add(Rr[:], Ee[:], 1.0), reads=[E_b], writes=[R_b])
            p.step("vector", lambda e: e.reciprocal(Rr[:], Rr[:]), reads=[R_b], writes=[R_b])
            p.step("vector", lambda e: e.tensor_scalar_min(kf[:], Rr[:], 0.95), reads=[R_b], writes=[kf_b])
            p.step("gpsimd", lambda e: e.tensor_mul(uu[:], Ee[:], Rr[:]), reads=[E_b, R_b], writes=[u_b])
            p.step("vector", lambda e: e.tensor_scalar_max(uu[:], uu[:], 0.05), reads=[u_b], writes=[u_b])
            p.step("scalar", lambda e: e.activation(gg[:], uu[:], AF.Ln), reads=[u_b], writes=[g_b], tag=f"g_g{pc}_{gb}")
            p.step("tensor", lambda e: e.matmul(ps_r[:], cst["GLA_R"][:], gg[:], start=True, stop=True),
                   reads=[g_b], writes=[ps_r_b], tag=f"g_mr{pc}_{gb}")
            def ctmm(e):
                for g4 in range(4):
                    inst = e.matmul(ps_cT[:, 128 * g4:128 * (g4 + 1)],
                                    gg[:, 128 * g4:128 * (g4 + 1)], cst["GLA_L"][:],
                                    start=True, stop=True)
                return inst
            p.step("tensor", ctmm, reads=[g_b], writes=[ps_cT_b], tag=f"g_mcT{pc}_{gb}")
            def kftmm(e):
                for g4 in range(4):
                    inst = e.transpose(ps_kT[:, 128 * g4:128 * (g4 + 1)],
                                       kf[:, 128 * g4:128 * (g4 + 1)], cst["IDF"][:])
                return inst
            p.step("tensor", kftmm, reads=[kf_b], writes=[ps_kT_b], tag=f"g_mkT{pc}_{gb}")
            p.step("scalar", lambda e: e.activation(er[:], ps_r[:], AF.Exp), reads=[ps_r_b], writes=[er_b])
            p.step("vector", lambda e: e.tensor_mul(knF[:], kf[:], er[:]),
                   reads=[kf_b, er_b], writes=[knAB_b])
            p.step("gpsimd", lambda e: e.tensor_copy(knQ1[:], knF[32:64, :]), reads=[knAB_b], writes=[knAB_b])
            p.step("gpsimd", lambda e: e.tensor_copy(knH[:], knF[64:128, :]), reads=[knAB_b], writes=[knAB_b])
            p.step("gpsimd", lambda e: e.tensor_copy(knQ3[:], knH[32:64, :]), reads=[knAB_b], writes=[knAB_b])
            p.step("vector", lambda e: e.tensor_copy(cT_sb[:], ps_cT[:]), reads=[ps_cT_b], writes=[cT_b])
            def csf(e):
                for g4 in range(4):
                    for sb_ in range(4):
                        o = 128 * g4 + 32 * sb_
                        inst = e.tensor_scalar(cs[:, o:o + 32], cT_sb[:, o:o + 32],
                                               cT_sb[:, o + 15:o + 16], None, ALU.subtract)
                return inst
            p.step("vector", csf, reads=[cT_b], writes=[cs_b], tag=f"g_cs{pc}_{gb}")
            p.step("scalar", lambda e: e.activation(eqc[:], cs[:], AF.Exp), reads=[cs_b], writes=[eqc_b])
            p.step("scalar", lambda e: e.activation(emc[:], cs[:], AF.Exp, scale=-1.0), reads=[cs_b], writes=[emc_b])
            p.step("vector", lambda e: e.tensor_mul(khatT[:], ps_kT[:], emc[:]),
                   reads=[ps_kT_b, emc_b], writes=[khatT_b])
            def ectf(e):
                cr = cT_sb[:].rearrange("p (g b t) -> p g b t", g=4, t=32)
                er_ = ect[:].rearrange("p (g b) -> p g b", g=4)
                for g4 in range(4):
                    inst = e.activation(er_[:, g4, :], cr[:, g4, :, 31:32].rearrange("p b t -> p (b t)"), AF.Exp)
                return inst
            p.step("scalar", ectf, reads=[cT_b], writes=[ect_b], tag=f"g_ect{pc}_{gb}")
            def ecmf(e):
                cr = cT_sb[:].rearrange("p (g b t) -> p g b t", g=4, t=32)
                er_ = ecm[:].rearrange("p (g b) -> p g b", g=4)
                for g4 in range(4):
                    inst = e.activation(er_[:, g4, :], cr[:, g4, :, 15:16].rearrange("p b t -> p (b t)"), AF.Exp)
                return inst
            p.step("scalar", ecmf, reads=[cT_b], writes=[ecm_b], tag=f"g_ecm{pc}_{gb}")
            gbt = PC * pc + t0
            xb2 = (pc * 4 + gb) % 2
            p.dma("sync", (lambda gbt, xb2: lambda e, s: e.dma_start(
                out=x2c[xb2][:].rearrange("p (j t) -> p j t", t=128),
                in_=fm_view(x2fm)[:, :, gbt:gbt + 128]).then_inc(s, 16))(gbt, xb2),
                writes=[x2c_b[xb2]], tag=f"g_x2l{pc}_{gb}")
            for g4 in range(4):
                def qstf(e, g4=g4, t0=t0):
                    for s_ in range(4):
                        for r_ in range(4):
                            inst = e.tensor_mul(qst[:, 128 * s_ + 32 * r_:128 * s_ + 32 * (r_ + 1)],
                                                qT[:, PC * (4 * g4 + r_) + t0 + 32 * s_:PC * (4 * g4 + r_) + t0 + 32 * (s_ + 1)],
                                                eqc[:, 128 * g4 + 32 * s_:128 * g4 + 32 * (s_ + 1)])
                    return inst
                p.step("vector", qstf, reads=[qT_b, eqc_b], writes=[qst_b], tag=f"g_qst{pc}_{gb}_{g4}")
                halves = [(knF, vF), (knQ1, vQ1), (knH, vH), (knQ3, vQ3)]
                for s_ in range(4):
                    kn_h, v_h = halves[s_]
                    ro = 0
                    p.step("scalar", (lambda g4, s_: lambda e: e.mul(
                        Sbf[:, 128 * g4:128 * (g4 + 1)], Sst[:, 128 * g4:128 * (g4 + 1)],
                        ecm[:, 4 * g4 + s_:4 * g4 + s_ + 1]))(g4, s_),
                        reads=[S_b[g4], ecm_b], writes=[Sbf_b[g4]], tag=f"g_sbr{pc}_{gb}_{g4}_{s_}")
                    p.step("tensor", (lambda g4, s_: lambda e: e.matmul(
                        ps_c[0:32, 0:128], khatT[:, 128 * g4 + 32 * s_:128 * g4 + 32 * (s_ + 1)],
                        qst[:, 128 * s_:128 * (s_ + 1)], start=True, stop=True))(g4, s_),
                        reads=[khatT_b, qst_b], writes=[ps_c_b], tag=f"g_pmm{pc}_{gb}_{g4}_{s_}")
                    p.step("vector", lambda e: e.tensor_mul(pT_sb[:], ps_c[0:32, 0:128], cst["GLA_PMASK"][:]),
                           reads=[ps_c_b], writes=[pT_b], tag=f"g_pev{pc}_{gb}_{g4}_{s_}")
                    def omm(e, g4=g4, s_=s_, kn_h=kn_h, v_h=v_h, ro=ro):
                        e.matmul(ps_o[:, 128 * s_:128 * (s_ + 1)], Sbf[:, 128 * g4:128 * (g4 + 1)],
                                 qst[:, 128 * s_:128 * (s_ + 1)], start=True, stop=False)
                        return e.matmul(ps_o[:, 128 * s_:128 * (s_ + 1)], v_h[ro:ro + 32, 128 * g4:128 * (g4 + 1)],
                                        pT_sb[:], start=False, stop=True)
                    ost = p.step("tensor", omm, reads=[Sbf_b[g4], qst_b, vAB_b, pT_b], writes=[ps_o_b],
                           tag=f"g_omm{pc}_{gb}_{g4}_{s_}")
                    if dbg_ap is not None and pc == 7 and gb == 3 and g4 == 3 and s_ == 0:
                        dbg_sb = p.sbuf("g_dbg", [128, 512], F32)
                        dbg_b = p.buf()
                        p.step("vector", lambda e: e.tensor_copy(dbg_sb[:, 0:128], Sbf[:, 384:512]),
                               reads=[Sbf_b[3]], writes=[dbg_b], deps=[ost], tag="g_dbg1")
                        p.step("vector", lambda e: e.tensor_copy(dbg_sb[:, 128:256], ps_o[:, 0:128]),
                               reads=[ps_o_b], writes=[dbg_b], deps=[ost], tag="g_dbg2")
                        p.step("vector", lambda e: e.tensor_copy(dbg_sb[:, 256:384], Sst[:, 384:512]),
                               reads=[S_b[3]], writes=[dbg_b], deps=[ost], tag="g_dbg2b")
                        p.step("vector", lambda e: e.tensor_copy(dbg_sb[:, 384:512], ecm[:].rearrange("p f -> p f").to_broadcast([128, 128]) if False else eqc[:, 384:512]),
                               reads=[eqc_b], writes=[dbg_b], deps=[ost], tag="g_dbg2c")
                        p.dma("sync", lambda e, s: e.dma_start(out=dbg_ap, in_=dbg_sb[:]).then_inc(s, 16),
                              reads=[dbg_b], tag="g_dbg3")
                    p.step("tensor", (lambda g4, s_, kn_h=kn_h, v_h=v_h, ro=ro: lambda e: e.matmul(
                        ps_su[:], kn_h[ro:ro + 32, 128 * g4:128 * (g4 + 1)],
                        v_h[ro:ro + 32, 128 * g4:128 * (g4 + 1)], start=True, stop=True))(g4, s_),
                        reads=[knAB_b, vAB_b], writes=[ps_su_b], tag=f"g_smm{pc}_{gb}_{g4}_{s_}")
                    p.step("vector", (lambda g4, s_: lambda e: e.scalar_tensor_tensor(
                        Sst[:, 128 * g4:128 * (g4 + 1)], Sst[:, 128 * g4:128 * (g4 + 1)],
                        ect[:, 4 * g4 + s_:4 * g4 + s_ + 1], ps_su[:], ALU.mult, ALU.add))(g4, s_),
                        reads=[ps_su_b, ect_b, S_b[g4]], writes=[S_b[g4]], tag=f"g_sup{pc}_{gb}_{g4}_{s_}")
                po = ps_o[:].rearrange("p (s2 r t) -> p s2 r t", r=4, t=32)
                for r_ in range(4):
                    h = 4 * g4 + r_
                    p.step("vector", (lambda h, r_, xb2: lambda e: e.tensor_add(
                        x3c[xb2][:, 128 * h:128 * h + 128].rearrange("p (s2 t) -> p s2 t", t=32),
                        po[:, :, r_, :],
                        x2c[xb2][:, 128 * h:128 * (h + 1)].rearrange("p (s2 t) -> p s2 t", t=32)))(h, r_, xb2),
                        reads=[ps_o_b, x2c_b[xb2]], writes=[x3c_b[xb2]], tag=f"g_oev{pc}_{gb}_{g4}_{r_}")
            p.dma("sync", (lambda gbt, xb2: lambda e, s: e.dma_start(
                out=fm_view(x3fm)[:, :, gbt:gbt + 128],
                in_=x3c[xb2][:].rearrange("p (j t) -> p j t", t=128)).then_inc(s, 16))(gbt, xb2),
                reads=[x3c_b[xb2]], tag=f"g_x3s{pc}_{gb}")
    p.pop_scope()
    return p.barrier("g_end")


L3_CONSTS = ["IDF", "ONESCOL", "ONESROW", "GLA_L", "GLA_R", "GLA_PMASK"]


def build_launch3(dbg=False):
    nc = bass.Bass("TRN2", target_bir_lowering=False, debug=False)
    x2fm = nc.dram_tensor("x2fm", [D, S], F32, kind="ExternalInput").ap()
    Wq = nc.dram_tensor("Wq", [D, D], F32, kind="ExternalInput").ap()
    bq = nc.dram_tensor("bq", [D], F32, kind="ExternalInput").ap()
    Wk = nc.dram_tensor("Wk", [512, D], F32, kind="ExternalInput").ap()
    bk = nc.dram_tensor("bk", [512], F32, kind="ExternalInput").ap()
    Wv = nc.dram_tensor("Wv", [512, D], F32, kind="ExternalInput").ap()
    bv = nc.dram_tensor("bv", [512], F32, kind="ExternalInput").ap()
    x3fm = nc.dram_tensor("x3fm", [D, S], F32, kind="ExternalOutput").ap()
    knd = "ExternalOutput" if dbg else "Internal"
    xn_hbm = nc.dram_tensor("xn_s", [D, S], BF16, kind=knd).ap()
    wqb = nc.dram_tensor("wqb", [D, D], BF16, kind="Internal").ap()
    wkb = nc.dram_tensor("wkb", [512, D], BF16, kind="Internal").ap()
    wvb = nc.dram_tensor("wvb", [512, D], BF16, kind="Internal").ap()

    p = Prog(nc)
    cst = declare_consts(nc, p, L3_CONSTS)
    p.dma("gpsimd", lambda e, s: [e.dma_start(out=wqb, in_=Wq).then_inc(s, 16),
                                  e.dma_start(out=wkb, in_=Wk).then_inc(s, 16),
                                  e.dma_start(out=wvb, in_=Wv).then_inc(s, 16)][-1], n_dma=3, tag="g_wc")
    dbg_ap = nc.dram_tensor("dbg", [128, 512], F32, kind="ExternalOutput").ap() if dbg else None
    emit_ln(p, nc, x2fm, xn_hbm, cst, "ln1a")
    emit_l1_attn(p, nc, x2fm, xn_hbm, wqb, wkb, wvb, bq, bk, bv, x3fm, cst, dbg_ap)
    p.emit()
    return nc




def emit_l1_ffn(p, nc, x3fm, xn_hbm, wcb, genes_ap, wdb, wub, temp_ap, scale_ap,
                w_hbm, out_tm, cst):
    G = 4096
    p.push_scope()
    srow = p.sbuf("m_srow", [1, 2], F32)
    srow_b = p.buf()
    p.dma("sync", lambda e, s: [e.dma_start(out=srow[0:1, 0:1], in_=temp_ap[None, :]).then_inc(s, 16),
                                e.dma_start(out=srow[0:1, 1:2], in_=scale_ap[None, :]).then_inc(s, 16)][-1],
          writes=[srow_b], n_dma=2, tag="m_sld")
    p.step("vector", lambda e: e.tensor_scalar_max(srow[0:1, 0:1], srow[0:1, 0:1], 0.1),
           reads=[srow_b], writes=[srow_b])
    p.step("vector", lambda e: e.reciprocal(srow[0:1, 0:1], srow[0:1, 0:1]),
           reads=[srow_b], writes=[srow_b])
    sccol = p.sbuf("m_sccol", [128, 2], F32)
    sccol_b = p.buf()
    p.push_scope()
    ps_tmp = p.psum("m_pst", [128, 128])
    ps_tmp_b = p.buf()
    p.step("tensor", lambda e: e.matmul(ps_tmp[:, 0:2], cst["ONESROW"][0:1, :], srow[0:1, 0:2], start=True, stop=True),
           reads=[srow_b], writes=[ps_tmp_b], tag="m_scbc")
    p.step("vector", lambda e: e.tensor_copy(sccol[:], ps_tmp[:, 0:2]), reads=[ps_tmp_b], writes=[sccol_b])
    p.pop_scope()
    p.barrier("m_pre")

    p.push_scope()
    wcT = p.sbuf("m_wcT", [128, NT * G], BF16)
    wcT_b = p.buf()

    def wcl(e, s):
        for et in range(NT):
            e.dma_start(out=wcT[:, G * et:G * (et + 1)],
                        in_=wcb[:, 128 * et:128 * (et + 1)], transpose=True).then_inc(s, 16)
    p.dma("sync", wcl, writes=[wcT_b], n_dma=NT, tag="m_wcl")
    xnt = [p.sbuf(f"m_xnt{i}", [128, NT * 128], BF16) for i in range(2)]
    xnt_b = [p.buf(), p.buf()]
    lg2 = [p.sbuf(f"m_lg{i}", [128, G], BF16) for i in range(2)]
    lg2_b = [p.buf(), p.buf()]
    sa = p.sbuf("m_sa", [128, G], BF16)
    sb_ = p.sbuf("m_sb", [128, G], BF16)
    m8 = p.sbuf("m_m8", [128, 40], BF16)
    m8f = p.sbuf("m_m8f", [128, 2], F32)
    ge = p.sbuf("m_ge", [128, G], BF16)
    ee = p.sbuf("m_ee", [128, G], BF16)
    wb = p.sbuf("m_wb", [128, G], BF16)
    sa_b, sb_b, m8_b, ge_b, ee_b, wb_b = [p.buf() for _ in range(6)]
    ps_l = [p.psum(f"m_psl{i}", [128, 2048]) for i in range(2)]
    ps_l_b = [p.buf(), p.buf()]
    for tt in range(S // 128):
        bi = tt % 2
        p.dma("gpsimd", (lambda bi, tt: lambda e, s: e.dma_start(
            out=xnt[bi][:].rearrange("p (j t) -> p j t", t=128),
            in_=fm_view(xn_hbm)[:, :, 128 * tt:128 * (tt + 1)]).then_inc(s, 16))(bi, tt),
            writes=[xnt_b[bi]], tag=f"m_xl{tt}")
        lg = lg2[bi]
        lg_b = lg2_b[bi]
        for gh in range(2):
            def lmm(e, bi=bi, gh=gh):
                for et in range(NT):
                    for q in range(4):
                        inst = e.matmul(ps_l[gh][:, 512 * q:512 * (q + 1)],
                                        xnt[bi][:, 128 * et:128 * (et + 1)],
                                        wcT[:, G * et + 2048 * gh + 512 * q:G * et + 2048 * gh + 512 * (q + 1)],
                                        start=(et == 0), stop=(et == NT - 1))
                return inst
            p.step("tensor", lmm, reads=[xnt_b[bi], wcT_b], writes=[ps_l_b[gh]], tag=f"m_lmm{tt}_{gh}")
            p.step("scalar", (lambda gh, lg=lg: lambda e: e.mul(
                lg[:, 2048 * gh:2048 * (gh + 1)], ps_l[gh][:], sccol[:, 0:1]))(gh),
                reads=[ps_l_b[gh], sccol_b], writes=[lg_b], tag=f"m_lev{tt}_{gh}")
        p.step("vector", (lambda lg: lambda e: e.max(out=m8[:, 0:8], in_=lg[:]))(lg), reads=[lg_b], writes=[m8_b])
        p.step("vector", (lambda lg: lambda e: e.match_replace(out=sa[:], in_to_replace=m8[:, 0:8], in_values=lg[:], imm_value=-1e30))(lg),
               reads=[lg_b, m8_b], writes=[sa_b])
        p.step("vector", lambda e: e.max(out=m8[:, 8:16], in_=sa[:]), reads=[sa_b], writes=[m8_b])
        p.step("vector", lambda e: e.match_replace(out=sb_[:], in_to_replace=m8[:, 8:16], in_values=sa[:], imm_value=-1e30),
               reads=[sa_b, m8_b], writes=[sb_b])
        p.step("vector", lambda e: e.max(out=m8[:, 16:24], in_=sb_[:]), reads=[sb_b], writes=[m8_b])
        p.step("vector", lambda e: e.match_replace(out=sa[:], in_to_replace=m8[:, 16:24], in_values=sb_[:], imm_value=-1e30),
               reads=[sb_b, m8_b], writes=[sa_b])
        p.step("vector", lambda e: e.max(out=m8[:, 24:32], in_=sa[:]), reads=[sa_b], writes=[m8_b])
        p.step("vector", lambda e: e.tensor_copy(m8f[:, 0:1], m8[:, 31:32]), reads=[m8_b], writes=[m8_b])
        p.step("vector", lambda e: e.tensor_scalar_mul(m8f[:, 1:2], m8[:, 0:1], -1.0), reads=[m8_b], writes=[m8_b])
        p.step("gpsimd", (lambda lg: lambda e: e.tensor_scalar(ge[:], lg[:], m8f[:, 0:1], None, ALU.is_ge))(lg),
               reads=[lg_b, m8_b], writes=[ge_b], tag=f"m_ge{tt}")
        p.step("scalar", (lambda lg: lambda e: e.activation(ee[:], lg[:], AF.Exp, bias=m8f[:, 1:2]))(lg),
               reads=[lg_b, m8_b], writes=[ee_b], tag=f"m_exp{tt}")
        p.step("gpsimd", lambda e: e.tensor_mul(wb[:], ee[:], ge[:]), reads=[ee_b, ge_b], writes=[wb_b], tag=f"m_w{tt}")
        p.dma("sync", (lambda tt: lambda e, s: e.dma_start(
            out=w_hbm[128 * tt:128 * (tt + 1), :], in_=wb[:]).then_inc(s, 16))(tt),
            reads=[wb_b], tag=f"m_ws{tt}")
    p.pop_scope()
    p.barrier("m_A_end")

    p.push_scope()
    TC = 512
    ntc = S // TC
    gen = p.sbuf("m_gen", [128, 32 * 256], BF16)
    gen_b = p.buf()

    def genl(e, s):
        e.dma_start(out=gen[:].rearrange("p (j c) -> p j c", c=256),
                    in_=genes_ap.rearrange("(j p) c -> p j c", p=128)).then_inc(s, 16)
    p.dma("gpsimd", genl, writes=[gen_b], n_dma=1, tag="m_genl")
    wdT = p.sbuf("m_wdT", [128, 2 * D], BF16)
    wuT = p.sbuf("m_wuT", [128, 2 * D], BF16)
    wdT_b, wuT_b = p.buf(), p.buf()

    def wdl(e, s):
        for ct in range(2):
            e.dma_start(out=wdT[:, D * ct:D * (ct + 1)],
                        in_=wdb[:, 128 * ct:128 * (ct + 1)], transpose=True).then_inc(s, 16)
            e.dma_start(out=wuT[:, D * ct:D * (ct + 1)],
                        in_=wub[:, 128 * ct:128 * (ct + 1)], transpose=True).then_inc(s, 16)
    p.dma("sync", wdl, writes=[wdT_b, wuT_b], n_dma=4, tag="m_wdl")
    wT = [p.sbuf(f"m_wT{i}", [128, 32 * TC], BF16) for i in range(2)]
    wT_b = [p.buf(), p.buf()]
    xnc = p.sbuf("m_xnc", [128, NT * TC], BF16)
    xnc_b = p.buf()
    x3c = p.sbuf("m_x3c", [128, NT * TC], F32)
    x3c_b = p.buf()
    rz = p.sbuf("m_rz", [1, TC], F32)
    rz_b = p.buf()
    rzb = p.sbuf("m_rzb", [128, TC], F32)
    rzb_b = p.buf()
    exT = p.sbuf("m_exT", [128, 2 * TC], BF16)
    exT_b = p.buf()
    dn = p.sbuf("m_dn", [128, TC], BF16)
    up = p.sbuf("m_up", [128, TC], BF16)
    dn_b, up_b = p.buf(), p.buf()
    t1 = p.sbuf("m_t1", [128, TC], BF16)
    t1_b = p.buf()
    ofm = p.sbuf("m_ofm", [128, NT * TC], F32)
    ofm_b = p.buf()
    otr = [p.sbuf(f"m_otr{i}", [128, D], F32) for i in range(2)]
    otr_b = [p.buf(), p.buf()]
    ps_z = p.psum("m_psz", [1, TC])
    ps_z_b = p.buf()
    ps_b1 = p.psum("m_psb1", [128, TC])
    ps_b1_b = p.buf()
    ps_b2 = p.psum("m_psb2", [128, TC])
    ps_b2_b = p.buf()
    ps_b3 = p.psum("m_psb3", [128, TC])
    ps_b3_b = p.buf()
    ps_tr = p.psum("m_pstr", [128, D])
    ps_tr_b = p.buf()
    for tc in range(ntc):
        wbi = tc % 2
        def wTl(e, s, tc=tc, wbi=wbi):
            for gt in range(32):
                e.dma_start(out=wT[wbi][:, TC * gt:TC * (gt + 1)],
                            in_=w_hbm[TC * tc:TC * (tc + 1), 128 * gt:128 * (gt + 1)], transpose=True).then_inc(s, 16)
        p.dma("sync", wTl, writes=[wT_b[wbi]], n_dma=32, tag=f"m_wTl{tc}")
        p.dma("gpsimd", (lambda tc: lambda e, s: e.dma_start(
            out=xnc[:].rearrange("p (j t) -> p j t", t=TC),
            in_=fm_view(xn_hbm)[:, :, TC * tc:TC * (tc + 1)]).then_inc(s, 16))(tc),
            writes=[xnc_b], tag=f"m_xnl{tc}")
        p.dma("gpsimd", (lambda tc: lambda e, s: e.dma_start(
            out=x3c[:].rearrange("p (j t) -> p j t", t=TC),
            in_=fm_view(x3fm)[:, :, TC * tc:TC * (tc + 1)]).then_inc(s, 16))(tc),
            writes=[x3c_b], tag=f"m_x3l{tc}")

        def zmm(e, wbi=wbi):
            for gt in range(32):
                inst = e.matmul(ps_z[0:1, :], cst["ONESCOL"][:], wT[wbi][:, TC * gt:TC * (gt + 1)],
                                start=(gt == 0), stop=(gt == 31))
            return inst
        p.step("tensor", zmm, reads=[wT_b[wbi]], writes=[ps_z_b], tag=f"m_zmm{tc}")
        p.step("vector", lambda e: e.reciprocal(rz[:], ps_z[:]), reads=[ps_z_b], writes=[rz_b])
        def zbc(e):
            for q in range(TC // 512):
                inst = e.matmul(ps_b1[:, 512 * q:512 * (q + 1)], cst["ONESROW"][0:1, :],
                                rz[0:1, 512 * q:512 * (q + 1)], start=True, stop=True)
            return inst
        p.step("tensor", zbc, reads=[rz_b], writes=[ps_b1_b], tag=f"m_zbc{tc}")
        p.step("vector", lambda e: e.tensor_copy(rzb[:], ps_b1[:]), reads=[ps_b1_b], writes=[rzb_b])
        for ct in range(2):
            psx, psx_b = (ps_b2, ps_b2_b) if ct == 0 else (ps_b3, ps_b3_b)
            def emm(e, ct=ct, psx=psx, wbi=wbi):
                for gt in range(32):
                    inst = e.matmul(psx[:], gen[:, 256 * gt + 128 * ct:256 * gt + 128 * (ct + 1)],
                                    wT[wbi][:, TC * gt:TC * (gt + 1)],
                                    start=(gt == 0), stop=(gt == 31))
                return inst
            p.step("tensor", emm, reads=[gen_b, wT_b[wbi]], writes=[psx_b], tag=f"m_emm{tc}_{ct}")
            p.step("vector", (lambda ct, psx: lambda e: e.tensor_mul(
                exT[:, TC * ct:TC * (ct + 1)], psx[:], rzb[:]))(ct, psx),
                reads=[psx_b, rzb_b], writes=[exT_b], tag=f"m_eev{tc}_{ct}")
        for ft in range(NT):
            def dmm(e, ft=ft):
                for ct in range(2):
                    inst = e.matmul(ps_b2[:], wdT[:, D * ct + 128 * ft:D * ct + 128 * (ft + 1)],
                                    exT[:, TC * ct:TC * (ct + 1)], start=(ct == 0), stop=(ct == 1))
                return inst
            p.step("tensor", dmm, reads=[wdT_b, exT_b], writes=[ps_b2_b], tag=f"m_dmm{tc}_{ft}")
            p.step("scalar", lambda e: e.activation(dn[:], ps_b2[:], AF.Tanh), reads=[ps_b2_b], writes=[dn_b])
            def umm(e, ft=ft):
                for ct in range(2):
                    inst = e.matmul(ps_b3[:], wuT[:, D * ct + 128 * ft:D * ct + 128 * (ft + 1)],
                                    exT[:, TC * ct:TC * (ct + 1)], start=(ct == 0), stop=(ct == 1))
                return inst
            p.step("tensor", umm, reads=[wuT_b, exT_b], writes=[ps_b3_b], tag=f"m_umm{tc}_{ft}")
            p.step("scalar", lambda e: e.activation(up[:], ps_b3[:], AF.Copy, scale=sccol[:, 1:2]),
                   reads=[ps_b3_b, sccol_b], writes=[up_b])
            p.step("vector", lambda e: e.tensor_mul(t1[:], dn[:], up[:]), reads=[dn_b, up_b], writes=[t1_b])
            p.step("gpsimd", (lambda ft: lambda e: e.tensor_mul(t1[:], t1[:], xnc[:, TC * ft:TC * (ft + 1)]))(ft),
                   reads=[t1_b, xnc_b], writes=[t1_b])
            p.step("vector", (lambda ft: lambda e: e.tensor_add(
                ofm[:, TC * ft:TC * (ft + 1)], t1[:], x3c[:, TC * ft:TC * (ft + 1)]))(ft),
                reads=[t1_b, x3c_b], writes=[ofm_b], tag=f"m_oev{tc}_{ft}")
        for tts in range(TC // 128):
            ob = tts % 2
            def trf(e, tts=tts):
                for ft in range(NT):
                    inst = e.transpose(ps_tr[:, 128 * ft:128 * (ft + 1)],
                                       ofm[:, TC * ft + 128 * tts:TC * ft + 128 * (tts + 1)], cst["IDF"][:])
                return inst
            p.step("tensor", trf, reads=[ofm_b], writes=[ps_tr_b], tag=f"m_tr{tc}_{tts}")
            p.step("vector", (lambda ob: lambda e: e.tensor_copy(otr[ob][:], ps_tr[:]))(ob),
                   reads=[ps_tr_b], writes=[otr_b[ob]], tag=f"m_trv{tc}_{tts}")
            p.dma("sync", (lambda tc, tts, ob: lambda e, s: e.dma_start(
                out=out_tm[TC * tc + 128 * tts:TC * tc + 128 * (tts + 1), :], in_=otr[ob][:]).then_inc(s, 16))(tc, tts, ob),
                reads=[otr_b[ob]], tag=f"m_os{tc}_{tts}")
    p.pop_scope()
    p.pop_scope()
    return p.barrier("m_end")


L4_CONSTS = ["IDF", "ONESCOL", "ONESROW"]


def build_launch4(dbg=False):
    nc = bass.Bass("TRN2", target_bir_lowering=False, debug=False)
    x3fm = nc.dram_tensor("x3fm", [D, S], F32, kind="ExternalInput").ap()
    Wc = nc.dram_tensor("Wc", [4096, D], F32, kind="ExternalInput").ap()
    temp = nc.dram_tensor("temp", [1], F32, kind="ExternalInput").ap()
    genes = nc.dram_tensor("genes", [4096, 256], F32, kind="ExternalInput").ap()
    Wd = nc.dram_tensor("Wd", [D, 256], F32, kind="ExternalInput").ap()
    Wu = nc.dram_tensor("Wu", [D, 256], F32, kind="ExternalInput").ap()
    nw = nc.dram_tensor("norm_w", [D], F32, kind="ExternalInput").ap()
    scale = nc.dram_tensor("scale", [1], F32, kind="ExternalInput").ap()
    out_tm = nc.dram_tensor("out", [S, D], F32, kind="ExternalOutput").ap()
    knd = "ExternalOutput" if dbg else "Internal"
    xn_hbm = nc.dram_tensor("xn_s", [D, S], BF16, kind=knd).ap()
    w_hbm = nc.dram_tensor("w_s", [S, 4096], BF16, kind=knd).ap()
    wcb = nc.dram_tensor("wcb", [4096, D], BF16, kind="Internal").ap()
    wdb = nc.dram_tensor("wdb", [D, 256], BF16, kind="Internal").ap()
    wub = nc.dram_tensor("wub", [D, 256], BF16, kind="Internal").ap()

    p = Prog(nc)
    cst = declare_consts(nc, p, L4_CONSTS)
    p.dma("gpsimd", lambda e, s: [e.dma_start(out=wcb, in_=Wc).then_inc(s, 16),
                                  e.dma_start(out=wdb, in_=Wd).then_inc(s, 16),
                                  e.dma_start(out=wub, in_=Wu).then_inc(s, 16)][-1], n_dma=3, tag="m_wc")
    nw_col = p.sbuf("m_nw", [128, NT], F32)
    p.push_scope()
    pst_sc = p.psum("m_scps", [128, 128])
    pst_sc_b = p.buf()
    nw_b = emit_scale_cols(p, nc, nw, NT, "mnw", cst, pst_sc, pst_sc_b, nw_col)
    p.pop_scope()
    p.barrier("m_nw_end")
    emit_ln(p, nc, x3fm, xn_hbm, cst, "ln1f", nw_col=nw_col, nw_col_buf=nw_b)
    emit_l1_ffn(p, nc, x3fm, xn_hbm, wcb, genes, wdb, wub, temp, scale, w_hbm, out_tm, cst)
    p.emit()
    return nc




ALL_CONSTS = ["IDF", "ONESCOL", "ONESROW", "TRI_LO", "TRI_HI", "GLA_L", "GLA_R", "GLA_PMASK"]


def build_fused():
    nc = bass.Bass("TRN2", target_bir_lowering=False, debug=False)
    x_in = nc.dram_tensor("x", [S, D], F32, kind="ExternalInput").ap()
    wqkv = nc.dram_tensor("w_qkv", [3 * D, D], F32, kind="ExternalInput").ap()
    wout = nc.dram_tensor("w_out", [D, D], F32, kind="ExternalInput").ap()
    wg8 = nc.dram_tensor("w_gate", [D, 8192], I8, kind="ExternalInput").ap()
    wu8 = nc.dram_tensor("w_up", [D, 8192], I8, kind="ExternalInput").ap()
    wd8 = nc.dram_tensor("w_down", [8192, D], I8, kind="ExternalInput").ap()
    sg = nc.dram_tensor("s_gate", [8192], F32, kind="ExternalInput").ap()
    su = nc.dram_tensor("s_up", [8192], F32, kind="ExternalInput").ap()
    sd = nc.dram_tensor("s_down", [D], F32, kind="ExternalInput").ap()
    Wq = nc.dram_tensor("Wq", [D, D], F32, kind="ExternalInput").ap()
    bq = nc.dram_tensor("bq", [D], F32, kind="ExternalInput").ap()
    Wk = nc.dram_tensor("Wk", [512, D], F32, kind="ExternalInput").ap()
    bk = nc.dram_tensor("bk", [512], F32, kind="ExternalInput").ap()
    Wv = nc.dram_tensor("Wv", [512, D], F32, kind="ExternalInput").ap()
    bv = nc.dram_tensor("bv", [512], F32, kind="ExternalInput").ap()
    Wc = nc.dram_tensor("Wc", [4096, D], F32, kind="ExternalInput").ap()
    temp = nc.dram_tensor("temp", [1], F32, kind="ExternalInput").ap()
    genes = nc.dram_tensor("genes", [4096, 256], F32, kind="ExternalInput").ap()
    Wd = nc.dram_tensor("Wd", [D, 256], F32, kind="ExternalInput").ap()
    Wu = nc.dram_tensor("Wu", [D, 256], F32, kind="ExternalInput").ap()
    nw = nc.dram_tensor("norm_w", [D], F32, kind="ExternalInput").ap()
    scale = nc.dram_tensor("scale", [1], F32, kind="ExternalInput").ap()
    out_tm = nc.dram_tensor("out", [S, D], F32, kind="ExternalOutput").ap()

    xfm = nc.dram_tensor("xfm_s", [D, S], F32, kind="Internal").ap()
    x1fm = nc.dram_tensor("x1fm_s", [D, S], F32, kind="Internal").ap()
    x2fm = nc.dram_tensor("x2fm_s", [D, S], F32, kind="Internal").ap()
    x3fm = nc.dram_tensor("x3fm_s", [D, S], F32, kind="Internal").ap()
    xn_hbm = nc.dram_tensor("xn_s", [D, S], BF16, kind="Internal").ap()
    attnT = nc.dram_tensor("attnT_s", [D, S], BF16, kind="Internal").ap()
    fuse_hbm = nc.dram_tensor("fuse_s", [8192, S], BF16, kind="Internal").ap()
    w_hbm = nc.dram_tensor("w_s", [S, 4096], BF16, kind="Internal").ap()
    wqkv_b = nc.dram_tensor("wqkv_b", [3 * D, D], BF16, kind="Internal").ap()
    wout_b = nc.dram_tensor("wout_b", [D, D], BF16, kind="Internal").ap()
    wqb = nc.dram_tensor("wqb", [D, D], BF16, kind="Internal").ap()
    wkb = nc.dram_tensor("wkb", [512, D], BF16, kind="Internal").ap()
    wvb = nc.dram_tensor("wvb", [512, D], BF16, kind="Internal").ap()
    wcb = nc.dram_tensor("wcb", [4096, D], BF16, kind="Internal").ap()
    wdb = nc.dram_tensor("wdb", [D, 256], BF16, kind="Internal").ap()
    wub = nc.dram_tensor("wub", [D, 256], BF16, kind="Internal").ap()

    p = Prog(nc)
    cst = declare_consts(nc, p, ALL_CONSTS)
    def wcasts(e, s):
        e.dma_start(out=wqkv_b, in_=wqkv).then_inc(s, 16)
        e.dma_start(out=wout_b, in_=wout).then_inc(s, 16)
        e.dma_start(out=wqb, in_=Wq).then_inc(s, 16)
        e.dma_start(out=wkb, in_=Wk).then_inc(s, 16)
        e.dma_start(out=wvb, in_=Wv).then_inc(s, 16)
        e.dma_start(out=wcb, in_=Wc).then_inc(s, 16)
        e.dma_start(out=wdb, in_=Wd).then_inc(s, 16)
        e.dma_start(out=wub, in_=Wu).then_inc(s, 16)
    p.dma("gpsimd", wcasts, n_dma=8, tag="wcasts")
    emit_in_transpose(p, nc, x_in, xfm, cst)
    emit_ln(p, nc, xfm, xn_hbm, cst, "ln0a")
    emit_l0_attn(p, nc, xn_hbm, wqkv_b, attnT, cst)
    emit_fm_proj_residual(p, nc, attnT, wout_b, xfm, x1fm, cst, "o")
    emit_ln(p, nc, x1fm, xn_hbm, cst, "ln0f")
    emit_l0_ffn(p, nc, x1fm, xn_hbm, wg8, wu8, wd8, sg, su, sd, fuse_hbm, x2fm, cst)
    emit_ln(p, nc, x2fm, xn_hbm, cst, "ln1a")
    emit_l1_attn(p, nc, x2fm, xn_hbm, wqb, wkb, wvb, bq, bk, bv, x3fm, cst)
    nw_col = p.sbuf("m_nw", [128, NT], F32)
    p.push_scope()
    pst_sc = p.psum("m_scps", [128, 128])
    pst_sc_b = p.buf()
    nw_b = emit_scale_cols(p, nc, nw, NT, "mnw", cst, pst_sc, pst_sc_b, nw_col)
    p.pop_scope()
    p.barrier("m_nw_end")
    emit_ln(p, nc, x3fm, xn_hbm, cst, "ln1f", nw_col=nw_col, nw_col_buf=nw_b)
    emit_l1_ffn(p, nc, x3fm, xn_hbm, wcb, genes, wdb, wub, temp, scale, w_hbm, out_tm, cst)
    p.emit()
    return nc


def kernel_fused(**inputs):
    x = inputs["L0_attn_x"]
    ncores = 8
    cc = const_inputs(ALL_CONSTS)
    ncf = _get("fused", build_fused)
    maps = []
    for c in range(ncores):
        m = {"x": np.ascontiguousarray(x[c]),
             "w_qkv": inputs["L0_attn_w_qkv"], "w_out": inputs["L0_attn_w_out"],
             "w_gate": inputs["L0_ffn_w_gate"], "w_up": inputs["L0_ffn_w_up"],
             "w_down": inputs["L0_ffn_w_down"], "s_gate": inputs["L0_ffn_s_gate"],
             "s_up": inputs["L0_ffn_s_up"], "s_down": inputs["L0_ffn_s_down"],
             "Wq": inputs["L1_attn_Wq"], "bq": inputs["L1_attn_bq"],
             "Wk": inputs["L1_attn_Wk"], "bk": inputs["L1_attn_bk"],
             "Wv": inputs["L1_attn_Wv"], "bv": inputs["L1_attn_bv"],
             "Wc": inputs["L1_ffn_Wc"],
             "temp": np.asarray(inputs["L1_ffn_temperature"], dtype=np.float32).reshape(1),
             "genes": inputs["L1_ffn_genes"], "Wd": inputs["L1_ffn_Wd"],
             "Wu": inputs["L1_ffn_Wu"], "norm_w": inputs["L1_ffn_norm_w"],
             "scale": np.asarray(inputs["L1_ffn_scale"], dtype=np.float32).reshape(1)}
        m.update(cc)
        maps.append(m)
    r = run_bass_kernel_spmd(ncf, maps, core_ids=list(range(ncores)))
    return np.stack([r.results[c]["out"] for c in range(ncores)], axis=0)


_BUILD_CACHE = {}


def _get(name, builder):
    if name not in _BUILD_CACHE:
        _BUILD_CACHE[name] = builder()
    return _BUILD_CACHE[name]


def kernel(**inputs):
    return kernel_fused(**inputs)


def kernel_multilaunch(**inputs):
    x = inputs["L0_attn_x"]
    ncores = 8
    c1 = const_inputs(L1_CONSTS)
    c2 = const_inputs(L2_CONSTS)
    c3 = const_inputs(L3_CONSTS)
    c4 = const_inputs(L4_CONSTS)

    nc1 = _get("l1", build_launch1)
    maps = []
    for c in range(ncores):
        m = {"x": np.ascontiguousarray(x[c]), "w_qkv": inputs["L0_attn_w_qkv"],
             "w_out": inputs["L0_attn_w_out"]}
        m.update(c1)
        maps.append(m)
    r1 = run_bass_kernel_spmd(nc1, maps, core_ids=list(range(ncores)))

    nc2 = _get("l2", build_launch2)
    maps = []
    for c in range(ncores):
        m = {"x1fm": np.ascontiguousarray(r1.results[c]["x1fm"]),
             "w_gate": inputs["L0_ffn_w_gate"], "w_up": inputs["L0_ffn_w_up"],
             "w_down": inputs["L0_ffn_w_down"], "s_gate": inputs["L0_ffn_s_gate"],
             "s_up": inputs["L0_ffn_s_up"], "s_down": inputs["L0_ffn_s_down"]}
        m.update(c2)
        maps.append(m)
    r2 = run_bass_kernel_spmd(nc2, maps, core_ids=list(range(ncores)))

    nc3 = _get("l3", build_launch3)
    maps = []
    for c in range(ncores):
        m = {"x2fm": np.ascontiguousarray(r2.results[c]["x2fm"]),
             "Wq": inputs["L1_attn_Wq"], "bq": inputs["L1_attn_bq"],
             "Wk": inputs["L1_attn_Wk"], "bk": inputs["L1_attn_bk"],
             "Wv": inputs["L1_attn_Wv"], "bv": inputs["L1_attn_bv"]}
        m.update(c3)
        maps.append(m)
    r3 = run_bass_kernel_spmd(nc3, maps, core_ids=list(range(ncores)))

    nc4 = _get("l4", build_launch4)
    maps = []
    for c in range(ncores):
        m = {"x3fm": np.ascontiguousarray(r3.results[c]["x3fm"]),
             "Wc": inputs["L1_ffn_Wc"],
             "temp": np.asarray(inputs["L1_ffn_temperature"], dtype=np.float32).reshape(1),
             "genes": inputs["L1_ffn_genes"], "Wd": inputs["L1_ffn_Wd"],
             "Wu": inputs["L1_ffn_Wu"], "norm_w": inputs["L1_ffn_norm_w"],
             "scale": np.asarray(inputs["L1_ffn_scale"], dtype=np.float32).reshape(1)}
        m.update(c4)
        maps.append(m)
    r4 = run_bass_kernel_spmd(nc4, maps, core_ids=list(range(ncores)))

    return np.stack([r4.results[c]["out"] for c in range(ncores)], axis=0)


# version 05
# speedup vs baseline: 1.0502x; 1.0502x over previous
import numpy as np
from contextlib import ExitStack
import concourse.bass as bass
import concourse.mybir as mybir
from concourse.bass_utils import run_bass_kernel_spmd
import ml_dtypes

F32 = mybir.dt.float32
BF16 = mybir.dt.bfloat16
I8 = mybir.dt.int8
AF = mybir.ActivationFunctionType
ALU = mybir.AluOpType
AX = mybir.AxisListType
BF = ml_dtypes.bfloat16

D = 2048
S = 4096
NT = D // 128
ASCALE = float(1.0 / np.sqrt(128.0))

ENGINES = ("tensor", "vector", "scalar", "gpsimd", "sync")
DMA_QUEUES = ("sync", "gpsimd", "scalar", "vector")


class Step:
    __slots__ = ("engine", "fn", "deps", "sem", "val", "is_dma", "n_dma", "tag")

    def __init__(self, engine, fn, deps, is_dma=False, n_dma=1, tag=""):
        self.engine = engine
        self.fn = fn
        self.deps = deps
        self.is_dma = is_dma
        self.n_dma = n_dma
        self.sem = None
        self.val = None
        self.tag = tag


class Buf:
    __slots__ = ("name", "last_w", "readers")

    def __init__(self, name):
        self.name = name
        self.last_w = None
        self.readers = []


class Prog:
    def __init__(self, nc: bass.Bass):
        self.nc = nc
        self.steps = []
        self.es = ExitStack()
        self._scopes = []
        self._sems = {}
        self._counts = {}
        self._barrier = None
        self._nbuf = 0

    def push_scope(self):
        self._scopes.append(ExitStack())

    def pop_scope(self):
        self._scopes.pop().close()

    def _stack(self):
        return self._scopes[-1] if self._scopes else self.es

    def sbuf(self, name, shape, dtype):
        return self._stack().enter_context(self.nc.sbuf_tensor(name, shape, dtype))

    def psum(self, name, shape, dtype=F32):
        return self._stack().enter_context(self.nc.psum_tensor(name, shape, dtype))

    def buf(self, name=None):
        self._nbuf += 1
        return Buf(name or f"b{self._nbuf}")

    SERIAL = False

    def _mkdeps(self, reads, writes, deps):
        dd = set(d for d in deps if d is not None)
        if Prog.SERIAL and self.steps:
            dd.add(self.steps[-1])
        if self._barrier is not None:
            dd.add(self._barrier)
        for b in reads:
            if b.last_w is not None:
                dd.add(b.last_w)
        for b in writes:
            if b.last_w is not None:
                dd.add(b.last_w)
            dd.update(b.readers)
        return dd

    def _post(self, s, reads, writes):
        for b in reads:
            b.readers.append(s)
        for b in writes:
            b.last_w = s
            b.readers = []
        return s

    def step(self, engine, fn, reads=(), writes=(), deps=(), tag=""):
        s = Step(engine, fn, tuple(self._mkdeps(reads, writes, deps)), tag=tag)
        self.steps.append(s)
        return self._post(s, reads, writes)

    def dma(self, queue, fn, reads=(), writes=(), deps=(), n_dma=1, tag=""):
        s = Step(queue, fn, tuple(self._mkdeps(reads, writes, deps)), is_dma=True, n_dma=n_dma, tag=tag)
        self.steps.append(s)
        return self._post(s, reads, writes)

    def barrier(self, tag="bar"):
        lasts = {}
        for s in self.steps:
            lasts[(s.is_dma, s.engine)] = s
        j = Step("sync", lambda e: e.nop(), tuple(set(lasts.values())), tag=tag)
        self.steps.append(j)
        self._barrier = j
        return j

    def emit(self):
        nc = self.nc
        for e in ENGINES:
            self._sems[("c", e)] = self.es.enter_context(nc.semaphore(f"sem_{e}"))
            self._counts[("c", e)] = 0
        for q in DMA_QUEUES:
            self._sems[("d", q)] = self.es.enter_context(nc.semaphore(f"dsem_{q}"))
            self._counts[("d", q)] = 0
        for s in self.steps:
            key = ("d", s.engine) if s.is_dma else ("c", s.engine)
            s.sem = self._sems[key]
            inc = 16 * s.n_dma if s.is_dma else 1
            self._counts[key] += inc
            s.val = self._counts[key]
        block = self.es.enter_context(nc.Block())
        by_engine = {e: [] for e in ENGINES}
        for s in self.steps:
            by_engine[s.engine].append(s)

        def make_body(steps):
            def body(eng):
                done = {}
                for s in steps:
                    waits = {}
                    if s.is_dma:
                        prev = s.val - 16 * s.n_dma
                        k = id(s.sem)
                        if prev > 0 and done.get(k, -1) < prev:
                            waits[k] = (s.sem, prev)
                    for d in s.deps:
                        k = id(d.sem)
                        if done.get(k, -1) >= d.val:
                            continue
                        if k not in waits or waits[k][1] < d.val:
                            waits[k] = (d.sem, d.val)
                    for (sem, val) in waits.values():
                        eng.wait_ge(sem, val)
                        done[id(sem)] = val
                    if s.is_dma:
                        s.fn(eng, s.sem)
                    else:
                        inst = s.fn(eng)
                        assert inst is not None, f"step {s.tag} returned no instruction"
                        inst.then_inc(s.sem, 1)
            return body

        for e in ENGINES:
            if by_engine[e]:
                getattr(block, e)(make_body(by_engine[e]))



def make_consts():
    c = {}
    c["IDF"] = np.eye(128, dtype=np.float32)
    c["ONESCOL"] = np.ones((128, 1), dtype=np.float32).astype(BF)
    c["ONESROW"] = np.ones((1, 128), dtype=np.float32)
    tri = np.arange(128)
    c["TRI_LO"] = (tri[None, :] <= tri[:, None]).astype(np.float32).astype(BF)
    c["TRI_HI"] = (tri[None, :] >= tri[:, None]).astype(np.float32).astype(BF)
    i32 = np.arange(128)
    blk = i32 // 32
    same = blk[:, None] == blk[None, :]
    c["GLA_L"] = (same & (i32[:, None] <= i32[None, :])).astype(np.float32)
    c["GLA_R"] = (same & (i32[:, None] > i32[None, :])).astype(np.float32)
    t32 = np.arange(32)
    c["GLA_PMASK"] = (t32[:, None] <= np.tile(t32, 4)[None, :]).astype(np.float32)
    return c


CONSTS = make_consts()


def declare_consts(nc, p, names):
    out = {}
    loads = []
    for n in names:
        arr = CONSTS[n]
        dt = {np.dtype(np.float32): F32, np.dtype(BF): BF16}[arr.dtype]
        ap = nc.dram_tensor("c_" + n, list(arr.shape), dt, kind="ExternalInput").ap()
        sb = p.sbuf("cs_" + n, list(arr.shape), dt)
        out[n] = sb
        loads.append((sb, ap))

    def fn(e, s):
        for sb, ap in loads:
            e.dma_start(out=sb[:], in_=ap).then_inc(s, 16)
    out["_step"] = p.dma("sync", fn, n_dma=len(loads), tag="consts")
    return out


def const_inputs(names):
    return {"c_" + n: np.ascontiguousarray(CONSTS[n]) for n in names}


def fm_view(hbm_ap):
    return hbm_ap.rearrange("(j p) t -> p j t", p=128)



def emit_in_transpose(p, nc, x_in, xfm, cst):
    p.push_scope()
    xt = [p.sbuf(f"it_x{i}", [128, D], F32) for i in range(2)]
    ot = [p.sbuf(f"it_o{i}", [128, D], F32) for i in range(2)]
    pst = [p.psum(f"it_ps{i}", [128, 512]) for i in range(2)]
    xt_b = [p.buf() for _ in range(2)]
    ot_b = [p.buf() for _ in range(2)]
    ps_b = [p.buf() for _ in range(2)]
    for i in range(S // 128):
        bi = i % 2
        p.dma("sync", (lambda xsb, src: lambda e, s: e.dma_start(out=xsb[:], in_=src).then_inc(s, 16))(
            xt[bi], x_in[i * 128:(i + 1) * 128, :]), writes=[xt_b[bi]], deps=[cst["_step"]], tag=f"it_ld{i}")
        for q in range(4):
            bq = q % 2
            def trf(e, xsb=xt[bi], ps=pst[bq], q=q, idf=cst["IDF"]):
                for j in range(4):
                    et = 4 * q + j
                    inst = e.transpose(ps[:, 128 * j:128 * (j + 1)], xsb[:, 128 * et:128 * et + 128], idf[:])
                return inst
            p.step("tensor", trf, reads=[xt_b[bi]], writes=[ps_b[bq]], tag=f"it_tr{i}_{q}")
            if q % 2 == 0:
                evf = (lambda ps, osb, q: lambda e: e.copy(osb[:, 512 * q:512 * (q + 1)], ps[:]))(pst[bq], ot[bi], q)
                p.step("scalar", evf, reads=[ps_b[bq]], writes=[ot_b[bi]], tag=f"it_ev{i}_{q}")
            else:
                evf = (lambda ps, osb, q: lambda e: e.tensor_copy(osb[:, 512 * q:512 * (q + 1)], ps[:]))(pst[bq], ot[bi], q)
                p.step("vector", evf, reads=[ps_b[bq]], writes=[ot_b[bi]], tag=f"it_ev{i}_{q}")
        p.dma("sync", (lambda osb, i: lambda e, s: e.dma_start(
            out=fm_view(xfm)[:, :, 128 * i:128 * (i + 1)],
            in_=osb[:].rearrange("p (j t) -> p j t", t=128)).then_inc(s, 16))(ot[bi], i),
            reads=[ot_b[bi]], tag=f"it_st{i}")
    p.pop_scope()
    return p.barrier("it_end")



def emit_ln(p, nc, xfm, xn_dst, cst, tag, nw_col=None, nw_col_buf=None):
    CH = 512
    nch = S // CH
    p.push_scope()
    xc = [p.sbuf(f"{tag}_x{i}", [128, NT * CH], F32) for i in range(2)]
    xbB = [p.sbuf(f"{tag}_xb{i}", [128, NT * CH], BF16) for i in range(2)]
    xqB = [p.sbuf(f"{tag}_xq{i}", [128, NT * CH], BF16) for i in range(2)]
    xnc = [p.sbuf(f"{tag}_xn{i}", [128, NT * CH], BF16) for i in range(2)]
    tmp = [p.sbuf(f"{tag}_tmp{i}", [128, CH], F32) for i in range(4)]
    rowB = [p.sbuf(f"{tag}_rows{i}", [1, 6 * CH], F32) for i in range(2)]
    b1sb = [p.sbuf(f"{tag}_b1{i}", [128, CH], F32) for i in range(2)]
    b2sb = [p.sbuf(f"{tag}_b2{i}", [128, CH], F32) for i in range(2)]
    ps_stB = [p.psum(f"{tag}_ps{i}", [1, 2 * CH]) for i in range(2)]
    ps_bc = [p.psum(f"{tag}_pbc{i}", [128, CH]) for i in range(2)]
    B = dict(xc=[p.buf(), p.buf()], xb=[p.buf(), p.buf()], xq=[p.buf(), p.buf()],
             xn=[p.buf(), p.buf()], tmp=[p.buf() for _ in range(4)],
             rows=[p.buf(), p.buf()], ps=[p.buf(), p.buf()],
             pbc=[p.buf(), p.buf()], b1=[p.buf(), p.buf()], b2=[p.buf(), p.buf()])
    for c in range(nch):
        bi = c % 2
        xb = xbB[bi]; xq = xqB[bi]; rowbuf = rowB[bi]; ps_st = ps_stB[bi]
        r_m = rowbuf[:, 0:CH]; r_sq = rowbuf[:, CH:2 * CH]; r_var = rowbuf[:, 2 * CH:3 * CH]
        r_rs = rowbuf[:, 3 * CH:4 * CH]; r_mrs = rowbuf[:, 4 * CH:5 * CH]; r_t = rowbuf[:, 5 * CH:6 * CH]
        Brows = B["rows"][bi]; Bps = B["ps"][bi]; Bxb = B["xb"][bi]; Bxq = B["xq"][bi]
        p.dma("sync", (lambda dst, c: lambda e, s: e.dma_start(
            out=dst[:].rearrange("p (j t) -> p j t", t=CH),
            in_=fm_view(xfm)[:, :, CH * c:CH * (c + 1)]).then_inc(s, 16))(xc[bi], c),
            writes=[B["xc"][bi]], tag=f"{tag}_ld{c}")
        p.step("vector", (lambda src, xb=xb: lambda e: e.tensor_copy(xb[:], src[:]))(xc[bi]),
               reads=[B["xc"][bi]], writes=[Bxb], tag=f"{tag}_cast{c}")
        p.step("scalar", (lambda xb=xb, xq=xq: lambda e: e.activation(xq[:], xb[:], AF.Square))(),
               reads=[Bxb], writes=[Bxq], tag=f"{tag}_sq{c}")

        def stats(e, xb=xb, xq=xq, ps_st=ps_st):
            for j in range(NT):
                e.matmul(ps_st[:, 0:CH], cst["ONESCOL"][:], xb[:, CH * j:CH * (j + 1)],
                         start=(j == 0), stop=(j == NT - 1))
            for j in range(NT):
                inst = e.matmul(ps_st[:, CH:2 * CH], cst["ONESCOL"][:], xq[:, CH * j:CH * (j + 1)],
                                start=(j == 0), stop=(j == NT - 1))
            return inst
        p.step("tensor", stats, reads=[Bxb, Bxq], writes=[Bps], tag=f"{tag}_stat{c}")
        p.step("vector", (lambda r_m=r_m, ps_st=ps_st: lambda e: e.tensor_scalar_mul(r_m, ps_st[:, 0:CH], 1.0 / D))(),
               reads=[Bps], writes=[Brows])
        p.step("vector", (lambda r_sq=r_sq, ps_st=ps_st: lambda e: e.tensor_scalar_mul(r_sq, ps_st[:, CH:2 * CH], 1.0 / D))(),
               reads=[Bps], writes=[Brows])
        p.step("vector", (lambda r_t=r_t, r_m=r_m: lambda e: e.tensor_mul(r_t, r_m, r_m))(),
               reads=[Brows], writes=[Brows])
        p.step("vector", (lambda r_var=r_var, r_sq=r_sq, r_t=r_t: lambda e: e.tensor_sub(r_var, r_sq, r_t))(),
               reads=[Brows], writes=[Brows])
        p.step("vector", (lambda r_t=r_t, r_var=r_var: lambda e: e.tensor_scalar_add(r_t, r_var, 1e-5))(),
               reads=[Brows], writes=[Brows])
        p.step("scalar", (lambda r_t=r_t: lambda e: e.activation(r_t, r_t, AF.Ln))(),
               reads=[Brows], writes=[Brows])
        if nw_col is None:
            p.step("scalar", (lambda r_rs=r_rs, r_t=r_t: lambda e: e.activation(r_rs, r_t, AF.Exp, scale=-0.5))(),
                   reads=[Brows], writes=[Brows])
        else:
            p.step("scalar", (lambda r_mrs=r_mrs, r_t=r_t: lambda e: e.activation(r_mrs, r_t, AF.Exp, scale=-1.0))(),
                   reads=[Brows], writes=[Brows])
            p.step("vector", (lambda r_mrs=r_mrs, r_var=r_var: lambda e: e.tensor_mul(r_mrs, r_var, r_mrs))(),
                   reads=[Brows], writes=[Brows])
            p.step("vector", (lambda r_mrs=r_mrs: lambda e: e.tensor_scalar_add(r_mrs, r_mrs, 1.1920929e-07))(),
                   reads=[Brows], writes=[Brows])
            p.step("scalar", (lambda r_mrs=r_mrs: lambda e: e.activation(r_mrs, r_mrs, AF.Ln))(),
                   reads=[Brows], writes=[Brows])
            p.step("vector", (lambda r_t=r_t, r_mrs=r_mrs: lambda e: e.tensor_add(r_t, r_t, r_mrs))(),
                   reads=[Brows], writes=[Brows])
            p.step("scalar", (lambda r_rs=r_rs, r_t=r_t: lambda e: e.activation(r_rs, r_t, AF.Exp, scale=-0.5))(),
                   reads=[Brows], writes=[Brows])
        p.step("vector", (lambda r_mrs=r_mrs, r_m=r_m, r_rs=r_rs: lambda e: e.tensor_mul(r_mrs, r_m, r_rs))(),
               reads=[Brows], writes=[Brows])
        bcA = p.step("tensor", (lambda bi, r_rs=r_rs: lambda e: e.matmul(
            ps_bc[bi][:], cst["ONESROW"][0:1, :], r_rs, start=True, stop=True))(bi),
            reads=[Brows], writes=[B["pbc"][bi]], tag=f"{tag}_bcA{c}")
        cpA = p.step("vector", (lambda bi: lambda e: e.tensor_copy(b1sb[bi][:], ps_bc[bi][:]))(bi),
                     reads=[B["pbc"][bi]], writes=[B["b1"][bi]])
        bcB = p.step("tensor", (lambda bi, r_mrs=r_mrs: lambda e: e.matmul(
            ps_bc[bi][:], cst["ONESROW"][0:1, :], r_mrs, start=True, stop=True))(bi),
            reads=[Brows], writes=[B["pbc"][bi]], deps=[cpA], tag=f"{tag}_bcB{c}")
        cpB = p.step("vector", (lambda bi: lambda e: e.tensor_copy(b2sb[bi][:], ps_bc[bi][:]))(bi),
                     reads=[B["pbc"][bi]], writes=[B["b2"][bi]])
        for j in range(NT):
            tb = j % 4
            p.step("gpsimd", (lambda bi, j, tb: lambda e: e.tensor_mul(
                tmp[tb][:], xc[bi][:, CH * j:CH * (j + 1)], b1sb[bi][:]))(bi, j, tb),
                reads=[B["xc"][bi], B["b1"][bi]], writes=[B["tmp"][tb]], tag=f"{tag}_t1_{c}_{j}")
            if nw_col is None:
                p.step("vector", (lambda bi, j, tb: lambda e: e.tensor_sub(
                    xnc[bi][:, CH * j:CH * (j + 1)], tmp[tb][:], b2sb[bi][:]))(bi, j, tb),
                    reads=[B["tmp"][tb], B["b2"][bi]], writes=[B["xn"][bi]], tag=f"{tag}_t2_{c}_{j}")
            else:
                p.step("vector", (lambda bi, j, tb: lambda e: e.tensor_sub(
                    tmp[tb][:], tmp[tb][:], b2sb[bi][:]))(bi, j, tb),
                    reads=[B["tmp"][tb], B["b2"][bi]], writes=[B["tmp"][tb]])
                p.step("vector", (lambda bi, j, tb: lambda e: e.tensor_scalar_mul(
                    xnc[bi][:, CH * j:CH * (j + 1)], tmp[tb][:], nw_col[:, j:j + 1]))(bi, j, tb),
                    reads=[B["tmp"][tb], nw_col_buf], writes=[B["xn"][bi]], tag=f"{tag}_t3_{c}_{j}")
        p.dma("sync", (lambda src, c: lambda e, s: e.dma_start(
            out=fm_view(xn_dst)[:, :, CH * c:CH * (c + 1)],
            in_=src[:].rearrange("p (j t) -> p j t", t=CH)).then_inc(s, 16))(xnc[bi], c),
            reads=[B["xn"][bi]], tag=f"{tag}_st{c}")
    p.pop_scope()
    return p.barrier(f"{tag}_end")



def band_sched(chunk0):
    out = []
    if chunk0:
        for k in range(4):
            out.append((k, 128 * k, 512, k, None))
    else:
        for k in range(8):
            qlo = max(0, 128 * (k - 4))
            qhi = min(512, 128 * (k + 1))
            lo = (k - 4) if k >= 4 else None
            hi = k if k < 4 else None
            out.append((k, qlo, qhi, lo, hi))
    return out


def emit_l0_attn(p, nc, xn_hbm, wqkv_b, attnT_hbm, cst):
    CH = 512
    nch = S // CH
    p.push_scope()
    wq = [p.sbuf(f"a_wq{i}", [128, NT * CH], BF16) for i in range(2)]
    wk = [p.sbuf(f"a_wk{i}", [128, NT * CH], BF16) for i in range(2)]
    wv = [p.sbuf(f"a_wv{i}", [128, NT * CH], BF16) for i in range(2)]
    wq_b = [p.buf(), p.buf()]
    wk_b = [p.buf(), p.buf()]
    wv_b = [p.buf(), p.buf()]
    xnc = [p.sbuf(f"a_xn{i}", [128, NT * CH], BF16) for i in range(2)]
    xnc_b = [p.buf(), p.buf()]
    qT = p.sbuf("a_qT", [128, 4 * CH], BF16)
    kT = [p.sbuf(f"a_kT{i}", [128, 4 * CH], BF16) for i in range(2)]
    vv = [p.sbuf(f"a_v{i}", [128, 4 * CH], BF16) for i in range(2)]
    qT_b = p.buf()
    kT_b = [p.buf(), p.buf()]
    vv_b = [p.buf(), p.buf()]
    pT2 = [p.sbuf(f"a_pT{i}", [128, 8 * CH], BF16) for i in range(2)]
    pT2_b = [p.buf(), p.buf()]
    rz2 = [p.sbuf(f"a_rz{i}", [1, CH], F32) for i in range(2)]
    rzb2 = [p.sbuf(f"a_rzb{i}", [128, CH], F32) for i in range(2)]
    rz2_b = [p.buf(), p.buf()]
    rzb2_b = [p.buf(), p.buf()]
    att2 = [p.sbuf(f"a_att{i}", [128, CH], BF16) for i in range(2)]
    att2_b = [p.buf(), p.buf()]
    ps_q = p.psum("a_psq", [128, 4 * CH])
    ps_s = p.psum("a_pss", [128, 2 * CH])
    ps_q_b = [p.buf() for _ in range(4)]
    ps_s_b = [p.buf(), p.buf()]
    ps_pv = p.psum("a_pspv", [128, CH])
    ps_pv_b = p.buf()
    ps_z = p.psum("a_psz", [1, CH])
    ps_z_b = p.buf()
    ps_bc = ps_pv
    ps_bc_b = ps_pv_b

    for g in range(4):
        gw = g % 2
        def wload(e, s, g=g, gw=gw):
            for et in range(NT):
                e.dma_start(out=wq[gw][:, CH * et:CH * (et + 1)],
                            in_=wqkv_b[512 * g:512 * (g + 1), 128 * et:128 * (et + 1)], transpose=True).then_inc(s, 16)
                e.dma_start(out=wk[gw][:, CH * et:CH * (et + 1)],
                            in_=wqkv_b[2048 + 512 * g:2048 + 512 * (g + 1), 128 * et:128 * (et + 1)], transpose=True).then_inc(s, 16)
                e.dma_start(out=wv[gw][:, CH * et:CH * (et + 1)],
                            in_=wqkv_b[4096 + 512 * g:4096 + 512 * (g + 1), 128 * et:128 * (et + 1)], transpose=True).then_inc(s, 16)
        p.dma("sync", wload, writes=[wq_b[gw], wk_b[gw], wv_b[gw]], n_dma=3 * NT, tag=f"a_wl{g}")
        for c in range(nch):
            bi = c % 2
            cur, prv = c % 2, 1 - c % 2
            p.dma("gpsimd", (lambda dst, c: lambda e, s: e.dma_start(
                out=dst[:].rearrange("p (j t) -> p j t", t=CH),
                in_=fm_view(xn_hbm)[:, :, CH * c:CH * (c + 1)]).then_inc(s, 16))(xnc[bi], c),
                writes=[xnc_b[bi]], tag=f"a_ldx{g}_{c}")

            for ft in range(4):
                def qmmf(e, ft=ft, bi=bi, gw=gw):
                    for et in range(NT):
                        inst = e.matmul(ps_q[:, CH * ft:CH * (ft + 1)],
                                        wq[gw][:, CH * et + 128 * ft:CH * et + 128 * (ft + 1)],
                                        xnc[bi][:, CH * et:CH * (et + 1)],
                                        start=(et == 0), stop=(et == NT - 1))
                    return inst
                p.step("tensor", qmmf, reads=[wq_b[gw], xnc_b[bi]], writes=[ps_q_b[ft]], tag=f"a_qmm{g}_{c}_{ft}")
                p.step("scalar", (lambda ft: lambda e: e.mul(qT[:, CH * ft:CH * (ft + 1)], ps_q[:, CH * ft:CH * (ft + 1)], ASCALE))(ft),
                       reads=[ps_q_b[ft]], writes=[qT_b], tag=f"a_qev{g}_{c}_{ft}")
            for ft in range(4):
                def kmmf(e, ft=ft, bi=bi, gw=gw):
                    for et in range(NT):
                        inst = e.matmul(ps_q[:, CH * ft:CH * (ft + 1)],
                                        wk[gw][:, CH * et + 128 * ft:CH * et + 128 * (ft + 1)],
                                        xnc[bi][:, CH * et:CH * (et + 1)],
                                        start=(et == 0), stop=(et == NT - 1))
                    return inst
                p.step("tensor", kmmf, reads=[wk_b[gw], xnc_b[bi]], writes=[ps_q_b[ft]], tag=f"a_kmm{g}_{c}_{ft}")
                p.step("scalar", (lambda ft, cur=cur: lambda e: e.copy(kT[cur][:, CH * ft:CH * (ft + 1)], ps_q[:, CH * ft:CH * (ft + 1)]))(ft),
                       reads=[ps_q_b[ft]], writes=[kT_b[cur]], tag=f"a_kev{g}_{c}_{ft}")
            for tt in range(4):
                def vmmf(e, tt=tt, bi=bi, gw=gw):
                    for et in range(NT):
                        inst = e.matmul(ps_q[:, CH * tt:CH * (tt + 1)],
                                        xnc[bi][:, CH * et + 128 * tt:CH * et + 128 * (tt + 1)],
                                        wv[gw][:, CH * et:CH * (et + 1)],
                                        start=(et == 0), stop=(et == NT - 1))
                    return inst
                p.step("tensor", vmmf, reads=[wv_b[gw], xnc_b[bi]], writes=[ps_q_b[tt]], tag=f"a_vmm{g}_{c}_{tt}")
                p.step("scalar", (lambda tt, cur=cur: lambda e: e.copy(vv[cur][:, CH * tt:CH * (tt + 1)], ps_q[:, CH * tt:CH * (tt + 1)]))(tt),
                       reads=[ps_q_b[tt]], writes=[vv_b[cur]], tag=f"a_vev{g}_{c}_{tt}")
            sched = band_sched(c == 0)
            kread = [kT_b[cur]] if c == 0 else [kT_b[cur], kT_b[prv]]
            vread = [vv_b[cur]] if c == 0 else [vv_b[cur], vv_b[prv]]
            for h in range(4):
                hp = h % 2
                pT = pT2[hp]; pT_b = pT2_b[hp]
                rz = rz2[hp]; rz_b = rz2_b[hp]
                rzb = rzb2[hp]; rzb_b = rzb2_b[hp]
                att = att2[hp]; att_b = att2_b[hp]
                qs = qT[:, CH * h:CH * (h + 1)]
                for it, (k, qlo, qhi, lo, hi) in enumerate(sched):
                    pb = it % 2
                    if c == 0:
                        ksrc, ktile = kT[cur], k
                    else:
                        ksrc, ktile = (kT[prv], k) if k < 4 else (kT[cur], k - 4)
                    stat = ksrc[:, CH * h + 128 * ktile:CH * h + 128 * (ktile + 1)]
                    p.step("tensor", (lambda stat, qs, pb, qlo, qhi: lambda e: e.matmul(
                        ps_s[:, CH * pb + qlo:CH * pb + qhi], stat, qs[:, qlo:qhi], start=True, stop=True))(stat, qs, pb, qlo, qhi),
                        reads=kread + [qT_b], writes=[ps_s_b[pb]], tag=f"a_smm{g}_{c}_{h}_{it}")
                    p.step("scalar", (lambda pb, it, qlo, qhi: lambda e: e.activation(
                        pT[:, CH * it + qlo:CH * it + qhi], ps_s[:, CH * pb + qlo:CH * pb + qhi], AF.Exp))(pb, it, qlo, qhi),
                        reads=[ps_s_b[pb]], writes=[pT_b], tag=f"a_sev{g}_{c}_{h}_{it}")
                    if lo is not None:
                        p.step("vector", (lambda it, lo: lambda e: e.tensor_mul(
                            pT[:, CH * it + 128 * lo:CH * it + 128 * (lo + 1)],
                            pT[:, CH * it + 128 * lo:CH * it + 128 * (lo + 1)], cst["TRI_HI"][:]))(it, lo),
                            reads=[pT_b], writes=[pT_b], tag=f"a_mlo{g}_{c}_{h}_{it}")
                    if hi is not None:
                        p.step("vector", (lambda it, hi: lambda e: e.tensor_mul(
                            pT[:, CH * it + 128 * hi:CH * it + 128 * (hi + 1)],
                            pT[:, CH * it + 128 * hi:CH * it + 128 * (hi + 1)], cst["TRI_LO"][:]))(it, hi),
                            reads=[pT_b], writes=[pT_b], tag=f"a_mhi{g}_{c}_{h}_{it}")

                def zmm(e, sched=sched):
                    for it, (k, qlo, qhi, lo, hi) in enumerate(sched):
                        inst = e.matmul(ps_z[0:1, qlo:qhi], cst["ONESCOL"][:],
                                        pT[:, CH * it + qlo:CH * it + qhi],
                                        start=(it == 0), stop=(it == len(sched) - 1))
                    return inst
                p.step("tensor", zmm, reads=[pT_b], writes=[ps_z_b], tag=f"a_zmm{g}_{c}_{h}")
                p.step("vector", lambda e: e.reciprocal(rz[:], ps_z[:]), reads=[ps_z_b], writes=[rz_b])
                p.step("tensor", lambda e: e.matmul(ps_bc[:], cst["ONESROW"][0:1, :], rz[:], start=True, stop=True),
                       reads=[rz_b], writes=[ps_bc_b])
                p.step("vector", lambda e: e.tensor_copy(rzb[:], ps_bc[:]), reads=[ps_bc_b], writes=[rzb_b])

                def pvmm(e, sched=sched, h=h, cur=cur, prv=prv, c=c):
                    for it, (k, qlo, qhi, lo, hi) in enumerate(sched):
                        if c == 0:
                            vsrc, vtile = vv[cur], k
                        else:
                            vsrc, vtile = (vv[prv], k) if k < 4 else (vv[cur], k - 4)
                        stat = vsrc[:, CH * vtile + 128 * h:CH * vtile + 128 * (h + 1)]
                        inst = e.matmul(ps_pv[:, qlo:qhi], stat,
                                        pT[:, CH * it + qlo:CH * it + qhi],
                                        start=(it == 0), stop=(it == len(sched) - 1))
                    return inst
                p.step("tensor", pvmm, reads=[pT_b] + vread, writes=[ps_pv_b], tag=f"a_pv{g}_{c}_{h}")
                p.step("vector", lambda e: e.tensor_mul(att[:], ps_pv[:], rzb[:]),
                       reads=[ps_pv_b, rzb_b], writes=[att_b], tag=f"a_nrm{g}_{c}_{h}")
                hg = 4 * g + h
                p.dma("sync", (lambda hg, c: lambda e, s: e.dma_start(
                    out=attnT_hbm[128 * hg:128 * (hg + 1), CH * c:CH * (c + 1)], in_=att[:]).then_inc(s, 16))(hg, c),
                    reads=[att_b], tag=f"a_stA{g}_{c}_{h}")
    p.pop_scope()
    return p.barrier("a_end")



def emit_fm_proj_residual(p, nc, src_hbm, w_b, resid_hbm, out_hbm, cst, tag):
    CH = 512
    nch = S // CH
    p.push_scope()
    wo = p.sbuf(f"{tag}_wo", [128, NT * D], BF16)
    wo_b = p.buf()
    atc = [p.sbuf(f"{tag}_at{i}", [128, NT * CH], BF16) for i in range(2)]
    atc_b = [p.buf(), p.buf()]
    xc = [p.sbuf(f"{tag}_x{i}", [128, NT * CH], F32) for i in range(2)]
    xc_b = [p.buf(), p.buf()]
    oc = [p.sbuf(f"{tag}_o{i}", [128, CH], F32) for i in range(4)]
    oc_b = [p.buf() for _ in range(4)]
    ps = p.psum(f"{tag}_ps", [128, 2 * CH])
    ps_b = [p.buf(), p.buf()]

    def wload(e, s):
        for et in range(NT):
            e.dma_start(out=wo[:, D * et:D * (et + 1)],
                        in_=w_b[:, 128 * et:128 * (et + 1)], transpose=True).then_inc(s, 16)
    p.dma("sync", wload, writes=[wo_b], n_dma=NT, tag=f"{tag}_wl")
    for c in range(nch):
        bi = c % 2
        p.dma("gpsimd", (lambda dst, c: lambda e, s: e.dma_start(
            out=dst[:].rearrange("p (j t) -> p j t", t=CH),
            in_=fm_view(src_hbm)[:, :, CH * c:CH * (c + 1)]).then_inc(s, 16))(atc[bi], c),
            writes=[atc_b[bi]], tag=f"{tag}_lda{c}")
        p.dma("sync", (lambda dst, c: lambda e, s: e.dma_start(
            out=dst[:].rearrange("p (j t) -> p j t", t=CH),
            in_=fm_view(resid_hbm)[:, :, CH * c:CH * (c + 1)]).then_inc(s, 16))(xc[bi], c),
            writes=[xc_b[bi]], tag=f"{tag}_ldx{c}")
        for ft in range(NT):
            pb = ft % 2
            ob = ft % 4

            def omm(e, bi=bi, ft=ft, pb=pb):
                for et in range(NT):
                    inst = e.matmul(ps[:, CH * pb:CH * (pb + 1)],
                                    wo[:, D * et + 128 * ft:D * et + 128 * (ft + 1)],
                                    atc[bi][:, CH * et:CH * (et + 1)],
                                    start=(et == 0), stop=(et == NT - 1))
                return inst
            p.step("tensor", omm, reads=[wo_b, atc_b[bi]], writes=[ps_b[pb]], tag=f"{tag}_mm{c}_{ft}")
            p.step("vector", (lambda bi, ft, pb, ob: lambda e: e.tensor_add(
                oc[ob][:], ps[:, CH * pb:CH * (pb + 1)], xc[bi][:, CH * ft:CH * (ft + 1)]))(bi, ft, pb, ob),
                reads=[ps_b[pb], xc_b[bi]], writes=[oc_b[ob]], tag=f"{tag}_ev{c}_{ft}")
            p.dma("sync", (lambda ft, c, ob: lambda e, s: e.dma_start(
                out=out_hbm[128 * ft:128 * (ft + 1), CH * c:CH * (c + 1)], in_=oc[ob][:]).then_inc(s, 16))(ft, c, ob),
                reads=[oc_b[ob]], tag=f"{tag}_st{c}_{ft}")
    p.pop_scope()
    return p.barrier(f"{tag}_end")



L1_CONSTS = ["IDF", "ONESCOL", "ONESROW", "TRI_LO", "TRI_HI"]


def build_launch1(dbg=False):
    nc = bass.Bass("TRN2", target_bir_lowering=False, debug=False)
    x_in = nc.dram_tensor("x", [S, D], F32, kind="ExternalInput").ap()
    wqkv = nc.dram_tensor("w_qkv", [3 * D, D], F32, kind="ExternalInput").ap()
    wout = nc.dram_tensor("w_out", [D, D], F32, kind="ExternalInput").ap()
    xfm_out = nc.dram_tensor("xfm", [D, S], F32, kind="ExternalOutput").ap()
    x1fm = nc.dram_tensor("x1fm", [D, S], F32, kind="ExternalOutput").ap()
    knd = "ExternalOutput" if dbg else "Internal"
    xn_hbm = nc.dram_tensor("xn_s", [D, S], BF16, kind=knd).ap()
    wqkv_b = nc.dram_tensor("wqkv_b", [3 * D, D], BF16, kind="Internal").ap()
    wout_b = nc.dram_tensor("wout_b", [D, D], BF16, kind="Internal").ap()
    attnT = nc.dram_tensor("attnT_s", [D, S], BF16, kind=knd).ap()

    p = Prog(nc)
    cst = declare_consts(nc, p, L1_CONSTS)
    p.dma("gpsimd", lambda e, s: e.dma_start(out=wqkv_b, in_=wqkv).then_inc(s, 16), n_dma=1, tag="wc1")
    p.dma("gpsimd", lambda e, s: e.dma_start(out=wout_b, in_=wout).then_inc(s, 16), n_dma=1, tag="wc2")
    emit_in_transpose(p, nc, x_in, xfm_out, cst)
    emit_ln(p, nc, xfm_out, xn_hbm, cst, "ln0a")
    emit_l0_attn(p, nc, xn_hbm, wqkv_b, attnT, cst)
    emit_fm_proj_residual(p, nc, attnT, wout_b, xfm_out, x1fm, cst, "o")
    p.emit()
    return nc




def emit_scale_cols(p, nc, vec_ap, n, tag, cst, pst, pst_b, col):
    tmp = p.sbuf(f"{tag}_r", [max(n, 1), 128], F32)
    tb, cb = p.buf(), p.buf()
    p.dma("sync", lambda e, s: e.dma_start(out=tmp[:], in_=vec_ap.rearrange("(i p) -> i p", p=128)).then_inc(s, 16),
          writes=[tb], tag=f"{tag}_ld")
    p.step("tensor", lambda e: e.transpose(pst[:, 0:n], tmp[:], cst["IDF"][0:max(n, 1), 0:max(n, 1)]),
           reads=[tb], writes=[pst_b], tag=f"{tag}_tr")
    p.step("vector", lambda e: e.tensor_copy(col[:], pst[:, 0:n]), reads=[pst_b], writes=[cb], tag=f"{tag}_cp")
    return cb


def emit_l0_ffn(p, nc, x1fm, xn_hbm, wgb16, wub16, wdb16, sg_ap, su_ap, sd_ap, fuse_hbm, x2fm, cst, gate_dbg=None):
    INTER = 8192
    NO = INTER // 128
    p.push_scope()
    sg_col = p.sbuf("sg_c", [128, 64], F32)
    su_col = p.sbuf("su_c", [128, 64], F32)
    sd_col = p.sbuf("sd_c", [128, 16], F32)
    p.push_scope()
    pst_sc = p.psum("sc_ps", [128, 128])
    pst_sc_b = p.buf()
    sg_b = emit_scale_cols(p, nc, sg_ap, 64, "sg", cst, pst_sc, pst_sc_b, sg_col)
    su_b = emit_scale_cols(p, nc, su_ap, 64, "su", cst, pst_sc, pst_sc_b, su_col)
    sd_b = emit_scale_cols(p, nc, sd_ap, 16, "sd", cst, pst_sc, pst_sc_b, sd_col)
    p.pop_scope()
    p.barrier("sc_end")

    p.push_scope()
    xn = p.sbuf("f_xn", [128, NT * S], BF16)
    xn_b = p.buf()
    p.dma("sync", lambda e, s: e.dma_start(
        out=xn[:].rearrange("p (j t) -> p j t", t=S), in_=fm_view(xn_hbm)).then_inc(s, 16),
        writes=[xn_b], tag="f_ldxn")
    wg = [p.sbuf("f_wg0", [128, NT * 512], BF16)] * 2
    wu = [p.sbuf("f_wu0", [128, NT * 512], BF16)] * 2
    _wgb = p.buf(); _wub = p.buf()
    wg_b = [_wgb, _wgb]
    wu_b = [_wub, _wub]
    gsb = p.sbuf("f_gate", [128, S], F32)
    gsb_b = p.buf()
    fsb = [p.sbuf(f"f_fuse{i}", [128, S], BF16) for i in range(2)]
    fsb_b = [p.buf(), p.buf()]
    ps = [p.psum(f"f_ps{i}", [128, 2048]) for i in range(2)]
    ps_b = [p.buf(), p.buf()]
    for og in range(16):
        gi = og % 2
        def wload(e, s, og=og, gi=gi):
            e.dma_start(out=wg[gi][:].rearrange("p (j o) -> p j o", o=512),
                        in_=wgb16.rearrange("(j p) o -> p j o", p=128)[:, :, 512 * og:512 * (og + 1)]).then_inc(s, 16)
            e.dma_start(out=wu[gi][:].rearrange("p (j o) -> p j o", o=512),
                        in_=wub16.rearrange("(j p) o -> p j o", p=128)[:, :, 512 * og:512 * (og + 1)]).then_inc(s, 16)
        p.dma("sync", wload, writes=[wg_b[gi], wu_b[gi]], n_dma=2, tag=f"f_wl{og}")
        for ot in range(4):
            o = 4 * og + ot
            for half in range(2):
                pb = half
                def gmm(e, gi=gi, ot=ot, half=half, pb=pb):
                    for h in range(NT):
                        for q in range(4):
                            inst = e.matmul(ps[pb][:, 512 * q:512 * (q + 1)],
                                            wg[gi][:, 512 * h + 128 * ot:512 * h + 128 * (ot + 1)],
                                            xn[:, S * h + 2048 * half + 512 * q:S * h + 2048 * half + 512 * (q + 1)],
                                            start=(h == 0), stop=(h == NT - 1))
                    return inst
                p.step("tensor", gmm, reads=[wg_b[gi], xn_b], writes=[ps_b[pb]], tag=f"f_gmm{o}_{half}")
                p.step("scalar", (lambda o, half, pb: lambda e: e.activation(
                    gsb[:, 2048 * half:2048 * (half + 1)], ps[pb][:], AF.Silu, scale=sg_col[:, o:o + 1]))(o, half, pb),
                    reads=[ps_b[pb], sg_b], writes=[gsb_b], tag=f"f_gev{o}_{half}")
            for half in range(2):
                pb = half
                def umm(e, gi=gi, ot=ot, half=half, pb=pb):
                    for h in range(NT):
                        for q in range(4):
                            inst = e.matmul(ps[pb][:, 512 * q:512 * (q + 1)],
                                            wu[gi][:, 512 * h + 128 * ot:512 * h + 128 * (ot + 1)],
                                            xn[:, S * h + 2048 * half + 512 * q:S * h + 2048 * half + 512 * (q + 1)],
                                            start=(h == 0), stop=(h == NT - 1))
                    return inst
                p.step("tensor", umm, reads=[wu_b[gi], xn_b], writes=[ps_b[pb]], tag=f"f_umm{o}_{half}")
                p.step("vector", (lambda o, half, pb, fi: lambda e: e.scalar_tensor_tensor(
                    fsb[fi][:, 2048 * half:2048 * (half + 1)], ps[pb][:], su_col[:, o:o + 1],
                    gsb[:, 2048 * half:2048 * (half + 1)], ALU.mult, ALU.mult))(o, half, pb, o % 2),
                    reads=[ps_b[pb], su_b, gsb_b], writes=[fsb_b[o % 2]], tag=f"f_uev{o}_{half}")
            if gate_dbg is not None:
                p.dma("scalar", (lambda o: lambda e, s: e.dma_start(
                    out=gate_dbg[128 * o:128 * (o + 1), :], in_=gsb[:]).then_inc(s, 16))(o),
                    reads=[gsb_b], tag=f"f_gdb{o}")
            p.dma("sync", (lambda o, fi: lambda e, s: e.dma_start(
                out=fuse_hbm[128 * o:128 * (o + 1), :], in_=fsb[fi][:]).then_inc(s, 16))(o, o % 2),
                reads=[fsb_b[o % 2]], tag=f"f_fst{o}")
    p.pop_scope()
    p.barrier("f_gu_end")

    p.push_scope()
    TC = 512
    ntc = S // TC
    wd = p.sbuf("f_wd", [128, NO * 1024], BF16)
    wd_b = p.buf()
    fc = p.sbuf("f_fc0", [128, NO * TC], BF16)
    fc_b = p.buf()
    xr = [p.sbuf(f"f_xr{i}", [128, TC], F32) for i in range(2)]
    xr_b = [p.buf(), p.buf()]
    ob = [p.sbuf(f"f_ob{i}", [128, TC], F32) for i in range(2)]
    ob_b = [p.buf() for _ in range(2)]
    psd = [p.psum(f"f_psd{i}", [128, 512]) for i in range(2)]
    psd_b = [p.buf(), p.buf()]
    for h2h in range(2):
        def wdload(e, s, h2h=h2h):
            e.dma_start(out=wd[:].rearrange("p (j o) -> p j o", o=1024),
                        in_=wdb16.rearrange("(j p) o -> p j o", p=128)[:, :, 1024 * h2h:1024 * (h2h + 1)]).then_inc(s, 16)
        p.dma("sync", wdload, writes=[wd_b], n_dma=1, tag=f"f_wdl{h2h}")
        for tc in range(ntc):
            p.dma("sync", (lambda tc: lambda e, s: e.dma_start(
                out=fc[:].rearrange("p (j t) -> p j t", t=TC),
                in_=fuse_hbm.rearrange("(j p) t -> p j t", p=128)[:, :, TC * tc:TC * (tc + 1)]).then_inc(s, 16))(tc),
                writes=[fc_b], tag=f"f_fld{h2h}_{tc}")
            for h2t in range(8):
                pb = h2t % 2
                h2 = 8 * h2h + h2t
                p.dma("scalar", (lambda h2, tc, pb: lambda e, s: e.dma_start(
                    out=xr[pb][:], in_=x1fm[128 * h2:128 * (h2 + 1), TC * tc:TC * (tc + 1)]).then_inc(s, 16))(h2, tc, pb),
                    writes=[xr_b[pb]], tag=f"f_xld{h2h}_{tc}_{h2t}")

                def dmm(e, h2t=h2t, pb=pb):
                    for ot in range(NO):
                        inst = e.matmul(psd[pb][:],
                                        wd[:, 1024 * ot + 128 * h2t:1024 * ot + 128 * (h2t + 1)],
                                        fc[:, TC * ot:TC * (ot + 1)],
                                        start=(ot == 0), stop=(ot == NO - 1))
                    return inst
                p.step("tensor", dmm, reads=[wd_b, fc_b], writes=[psd_b[pb]], tag=f"f_dmm{h2h}_{tc}_{h2t}")
                p.step("vector", (lambda h2, pb: lambda e: e.scalar_tensor_tensor(
                    ob[pb][:], psd[pb][:], sd_col[:, h2:h2 + 1],
                    xr[pb][:], ALU.mult, ALU.add))(h2, pb),
                    reads=[psd_b[pb], sd_b, xr_b[pb]], writes=[ob_b[pb]], tag=f"f_dev{h2h}_{tc}_{h2t}")
                p.dma("sync", (lambda h2, tc, pb: lambda e, s: e.dma_start(
                    out=x2fm[128 * h2:128 * (h2 + 1), TC * tc:TC * (tc + 1)], in_=ob[pb][:]).then_inc(s, 16))(h2, tc, pb),
                    reads=[ob_b[pb]], tag=f"f_dst{h2h}_{tc}_{h2t}")
    p.pop_scope()
    p.pop_scope()
    return p.barrier("f_end")


L2_CONSTS = ["IDF", "ONESCOL", "ONESROW"]


def build_launch2(dbg=False):
    nc = bass.Bass("TRN2", target_bir_lowering=False, debug=False)
    x1fm = nc.dram_tensor("x1fm", [D, S], F32, kind="ExternalInput").ap()
    wg8 = nc.dram_tensor("w_gate", [D, 8192], I8, kind="ExternalInput").ap()
    wu8 = nc.dram_tensor("w_up", [D, 8192], I8, kind="ExternalInput").ap()
    wd8 = nc.dram_tensor("w_down", [8192, D], I8, kind="ExternalInput").ap()
    sg = nc.dram_tensor("s_gate", [8192], F32, kind="ExternalInput").ap()
    su = nc.dram_tensor("s_up", [8192], F32, kind="ExternalInput").ap()
    sd = nc.dram_tensor("s_down", [D], F32, kind="ExternalInput").ap()
    x2fm = nc.dram_tensor("x2fm", [D, S], F32, kind="ExternalOutput").ap()
    knd = "ExternalOutput" if dbg else "Internal"
    xn_hbm = nc.dram_tensor("xn_s", [D, S], BF16, kind=knd).ap()
    fuse_hbm = nc.dram_tensor("fuse_s", [8192, S], BF16, kind=knd).ap()
    wgb16 = nc.dram_tensor("wgb16", [D, 8192], BF16, kind="Internal").ap()
    wub16 = nc.dram_tensor("wub16", [D, 8192], BF16, kind="Internal").ap()
    wdb16 = nc.dram_tensor("wdb16", [8192, D], BF16, kind="Internal").ap()

    p = Prog(nc)
    cst = declare_consts(nc, p, L2_CONSTS)
    def ffnpc(e, s):
        e.dma_start(out=wgb16, in_=wg8).then_inc(s, 16)
        e.dma_start(out=wub16, in_=wu8).then_inc(s, 16)
        e.dma_start(out=wdb16, in_=wd8).then_inc(s, 16)
    p.dma("gpsimd", ffnpc, n_dma=3, tag="f_precast")
    gate_dbg = nc.dram_tensor("gate_s", [8192, S], F32, kind="ExternalOutput").ap() if dbg else None
    emit_ln(p, nc, x1fm, xn_hbm, cst, "ln0f")
    emit_l0_ffn(p, nc, x1fm, xn_hbm, wgb16, wub16, wdb16, sg, su, sd, fuse_hbm, x2fm, cst, gate_dbg)
    p.emit()
    return nc




def emit_l1_attn(p, nc, x2fm, xn_hbm, wqb, wkb, wvb, bq_ap, bk_ap, bv_ap, x3fm, cst, dbg_ap=None):
    PC = 512
    npc = S // PC
    p.push_scope()
    bqs_col = p.sbuf("g_bqs", [128, NT], F32)
    bkr = p.sbuf("g_bkr", [1, 1024], F32)
    bkr_b = p.buf()
    bkb = p.sbuf("g_bkb", [128, 1024], F32)
    bkb_b = p.buf()
    p.push_scope()
    pst_sc = p.psum("g_scps", [128, 128])
    pst_sc_b = p.buf()
    bqs_b = emit_scale_cols(p, nc, bq_ap, NT, "gbq", cst, pst_sc, pst_sc_b, bqs_col)
    sc = p.step("vector", lambda e: e.tensor_scalar_mul(bqs_col[:], bqs_col[:], ASCALE),
                reads=[bqs_b], writes=[bqs_b], tag="g_bqs_s")
    p.dma("sync", lambda e, s: [e.dma_start(out=bkr[0:1, 0:512], in_=bk_ap[None, :]).then_inc(s, 16),
                                e.dma_start(out=bkr[0:1, 512:1024], in_=bv_ap[None, :]).then_inc(s, 16)][-1],
          writes=[bkr_b], n_dma=2, tag="g_bkr")
    for i in range(8):
        p.step("tensor", (lambda i: lambda e: e.matmul(pst_sc[:], cst["ONESROW"][0:1, :], bkr[0:1, 128 * i:128 * (i + 1)],
                                                       start=True, stop=True))(i),
               reads=[bkr_b], writes=[pst_sc_b], tag=f"g_bkb_mm{i}")
        p.step("vector", (lambda i: lambda e: e.tensor_copy(bkb[:, 128 * i:128 * (i + 1)], pst_sc[:]))(i),
               reads=[pst_sc_b], writes=[bkb_b], tag=f"g_bkb_cp{i}")
    p.pop_scope()
    p.barrier("g_pre")

    wqT = p.sbuf("g_wqT", [128, NT * D], BF16)
    wkT = p.sbuf("g_wkT", [128, NT * 512], BF16)
    wvT = p.sbuf("g_wvT", [128, NT * 512], BF16)
    wqT_b, wkT_b, wvT_b = p.buf(), p.buf(), p.buf()

    def wload(e, s):
        for et in range(NT):
            e.dma_start(out=wqT[:, D * et:D * (et + 1)],
                        in_=wqb[:, 128 * et:128 * (et + 1)], transpose=True).then_inc(s, 16)
            e.dma_start(out=wkT[:, 512 * et:512 * (et + 1)],
                        in_=wkb[:, 128 * et:128 * (et + 1)], transpose=True).then_inc(s, 16)
            e.dma_start(out=wvT[:, 512 * et:512 * (et + 1)],
                        in_=wvb[:, 128 * et:128 * (et + 1)], transpose=True).then_inc(s, 16)
    p.dma("sync", wload, writes=[wqT_b, wkT_b, wvT_b], n_dma=3 * NT, tag="g_wl")

    Sst = p.sbuf("g_S", [128, 4 * 128], F32)
    Sbf = p.sbuf("g_Sbf", [128, 4 * 128], BF16)
    S_b = [p.buf() for _ in range(4)]
    Sbf_b = [p.buf() for _ in range(4)]
    p.step("vector", lambda e: e.memset(Sst[:], 0.0), writes=S_b, tag="g_S0")
    p.step("gpsimd", lambda e: e.memset(Sbf[:], 0.0), writes=Sbf_b, tag="g_Sbf0")

    xnc = p.sbuf("g_xn", [128, NT * PC], BF16)
    xnc_b = p.buf()
    qT = p.sbuf("g_qT", [128, NT * PC], BF16)
    qT_b = p.buf()
    kpre = p.sbuf("g_kpre", [128, 512], F32)
    vF = p.sbuf("g_vF", [128, 512], BF16)
    vH = p.sbuf("g_vH", [64, 512], BF16)
    vQ1 = p.sbuf("g_vQ1", [32, 512], BF16)
    vQ3 = p.sbuf("g_vQ3", [32, 512], BF16)
    knF = p.sbuf("g_knF", [128, 512], BF16)
    knH = p.sbuf("g_knH", [64, 512], BF16)
    knQ1 = p.sbuf("g_knQ1", [32, 512], BF16)
    knQ3 = p.sbuf("g_knQ3", [32, 512], BF16)
    Ee = p.sbuf("g_E", [128, 512], F32)
    Rr = p.sbuf("g_R", [128, 512], F32)
    kf = p.sbuf("g_kf", [128, 512], F32)
    uu = p.sbuf("g_u", [128, 512], F32)
    gg = p.sbuf("g_g", [128, 512], F32)
    er = p.sbuf("g_er", [128, 512], F32)
    cT_sb = p.sbuf("g_cT", [128, 512], F32)
    cs = p.sbuf("g_cs", [128, 512], F32)
    eqc = p.sbuf("g_eqc", [128, 512], BF16)
    emc = p.sbuf("g_emc", [128, 512], F32)
    khatT = p.sbuf("g_khatT", [128, 512], BF16)
    qst = p.sbuf("g_qst", [128, 512], BF16)
    ect = p.sbuf("g_ect", [128, 16], F32)
    ecm = p.sbuf("g_ecm", [128, 16], F32)
    pT_sb = p.sbuf("g_pT", [32, 128], BF16)
    x2c = [p.sbuf(f"g_x2c{i}", [128, NT * 128], F32) for i in range(2)]
    x3c = [p.sbuf(f"g_x3c{i}", [128, NT * 128], F32) for i in range(2)]
    kpre_b = p.buf(); vAB_b = p.buf(); knAB_b = p.buf()
    E_b = p.buf(); R_b = p.buf(); kf_b = p.buf(); u_b = p.buf(); g_b = p.buf(); er_b = p.buf()
    cT_b = p.buf(); cs_b = p.buf(); eqc_b = p.buf(); emc_b = p.buf(); khatT_b = p.buf()
    qst_b = p.buf(); ect_b = p.buf(); ecm_b = p.buf(); pT_b = p.buf()
    x2c_b = [p.buf(), p.buf()]; x3c_b = [p.buf(), p.buf()]

    ps_q = p.psum("g_psq", [128, 1024])
    ps_q_b = [p.buf(), p.buf()]
    ps_c = p.psum("g_psc", [128, 512])
    ps_c_b = p.buf()
    ps_r = p.psum("g_psr", [128, 512])
    ps_r_b = p.buf()
    ps_cT = p.psum("g_pscT", [128, 512])
    ps_cT_b = p.buf()
    ps_kT = p.psum("g_pskT", [128, 512])
    ps_kT_b = p.buf()
    ps_o = p.psum("g_pso", [128, 512])
    ps_o_b = p.buf()
    ps_su = p.psum("g_pssu", [128, 128])
    ps_su_b = p.buf()

    for pc in range(npc):
        p.dma("gpsimd", (lambda pc: lambda e, s: e.dma_start(
            out=xnc[:].rearrange("p (j t) -> p j t", t=PC),
            in_=fm_view(xn_hbm)[:, :, PC * pc:PC * (pc + 1)]).then_inc(s, 16))(pc),
            writes=[xnc_b], tag=f"g_ldx{pc}")
        for ft in range(NT):
            pb = ft % 2
            def qmm(e, ft=ft, pb=pb):
                for et in range(NT):
                    inst = e.matmul(ps_q[:, 512 * pb:512 * (pb + 1)],
                                    wqT[:, D * et + 128 * ft:D * et + 128 * (ft + 1)],
                                    xnc[:, PC * et:PC * (et + 1)],
                                    start=(et == 0), stop=(et == NT - 1))
                return inst
            p.step("tensor", qmm, reads=[wqT_b, xnc_b], writes=[ps_q_b[pb]], tag=f"g_qmm{pc}_{ft}")
            p.step("scalar", (lambda ft, pb: lambda e: e.activation(
                qT[:, PC * ft:PC * (ft + 1)], ps_q[:, 512 * pb:512 * (pb + 1)], AF.Relu,
                scale=ASCALE, bias=bqs_col[:, ft:ft + 1]))(ft, pb),
                reads=[ps_q_b[pb], bqs_b], writes=[qT_b], tag=f"g_qev{pc}_{ft}")
        for gb in range(PC // 128):
            t0 = 128 * gb
            def kvmm(e, gb=gb, t0=t0):
                for et in range(NT):
                    e.matmul(ps_q[:, 0:512],
                             xnc[:, PC * et + t0:PC * et + t0 + 128],
                             wkT[:, 512 * et:512 * (et + 1)],
                             start=(et == 0), stop=(et == NT - 1))
                for et in range(NT):
                    inst = e.matmul(ps_q[:, 512:1024],
                                    xnc[:, PC * et + t0:PC * et + t0 + 128],
                                    wvT[:, 512 * et:512 * (et + 1)],
                                    start=(et == 0), stop=(et == NT - 1))
                return inst
            p.step("tensor", kvmm, reads=[wkT_b, wvT_b, xnc_b], writes=[ps_q_b[0], ps_q_b[1]], tag=f"g_kv{pc}_{gb}")
            p.step("vector", lambda e: e.tensor_add(kpre[:], ps_q[:, 0:512], bkb[:, 0:512]),
                   reads=[ps_q_b[0], bkb_b], writes=[kpre_b], tag=f"g_kev{pc}_{gb}")
            p.step("vector", lambda e: e.tensor_add(vF[:], ps_q[:, 512:1024], bkb[:, 512:1024]),
                   reads=[ps_q_b[1], bkb_b], writes=[vAB_b], tag=f"g_vev{pc}_{gb}")
            p.step("gpsimd", lambda e: e.tensor_copy(vQ1[:], vF[32:64, :]), reads=[vAB_b], writes=[vAB_b])
            p.step("gpsimd", lambda e: e.tensor_copy(vH[:], vF[64:128, :]), reads=[vAB_b], writes=[vAB_b])
            p.step("gpsimd", lambda e: e.tensor_copy(vQ3[:], vH[32:64, :]), reads=[vAB_b], writes=[vAB_b])
            p.step("scalar", lambda e: e.activation(Ee[:], kpre[:], AF.Exp, scale=-1.0),
                   reads=[kpre_b], writes=[E_b], tag=f"g_E{pc}_{gb}")
            p.step("vector", lambda e: e.tensor_scalar_add(Rr[:], Ee[:], 1.0), reads=[E_b], writes=[R_b])
            p.step("vector", lambda e: e.reciprocal(Rr[:], Rr[:]), reads=[R_b], writes=[R_b])
            p.step("vector", lambda e: e.tensor_scalar_min(kf[:], Rr[:], 0.95), reads=[R_b], writes=[kf_b])
            p.step("gpsimd", lambda e: e.tensor_mul(uu[:], Ee[:], Rr[:]), reads=[E_b, R_b], writes=[u_b])
            p.step("vector", lambda e: e.tensor_scalar_max(uu[:], uu[:], 0.05), reads=[u_b], writes=[u_b])
            p.step("scalar", lambda e: e.activation(gg[:], uu[:], AF.Ln), reads=[u_b], writes=[g_b], tag=f"g_g{pc}_{gb}")
            p.step("tensor", lambda e: e.matmul(ps_r[:], cst["GLA_R"][:], gg[:], start=True, stop=True),
                   reads=[g_b], writes=[ps_r_b], tag=f"g_mr{pc}_{gb}")
            def ctmm(e):
                for g4 in range(4):
                    inst = e.matmul(ps_cT[:, 128 * g4:128 * (g4 + 1)],
                                    gg[:, 128 * g4:128 * (g4 + 1)], cst["GLA_L"][:],
                                    start=True, stop=True)
                return inst
            p.step("tensor", ctmm, reads=[g_b], writes=[ps_cT_b], tag=f"g_mcT{pc}_{gb}")
            def kftmm(e):
                for g4 in range(4):
                    inst = e.transpose(ps_kT[:, 128 * g4:128 * (g4 + 1)],
                                       kf[:, 128 * g4:128 * (g4 + 1)], cst["IDF"][:])
                return inst
            p.step("tensor", kftmm, reads=[kf_b], writes=[ps_kT_b], tag=f"g_mkT{pc}_{gb}")
            p.step("scalar", lambda e: e.activation(er[:], ps_r[:], AF.Exp), reads=[ps_r_b], writes=[er_b])
            p.step("vector", lambda e: e.tensor_mul(knF[:], kf[:], er[:]),
                   reads=[kf_b, er_b], writes=[knAB_b])
            p.step("gpsimd", lambda e: e.tensor_copy(knQ1[:], knF[32:64, :]), reads=[knAB_b], writes=[knAB_b])
            p.step("gpsimd", lambda e: e.tensor_copy(knH[:], knF[64:128, :]), reads=[knAB_b], writes=[knAB_b])
            p.step("gpsimd", lambda e: e.tensor_copy(knQ3[:], knH[32:64, :]), reads=[knAB_b], writes=[knAB_b])
            p.step("vector", lambda e: e.tensor_copy(cT_sb[:], ps_cT[:]), reads=[ps_cT_b], writes=[cT_b])
            def csf(e):
                for g4 in range(4):
                    for sb_ in range(4):
                        o = 128 * g4 + 32 * sb_
                        inst = e.tensor_scalar(cs[:, o:o + 32], cT_sb[:, o:o + 32],
                                               cT_sb[:, o + 15:o + 16], None, ALU.subtract)
                return inst
            p.step("vector", csf, reads=[cT_b], writes=[cs_b], tag=f"g_cs{pc}_{gb}")
            p.step("scalar", lambda e: e.activation(eqc[:], cs[:], AF.Exp), reads=[cs_b], writes=[eqc_b])
            p.step("scalar", lambda e: e.activation(emc[:], cs[:], AF.Exp, scale=-1.0), reads=[cs_b], writes=[emc_b])
            p.step("vector", lambda e: e.tensor_mul(khatT[:], ps_kT[:], emc[:]),
                   reads=[ps_kT_b, emc_b], writes=[khatT_b])
            def ectf(e):
                cr = cT_sb[:].rearrange("p (g b t) -> p g b t", g=4, t=32)
                er_ = ect[:].rearrange("p (g b) -> p g b", g=4)
                for g4 in range(4):
                    inst = e.activation(er_[:, g4, :], cr[:, g4, :, 31:32].rearrange("p b t -> p (b t)"), AF.Exp)
                return inst
            p.step("scalar", ectf, reads=[cT_b], writes=[ect_b], tag=f"g_ect{pc}_{gb}")
            def ecmf(e):
                cr = cT_sb[:].rearrange("p (g b t) -> p g b t", g=4, t=32)
                er_ = ecm[:].rearrange("p (g b) -> p g b", g=4)
                for g4 in range(4):
                    inst = e.activation(er_[:, g4, :], cr[:, g4, :, 15:16].rearrange("p b t -> p (b t)"), AF.Exp)
                return inst
            p.step("scalar", ecmf, reads=[cT_b], writes=[ecm_b], tag=f"g_ecm{pc}_{gb}")
            gbt = PC * pc + t0
            xb2 = (pc * 4 + gb) % 2
            p.dma("sync", (lambda gbt, xb2: lambda e, s: e.dma_start(
                out=x2c[xb2][:].rearrange("p (j t) -> p j t", t=128),
                in_=fm_view(x2fm)[:, :, gbt:gbt + 128]).then_inc(s, 16))(gbt, xb2),
                writes=[x2c_b[xb2]], tag=f"g_x2l{pc}_{gb}")
            for g4 in range(4):
                def qstf(e, g4=g4, t0=t0):
                    for s_ in range(4):
                        for r_ in range(4):
                            inst = e.tensor_mul(qst[:, 128 * s_ + 32 * r_:128 * s_ + 32 * (r_ + 1)],
                                                qT[:, PC * (4 * g4 + r_) + t0 + 32 * s_:PC * (4 * g4 + r_) + t0 + 32 * (s_ + 1)],
                                                eqc[:, 128 * g4 + 32 * s_:128 * g4 + 32 * (s_ + 1)])
                    return inst
                p.step("vector", qstf, reads=[qT_b, eqc_b], writes=[qst_b], tag=f"g_qst{pc}_{gb}_{g4}")
                halves = [(knF, vF), (knQ1, vQ1), (knH, vH), (knQ3, vQ3)]
                for s_ in range(4):
                    kn_h, v_h = halves[s_]
                    ro = 0
                    p.step("scalar", (lambda g4, s_: lambda e: e.mul(
                        Sbf[:, 128 * g4:128 * (g4 + 1)], Sst[:, 128 * g4:128 * (g4 + 1)],
                        ecm[:, 4 * g4 + s_:4 * g4 + s_ + 1]))(g4, s_),
                        reads=[S_b[g4], ecm_b], writes=[Sbf_b[g4]], tag=f"g_sbr{pc}_{gb}_{g4}_{s_}")
                    p.step("tensor", (lambda g4, s_: lambda e: e.matmul(
                        ps_c[0:32, 0:128], khatT[:, 128 * g4 + 32 * s_:128 * g4 + 32 * (s_ + 1)],
                        qst[:, 128 * s_:128 * (s_ + 1)], start=True, stop=True))(g4, s_),
                        reads=[khatT_b, qst_b], writes=[ps_c_b], tag=f"g_pmm{pc}_{gb}_{g4}_{s_}")
                    p.step("vector", lambda e: e.tensor_mul(pT_sb[:], ps_c[0:32, 0:128], cst["GLA_PMASK"][:]),
                           reads=[ps_c_b], writes=[pT_b], tag=f"g_pev{pc}_{gb}_{g4}_{s_}")
                    def omm(e, g4=g4, s_=s_, kn_h=kn_h, v_h=v_h, ro=ro):
                        e.matmul(ps_o[:, 128 * s_:128 * (s_ + 1)], Sbf[:, 128 * g4:128 * (g4 + 1)],
                                 qst[:, 128 * s_:128 * (s_ + 1)], start=True, stop=False)
                        return e.matmul(ps_o[:, 128 * s_:128 * (s_ + 1)], v_h[ro:ro + 32, 128 * g4:128 * (g4 + 1)],
                                        pT_sb[:], start=False, stop=True)
                    ost = p.step("tensor", omm, reads=[Sbf_b[g4], qst_b, vAB_b, pT_b], writes=[ps_o_b],
                           tag=f"g_omm{pc}_{gb}_{g4}_{s_}")
                    if dbg_ap is not None and pc == 7 and gb == 3 and g4 == 3 and s_ == 0:
                        dbg_sb = p.sbuf("g_dbg", [128, 512], F32)
                        dbg_b = p.buf()
                        p.step("vector", lambda e: e.tensor_copy(dbg_sb[:, 0:128], Sbf[:, 384:512]),
                               reads=[Sbf_b[3]], writes=[dbg_b], deps=[ost], tag="g_dbg1")
                        p.step("vector", lambda e: e.tensor_copy(dbg_sb[:, 128:256], ps_o[:, 0:128]),
                               reads=[ps_o_b], writes=[dbg_b], deps=[ost], tag="g_dbg2")
                        p.step("vector", lambda e: e.tensor_copy(dbg_sb[:, 256:384], Sst[:, 384:512]),
                               reads=[S_b[3]], writes=[dbg_b], deps=[ost], tag="g_dbg2b")
                        p.step("vector", lambda e: e.tensor_copy(dbg_sb[:, 384:512], ecm[:].rearrange("p f -> p f").to_broadcast([128, 128]) if False else eqc[:, 384:512]),
                               reads=[eqc_b], writes=[dbg_b], deps=[ost], tag="g_dbg2c")
                        p.dma("sync", lambda e, s: e.dma_start(out=dbg_ap, in_=dbg_sb[:]).then_inc(s, 16),
                              reads=[dbg_b], tag="g_dbg3")
                    p.step("tensor", (lambda g4, s_, kn_h=kn_h, v_h=v_h, ro=ro: lambda e: e.matmul(
                        ps_su[:], kn_h[ro:ro + 32, 128 * g4:128 * (g4 + 1)],
                        v_h[ro:ro + 32, 128 * g4:128 * (g4 + 1)], start=True, stop=True))(g4, s_),
                        reads=[knAB_b, vAB_b], writes=[ps_su_b], tag=f"g_smm{pc}_{gb}_{g4}_{s_}")
                    p.step("vector", (lambda g4, s_: lambda e: e.scalar_tensor_tensor(
                        Sst[:, 128 * g4:128 * (g4 + 1)], Sst[:, 128 * g4:128 * (g4 + 1)],
                        ect[:, 4 * g4 + s_:4 * g4 + s_ + 1], ps_su[:], ALU.mult, ALU.add))(g4, s_),
                        reads=[ps_su_b, ect_b, S_b[g4]], writes=[S_b[g4]], tag=f"g_sup{pc}_{gb}_{g4}_{s_}")
                po = ps_o[:].rearrange("p (s2 r t) -> p s2 r t", r=4, t=32)
                for r_ in range(4):
                    h = 4 * g4 + r_
                    p.step("vector", (lambda h, r_, xb2: lambda e: e.tensor_add(
                        x3c[xb2][:, 128 * h:128 * h + 128].rearrange("p (s2 t) -> p s2 t", t=32),
                        po[:, :, r_, :],
                        x2c[xb2][:, 128 * h:128 * (h + 1)].rearrange("p (s2 t) -> p s2 t", t=32)))(h, r_, xb2),
                        reads=[ps_o_b, x2c_b[xb2]], writes=[x3c_b[xb2]], tag=f"g_oev{pc}_{gb}_{g4}_{r_}")
            p.dma("sync", (lambda gbt, xb2: lambda e, s: e.dma_start(
                out=fm_view(x3fm)[:, :, gbt:gbt + 128],
                in_=x3c[xb2][:].rearrange("p (j t) -> p j t", t=128)).then_inc(s, 16))(gbt, xb2),
                reads=[x3c_b[xb2]], tag=f"g_x3s{pc}_{gb}")
    p.pop_scope()
    return p.barrier("g_end")


L3_CONSTS = ["IDF", "ONESCOL", "ONESROW", "GLA_L", "GLA_R", "GLA_PMASK"]


def build_launch3(dbg=False):
    nc = bass.Bass("TRN2", target_bir_lowering=False, debug=False)
    x2fm = nc.dram_tensor("x2fm", [D, S], F32, kind="ExternalInput").ap()
    Wq = nc.dram_tensor("Wq", [D, D], F32, kind="ExternalInput").ap()
    bq = nc.dram_tensor("bq", [D], F32, kind="ExternalInput").ap()
    Wk = nc.dram_tensor("Wk", [512, D], F32, kind="ExternalInput").ap()
    bk = nc.dram_tensor("bk", [512], F32, kind="ExternalInput").ap()
    Wv = nc.dram_tensor("Wv", [512, D], F32, kind="ExternalInput").ap()
    bv = nc.dram_tensor("bv", [512], F32, kind="ExternalInput").ap()
    x3fm = nc.dram_tensor("x3fm", [D, S], F32, kind="ExternalOutput").ap()
    knd = "ExternalOutput" if dbg else "Internal"
    xn_hbm = nc.dram_tensor("xn_s", [D, S], BF16, kind=knd).ap()
    wqb = nc.dram_tensor("wqb", [D, D], BF16, kind="Internal").ap()
    wkb = nc.dram_tensor("wkb", [512, D], BF16, kind="Internal").ap()
    wvb = nc.dram_tensor("wvb", [512, D], BF16, kind="Internal").ap()

    p = Prog(nc)
    cst = declare_consts(nc, p, L3_CONSTS)
    p.dma("gpsimd", lambda e, s: [e.dma_start(out=wqb, in_=Wq).then_inc(s, 16),
                                  e.dma_start(out=wkb, in_=Wk).then_inc(s, 16),
                                  e.dma_start(out=wvb, in_=Wv).then_inc(s, 16)][-1], n_dma=3, tag="g_wc")
    dbg_ap = nc.dram_tensor("dbg", [128, 512], F32, kind="ExternalOutput").ap() if dbg else None
    emit_ln(p, nc, x2fm, xn_hbm, cst, "ln1a")
    emit_l1_attn(p, nc, x2fm, xn_hbm, wqb, wkb, wvb, bq, bk, bv, x3fm, cst, dbg_ap)
    p.emit()
    return nc




def emit_l1_ffn(p, nc, x3fm, xn_hbm, wcb, genes_ap, wdb, wub, temp_ap, scale_ap,
                w_hbm, out_tm, cst):
    G = 4096
    p.push_scope()
    srow = p.sbuf("m_srow", [1, 2], F32)
    srow_b = p.buf()
    p.dma("sync", lambda e, s: [e.dma_start(out=srow[0:1, 0:1], in_=temp_ap[None, :]).then_inc(s, 16),
                                e.dma_start(out=srow[0:1, 1:2], in_=scale_ap[None, :]).then_inc(s, 16)][-1],
          writes=[srow_b], n_dma=2, tag="m_sld")
    p.step("vector", lambda e: e.tensor_scalar_max(srow[0:1, 0:1], srow[0:1, 0:1], 0.1),
           reads=[srow_b], writes=[srow_b])
    p.step("vector", lambda e: e.reciprocal(srow[0:1, 0:1], srow[0:1, 0:1]),
           reads=[srow_b], writes=[srow_b])
    sccol = p.sbuf("m_sccol", [128, 2], F32)
    sccol_b = p.buf()
    p.push_scope()
    ps_tmp = p.psum("m_pst", [128, 128])
    ps_tmp_b = p.buf()
    p.step("tensor", lambda e: e.matmul(ps_tmp[:, 0:2], cst["ONESROW"][0:1, :], srow[0:1, 0:2], start=True, stop=True),
           reads=[srow_b], writes=[ps_tmp_b], tag="m_scbc")
    p.step("vector", lambda e: e.tensor_copy(sccol[:], ps_tmp[:, 0:2]), reads=[ps_tmp_b], writes=[sccol_b])
    p.pop_scope()
    p.barrier("m_pre")

    p.push_scope()
    wcT = p.sbuf("m_wcT", [128, NT * G], BF16)
    wcT_b = p.buf()

    def wcl(e, s):
        for et in range(NT):
            e.dma_start(out=wcT[:, G * et:G * (et + 1)],
                        in_=wcb[:, 128 * et:128 * (et + 1)], transpose=True).then_inc(s, 16)
    p.dma("sync", wcl, writes=[wcT_b], n_dma=NT, tag="m_wcl")
    xnt = [p.sbuf(f"m_xnt{i}", [128, NT * 128], BF16) for i in range(2)]
    xnt_b = [p.buf(), p.buf()]
    lg2 = [p.sbuf(f"m_lg{i}", [128, G], BF16) for i in range(2)]
    lg2_b = [p.buf(), p.buf()]
    sa = p.sbuf("m_sa", [128, G], BF16)
    sb_ = p.sbuf("m_sb", [128, G], BF16)
    m8 = p.sbuf("m_m8", [128, 40], BF16)
    m8f = p.sbuf("m_m8f", [128, 2], F32)
    ge = p.sbuf("m_ge", [128, G], BF16)
    ee = p.sbuf("m_ee", [128, G], BF16)
    wb = p.sbuf("m_wb", [128, G], BF16)
    sa_b, sb_b, m8_b, ge_b, ee_b, wb_b = [p.buf() for _ in range(6)]
    ps_l = [p.psum(f"m_psl{i}", [128, 2048]) for i in range(2)]
    ps_l_b = [p.buf(), p.buf()]
    for tt in range(S // 128):
        bi = tt % 2
        p.dma("scalar", (lambda bi, tt: lambda e, s: e.dma_start(
            out=xnt[bi][:].rearrange("p (j t) -> p j t", t=128),
            in_=fm_view(xn_hbm)[:, :, 128 * tt:128 * (tt + 1)]).then_inc(s, 16))(bi, tt),
            writes=[xnt_b[bi]], tag=f"m_xl{tt}")
        lg = lg2[bi]
        lg_b = lg2_b[bi]
        for gh in range(2):
            def lmm(e, bi=bi, gh=gh):
                for et in range(NT):
                    for q in range(4):
                        inst = e.matmul(ps_l[gh][:, 512 * q:512 * (q + 1)],
                                        xnt[bi][:, 128 * et:128 * (et + 1)],
                                        wcT[:, G * et + 2048 * gh + 512 * q:G * et + 2048 * gh + 512 * (q + 1)],
                                        start=(et == 0), stop=(et == NT - 1))
                return inst
            p.step("tensor", lmm, reads=[xnt_b[bi], wcT_b], writes=[ps_l_b[gh]], tag=f"m_lmm{tt}_{gh}")
            p.step("vector", (lambda gh, lg=lg: lambda e: e.tensor_scalar_mul(
                lg[:, 2048 * gh:2048 * (gh + 1)], ps_l[gh][:], sccol[:, 0:1]))(gh),
                reads=[ps_l_b[gh], sccol_b], writes=[lg_b], tag=f"m_lev{tt}_{gh}")
        p.step("vector", (lambda lg: lambda e: e.max(out=m8[:, 0:8], in_=lg[:]))(lg), reads=[lg_b], writes=[m8_b])
        p.step("vector", (lambda lg: lambda e: e.match_replace(out=sa[:], in_to_replace=m8[:, 0:8], in_values=lg[:], imm_value=-1e30))(lg),
               reads=[lg_b, m8_b], writes=[sa_b])
        p.step("vector", lambda e: e.max(out=m8[:, 8:16], in_=sa[:]), reads=[sa_b], writes=[m8_b])
        p.step("vector", lambda e: e.match_replace(out=sb_[:], in_to_replace=m8[:, 8:16], in_values=sa[:], imm_value=-1e30),
               reads=[sa_b, m8_b], writes=[sb_b])
        p.step("vector", lambda e: e.max(out=m8[:, 16:24], in_=sb_[:]), reads=[sb_b], writes=[m8_b])
        p.step("vector", lambda e: e.match_replace(out=sa[:], in_to_replace=m8[:, 16:24], in_values=sb_[:], imm_value=-1e30),
               reads=[sb_b, m8_b], writes=[sa_b])
        p.step("vector", lambda e: e.max(out=m8[:, 24:32], in_=sa[:]), reads=[sa_b], writes=[m8_b])
        p.step("vector", lambda e: e.tensor_copy(m8f[:, 0:1], m8[:, 31:32]), reads=[m8_b], writes=[m8_b])
        p.step("vector", lambda e: e.tensor_scalar_mul(m8f[:, 1:2], m8[:, 0:1], -1.0), reads=[m8_b], writes=[m8_b])
        p.step("gpsimd", (lambda lg: lambda e: e.tensor_scalar(ge[:], lg[:], m8f[:, 0:1], None, ALU.is_ge))(lg),
               reads=[lg_b, m8_b], writes=[ge_b], tag=f"m_ge{tt}")
        p.step("scalar", (lambda lg: lambda e: e.activation(ee[:], lg[:], AF.Exp, bias=m8f[:, 1:2]))(lg),
               reads=[lg_b, m8_b], writes=[ee_b], tag=f"m_exp{tt}")
        p.step("gpsimd", lambda e: e.tensor_mul(wb[:], ee[:], ge[:]), reads=[ee_b, ge_b], writes=[wb_b], tag=f"m_w{tt}")
        p.dma("sync", (lambda tt: lambda e, s: e.dma_start(
            out=w_hbm[128 * tt:128 * (tt + 1), :], in_=wb[:]).then_inc(s, 16))(tt),
            reads=[wb_b], tag=f"m_ws{tt}")
    p.pop_scope()
    p.barrier("m_A_end")

    p.push_scope()
    TC = 512
    ntc = S // TC
    gen = p.sbuf("m_gen", [128, 32 * 256], BF16)
    gen_b = p.buf()

    def genl(e, s):
        e.dma_start(out=gen[:].rearrange("p (j c) -> p j c", c=256),
                    in_=genes_ap.rearrange("(j p) c -> p j c", p=128)).then_inc(s, 16)
    p.dma("gpsimd", genl, writes=[gen_b], n_dma=1, tag="m_genl")
    wdT = p.sbuf("m_wdT", [128, 2 * D], BF16)
    wuT = p.sbuf("m_wuT", [128, 2 * D], BF16)
    wdT_b, wuT_b = p.buf(), p.buf()

    def wdl(e, s):
        for ct in range(2):
            e.dma_start(out=wdT[:, D * ct:D * (ct + 1)],
                        in_=wdb[:, 128 * ct:128 * (ct + 1)], transpose=True).then_inc(s, 16)
            e.dma_start(out=wuT[:, D * ct:D * (ct + 1)],
                        in_=wub[:, 128 * ct:128 * (ct + 1)], transpose=True).then_inc(s, 16)
    p.dma("sync", wdl, writes=[wdT_b, wuT_b], n_dma=4, tag="m_wdl")
    wT = [p.sbuf(f"m_wT{i}", [128, 32 * TC], BF16) for i in range(2)]
    wT_b = [p.buf(), p.buf()]
    xnc = p.sbuf("m_xnc", [128, NT * TC], BF16)
    xnc_b = p.buf()
    x3c = p.sbuf("m_x3c", [128, NT * TC], F32)
    x3c_b = p.buf()
    rz = p.sbuf("m_rz", [1, TC], F32)
    rz_b = p.buf()
    rzb = p.sbuf("m_rzb", [128, TC], F32)
    rzb_b = p.buf()
    exT = p.sbuf("m_exT", [128, 2 * TC], BF16)
    exT_b = p.buf()
    dn = p.sbuf("m_dn", [128, TC], BF16)
    up = p.sbuf("m_up", [128, TC], BF16)
    dn_b, up_b = p.buf(), p.buf()
    t1 = p.sbuf("m_t1", [128, TC], BF16)
    t1_b = p.buf()
    ofm = p.sbuf("m_ofm", [128, NT * TC], F32)
    ofm_b = p.buf()
    otr = [p.sbuf(f"m_otr{i}", [128, D], F32) for i in range(2)]
    otr_b = [p.buf(), p.buf()]
    ps_z = p.psum("m_psz", [1, TC])
    ps_z_b = p.buf()
    ps_b1 = p.psum("m_psb1", [128, TC])
    ps_b1_b = p.buf()
    ps_b2 = p.psum("m_psb2", [128, TC])
    ps_b2_b = p.buf()
    ps_b3 = p.psum("m_psb3", [128, TC])
    ps_b3_b = p.buf()
    ps_tr = p.psum("m_pstr", [128, D])
    ps_tr_b = p.buf()
    for tc in range(ntc):
        wbi = tc % 2
        def wTl(e, s, tc=tc, wbi=wbi):
            for gt in range(32):
                e.dma_start(out=wT[wbi][:, TC * gt:TC * (gt + 1)],
                            in_=w_hbm[TC * tc:TC * (tc + 1), 128 * gt:128 * (gt + 1)], transpose=True).then_inc(s, 16)
        p.dma("sync", wTl, writes=[wT_b[wbi]], n_dma=32, tag=f"m_wTl{tc}")
        p.dma("gpsimd", (lambda tc: lambda e, s: e.dma_start(
            out=xnc[:].rearrange("p (j t) -> p j t", t=TC),
            in_=fm_view(xn_hbm)[:, :, TC * tc:TC * (tc + 1)]).then_inc(s, 16))(tc),
            writes=[xnc_b], tag=f"m_xnl{tc}")
        p.dma("gpsimd", (lambda tc: lambda e, s: e.dma_start(
            out=x3c[:].rearrange("p (j t) -> p j t", t=TC),
            in_=fm_view(x3fm)[:, :, TC * tc:TC * (tc + 1)]).then_inc(s, 16))(tc),
            writes=[x3c_b], tag=f"m_x3l{tc}")

        def zmm(e, wbi=wbi):
            for gt in range(32):
                inst = e.matmul(ps_z[0:1, :], cst["ONESCOL"][:], wT[wbi][:, TC * gt:TC * (gt + 1)],
                                start=(gt == 0), stop=(gt == 31))
            return inst
        p.step("tensor", zmm, reads=[wT_b[wbi]], writes=[ps_z_b], tag=f"m_zmm{tc}")
        p.step("vector", lambda e: e.reciprocal(rz[:], ps_z[:]), reads=[ps_z_b], writes=[rz_b])
        def zbc(e):
            for q in range(TC // 512):
                inst = e.matmul(ps_b1[:, 512 * q:512 * (q + 1)], cst["ONESROW"][0:1, :],
                                rz[0:1, 512 * q:512 * (q + 1)], start=True, stop=True)
            return inst
        p.step("tensor", zbc, reads=[rz_b], writes=[ps_b1_b], tag=f"m_zbc{tc}")
        p.step("vector", lambda e: e.tensor_copy(rzb[:], ps_b1[:]), reads=[ps_b1_b], writes=[rzb_b])
        for ct in range(2):
            psx, psx_b = (ps_b2, ps_b2_b) if ct == 0 else (ps_b3, ps_b3_b)
            def emm(e, ct=ct, psx=psx, wbi=wbi):
                for gt in range(32):
                    inst = e.matmul(psx[:], gen[:, 256 * gt + 128 * ct:256 * gt + 128 * (ct + 1)],
                                    wT[wbi][:, TC * gt:TC * (gt + 1)],
                                    start=(gt == 0), stop=(gt == 31))
                return inst
            p.step("tensor", emm, reads=[gen_b, wT_b[wbi]], writes=[psx_b], tag=f"m_emm{tc}_{ct}")
            p.step("vector", (lambda ct, psx: lambda e: e.tensor_mul(
                exT[:, TC * ct:TC * (ct + 1)], psx[:], rzb[:]))(ct, psx),
                reads=[psx_b, rzb_b], writes=[exT_b], tag=f"m_eev{tc}_{ct}")
        for ft in range(NT):
            def dmm(e, ft=ft):
                for ct in range(2):
                    inst = e.matmul(ps_b2[:], wdT[:, D * ct + 128 * ft:D * ct + 128 * (ft + 1)],
                                    exT[:, TC * ct:TC * (ct + 1)], start=(ct == 0), stop=(ct == 1))
                return inst
            p.step("tensor", dmm, reads=[wdT_b, exT_b], writes=[ps_b2_b], tag=f"m_dmm{tc}_{ft}")
            p.step("scalar", lambda e: e.activation(dn[:], ps_b2[:], AF.Tanh), reads=[ps_b2_b], writes=[dn_b])
            def umm(e, ft=ft):
                for ct in range(2):
                    inst = e.matmul(ps_b3[:], wuT[:, D * ct + 128 * ft:D * ct + 128 * (ft + 1)],
                                    exT[:, TC * ct:TC * (ct + 1)], start=(ct == 0), stop=(ct == 1))
                return inst
            p.step("tensor", umm, reads=[wuT_b, exT_b], writes=[ps_b3_b], tag=f"m_umm{tc}_{ft}")
            p.step("scalar", lambda e: e.activation(up[:], ps_b3[:], AF.Copy, scale=sccol[:, 1:2]),
                   reads=[ps_b3_b, sccol_b], writes=[up_b])
            p.step("vector", lambda e: e.tensor_mul(t1[:], dn[:], up[:]), reads=[dn_b, up_b], writes=[t1_b])
            p.step("gpsimd", (lambda ft: lambda e: e.tensor_mul(t1[:], t1[:], xnc[:, TC * ft:TC * (ft + 1)]))(ft),
                   reads=[t1_b, xnc_b], writes=[t1_b])
            p.step("vector", (lambda ft: lambda e: e.tensor_add(
                ofm[:, TC * ft:TC * (ft + 1)], t1[:], x3c[:, TC * ft:TC * (ft + 1)]))(ft),
                reads=[t1_b, x3c_b], writes=[ofm_b], tag=f"m_oev{tc}_{ft}")
        for tts in range(TC // 128):
            ob = tts % 2
            def trf(e, tts=tts):
                for ft in range(NT):
                    inst = e.transpose(ps_tr[:, 128 * ft:128 * (ft + 1)],
                                       ofm[:, TC * ft + 128 * tts:TC * ft + 128 * (tts + 1)], cst["IDF"][:])
                return inst
            p.step("tensor", trf, reads=[ofm_b], writes=[ps_tr_b], tag=f"m_tr{tc}_{tts}")
            p.step("vector", (lambda ob: lambda e: e.tensor_copy(otr[ob][:], ps_tr[:]))(ob),
                   reads=[ps_tr_b], writes=[otr_b[ob]], tag=f"m_trv{tc}_{tts}")
            p.dma("sync", (lambda tc, tts, ob: lambda e, s: e.dma_start(
                out=out_tm[TC * tc + 128 * tts:TC * tc + 128 * (tts + 1), :], in_=otr[ob][:]).then_inc(s, 16))(tc, tts, ob),
                reads=[otr_b[ob]], tag=f"m_os{tc}_{tts}")
    p.pop_scope()
    p.pop_scope()
    return p.barrier("m_end")


L4_CONSTS = ["IDF", "ONESCOL", "ONESROW"]


def build_launch4(dbg=False):
    nc = bass.Bass("TRN2", target_bir_lowering=False, debug=False)
    x3fm = nc.dram_tensor("x3fm", [D, S], F32, kind="ExternalInput").ap()
    Wc = nc.dram_tensor("Wc", [4096, D], F32, kind="ExternalInput").ap()
    temp = nc.dram_tensor("temp", [1], F32, kind="ExternalInput").ap()
    genes = nc.dram_tensor("genes", [4096, 256], F32, kind="ExternalInput").ap()
    Wd = nc.dram_tensor("Wd", [D, 256], F32, kind="ExternalInput").ap()
    Wu = nc.dram_tensor("Wu", [D, 256], F32, kind="ExternalInput").ap()
    nw = nc.dram_tensor("norm_w", [D], F32, kind="ExternalInput").ap()
    scale = nc.dram_tensor("scale", [1], F32, kind="ExternalInput").ap()
    out_tm = nc.dram_tensor("out", [S, D], F32, kind="ExternalOutput").ap()
    knd = "ExternalOutput" if dbg else "Internal"
    xn_hbm = nc.dram_tensor("xn_s", [D, S], BF16, kind=knd).ap()
    w_hbm = nc.dram_tensor("w_s", [S, 4096], BF16, kind=knd).ap()
    wcb = nc.dram_tensor("wcb", [4096, D], BF16, kind="Internal").ap()
    wdb = nc.dram_tensor("wdb", [D, 256], BF16, kind="Internal").ap()
    wub = nc.dram_tensor("wub", [D, 256], BF16, kind="Internal").ap()

    p = Prog(nc)
    cst = declare_consts(nc, p, L4_CONSTS)
    p.dma("gpsimd", lambda e, s: [e.dma_start(out=wcb, in_=Wc).then_inc(s, 16),
                                  e.dma_start(out=wdb, in_=Wd).then_inc(s, 16),
                                  e.dma_start(out=wub, in_=Wu).then_inc(s, 16)][-1], n_dma=3, tag="m_wc")
    nw_col = p.sbuf("m_nw", [128, NT], F32)
    p.push_scope()
    pst_sc = p.psum("m_scps", [128, 128])
    pst_sc_b = p.buf()
    nw_b = emit_scale_cols(p, nc, nw, NT, "mnw", cst, pst_sc, pst_sc_b, nw_col)
    p.pop_scope()
    p.barrier("m_nw_end")
    emit_ln(p, nc, x3fm, xn_hbm, cst, "ln1f", nw_col=nw_col, nw_col_buf=nw_b)
    emit_l1_ffn(p, nc, x3fm, xn_hbm, wcb, genes, wdb, wub, temp, scale, w_hbm, out_tm, cst)
    p.emit()
    return nc




ALL_CONSTS = ["IDF", "ONESCOL", "ONESROW", "TRI_LO", "TRI_HI", "GLA_L", "GLA_R", "GLA_PMASK"]


def build_fused():
    nc = bass.Bass("TRN2", target_bir_lowering=False, debug=False)
    x_in = nc.dram_tensor("x", [S, D], F32, kind="ExternalInput").ap()
    wqkv = nc.dram_tensor("w_qkv", [3 * D, D], F32, kind="ExternalInput").ap()
    wout = nc.dram_tensor("w_out", [D, D], F32, kind="ExternalInput").ap()
    wg8 = nc.dram_tensor("w_gate", [D, 8192], I8, kind="ExternalInput").ap()
    wu8 = nc.dram_tensor("w_up", [D, 8192], I8, kind="ExternalInput").ap()
    wd8 = nc.dram_tensor("w_down", [8192, D], I8, kind="ExternalInput").ap()
    sg = nc.dram_tensor("s_gate", [8192], F32, kind="ExternalInput").ap()
    su = nc.dram_tensor("s_up", [8192], F32, kind="ExternalInput").ap()
    sd = nc.dram_tensor("s_down", [D], F32, kind="ExternalInput").ap()
    Wq = nc.dram_tensor("Wq", [D, D], F32, kind="ExternalInput").ap()
    bq = nc.dram_tensor("bq", [D], F32, kind="ExternalInput").ap()
    Wk = nc.dram_tensor("Wk", [512, D], F32, kind="ExternalInput").ap()
    bk = nc.dram_tensor("bk", [512], F32, kind="ExternalInput").ap()
    Wv = nc.dram_tensor("Wv", [512, D], F32, kind="ExternalInput").ap()
    bv = nc.dram_tensor("bv", [512], F32, kind="ExternalInput").ap()
    Wc = nc.dram_tensor("Wc", [4096, D], F32, kind="ExternalInput").ap()
    temp = nc.dram_tensor("temp", [1], F32, kind="ExternalInput").ap()
    genes = nc.dram_tensor("genes", [4096, 256], F32, kind="ExternalInput").ap()
    Wd = nc.dram_tensor("Wd", [D, 256], F32, kind="ExternalInput").ap()
    Wu = nc.dram_tensor("Wu", [D, 256], F32, kind="ExternalInput").ap()
    nw = nc.dram_tensor("norm_w", [D], F32, kind="ExternalInput").ap()
    scale = nc.dram_tensor("scale", [1], F32, kind="ExternalInput").ap()
    out_tm = nc.dram_tensor("out", [S, D], F32, kind="ExternalOutput").ap()

    xfm = nc.dram_tensor("xfm_s", [D, S], F32, kind="Internal").ap()
    x1fm = nc.dram_tensor("x1fm_s", [D, S], F32, kind="Internal").ap()
    x2fm = nc.dram_tensor("x2fm_s", [D, S], F32, kind="Internal").ap()
    x3fm = nc.dram_tensor("x3fm_s", [D, S], F32, kind="Internal").ap()
    xn_hbm = nc.dram_tensor("xn_s", [D, S], BF16, kind="Internal").ap()
    attnT = nc.dram_tensor("attnT_s", [D, S], BF16, kind="Internal").ap()
    fuse_hbm = nc.dram_tensor("fuse_s", [8192, S], BF16, kind="Internal").ap()
    w_hbm = nc.dram_tensor("w_s", [S, 4096], BF16, kind="Internal").ap()
    wqkv_b = nc.dram_tensor("wqkv_b", [3 * D, D], BF16, kind="Internal").ap()
    wout_b = nc.dram_tensor("wout_b", [D, D], BF16, kind="Internal").ap()
    wqb = nc.dram_tensor("wqb", [D, D], BF16, kind="Internal").ap()
    wkb = nc.dram_tensor("wkb", [512, D], BF16, kind="Internal").ap()
    wvb = nc.dram_tensor("wvb", [512, D], BF16, kind="Internal").ap()
    wcb = nc.dram_tensor("wcb", [4096, D], BF16, kind="Internal").ap()
    wdb = nc.dram_tensor("wdb", [D, 256], BF16, kind="Internal").ap()
    wub = nc.dram_tensor("wub", [D, 256], BF16, kind="Internal").ap()

    p = Prog(nc)
    cst = declare_consts(nc, p, ALL_CONSTS)
    wgb16 = nc.dram_tensor("wgb16", [D, 8192], BF16, kind="Internal").ap()
    wub16 = nc.dram_tensor("wub16", [D, 8192], BF16, kind="Internal").ap()
    wdb16 = nc.dram_tensor("wdb16", [8192, D], BF16, kind="Internal").ap()

    def wcasts(e, s):
        e.dma_start(out=wqkv_b, in_=wqkv).then_inc(s, 16)
        e.dma_start(out=wout_b, in_=wout).then_inc(s, 16)
        e.dma_start(out=wqb, in_=Wq).then_inc(s, 16)
        e.dma_start(out=wkb, in_=Wk).then_inc(s, 16)
        e.dma_start(out=wvb, in_=Wv).then_inc(s, 16)
        e.dma_start(out=wcb, in_=Wc).then_inc(s, 16)
        e.dma_start(out=wdb, in_=Wd).then_inc(s, 16)
        e.dma_start(out=wub, in_=Wu).then_inc(s, 16)
        e.dma_start(out=wgb16, in_=wg8).then_inc(s, 16)
        e.dma_start(out=wub16, in_=wu8).then_inc(s, 16)
        e.dma_start(out=wdb16, in_=wd8).then_inc(s, 16)
    p.dma("gpsimd", wcasts, n_dma=11, tag="wcasts")
    emit_in_transpose(p, nc, x_in, xfm, cst)
    emit_ln(p, nc, xfm, xn_hbm, cst, "ln0a")
    emit_l0_attn(p, nc, xn_hbm, wqkv_b, attnT, cst)
    emit_fm_proj_residual(p, nc, attnT, wout_b, xfm, x1fm, cst, "o")
    emit_ln(p, nc, x1fm, xn_hbm, cst, "ln0f")
    emit_l0_ffn(p, nc, x1fm, xn_hbm, wgb16, wub16, wdb16, sg, su, sd, fuse_hbm, x2fm, cst)
    emit_ln(p, nc, x2fm, xn_hbm, cst, "ln1a")
    emit_l1_attn(p, nc, x2fm, xn_hbm, wqb, wkb, wvb, bq, bk, bv, x3fm, cst)
    nw_col = p.sbuf("m_nw", [128, NT], F32)
    p.push_scope()
    pst_sc = p.psum("m_scps", [128, 128])
    pst_sc_b = p.buf()
    nw_b = emit_scale_cols(p, nc, nw, NT, "mnw", cst, pst_sc, pst_sc_b, nw_col)
    p.pop_scope()
    p.barrier("m_nw_end")
    emit_ln(p, nc, x3fm, xn_hbm, cst, "ln1f", nw_col=nw_col, nw_col_buf=nw_b)
    emit_l1_ffn(p, nc, x3fm, xn_hbm, wcb, genes, wdb, wub, temp, scale, w_hbm, out_tm, cst)
    p.emit()
    return nc


def kernel_fused(**inputs):
    x = inputs["L0_attn_x"]
    ncores = 8
    cc = const_inputs(ALL_CONSTS)
    ncf = _get("fused", build_fused)
    maps = []
    for c in range(ncores):
        m = {"x": np.ascontiguousarray(x[c]),
             "w_qkv": inputs["L0_attn_w_qkv"], "w_out": inputs["L0_attn_w_out"],
             "w_gate": inputs["L0_ffn_w_gate"], "w_up": inputs["L0_ffn_w_up"],
             "w_down": inputs["L0_ffn_w_down"], "s_gate": inputs["L0_ffn_s_gate"],
             "s_up": inputs["L0_ffn_s_up"], "s_down": inputs["L0_ffn_s_down"],
             "Wq": inputs["L1_attn_Wq"], "bq": inputs["L1_attn_bq"],
             "Wk": inputs["L1_attn_Wk"], "bk": inputs["L1_attn_bk"],
             "Wv": inputs["L1_attn_Wv"], "bv": inputs["L1_attn_bv"],
             "Wc": inputs["L1_ffn_Wc"],
             "temp": np.asarray(inputs["L1_ffn_temperature"], dtype=np.float32).reshape(1),
             "genes": inputs["L1_ffn_genes"], "Wd": inputs["L1_ffn_Wd"],
             "Wu": inputs["L1_ffn_Wu"], "norm_w": inputs["L1_ffn_norm_w"],
             "scale": np.asarray(inputs["L1_ffn_scale"], dtype=np.float32).reshape(1)}
        m.update(cc)
        maps.append(m)
    r = run_bass_kernel_spmd(ncf, maps, core_ids=list(range(ncores)))
    return np.stack([r.results[c]["out"] for c in range(ncores)], axis=0)


_BUILD_CACHE = {}


def _get(name, builder):
    if name not in _BUILD_CACHE:
        _BUILD_CACHE[name] = builder()
    return _BUILD_CACHE[name]


def kernel(**inputs):
    return kernel_fused(**inputs)


def kernel_multilaunch(**inputs):
    x = inputs["L0_attn_x"]
    ncores = 8
    c1 = const_inputs(L1_CONSTS)
    c2 = const_inputs(L2_CONSTS)
    c3 = const_inputs(L3_CONSTS)
    c4 = const_inputs(L4_CONSTS)

    nc1 = _get("l1", build_launch1)
    maps = []
    for c in range(ncores):
        m = {"x": np.ascontiguousarray(x[c]), "w_qkv": inputs["L0_attn_w_qkv"],
             "w_out": inputs["L0_attn_w_out"]}
        m.update(c1)
        maps.append(m)
    r1 = run_bass_kernel_spmd(nc1, maps, core_ids=list(range(ncores)))

    nc2 = _get("l2", build_launch2)
    maps = []
    for c in range(ncores):
        m = {"x1fm": np.ascontiguousarray(r1.results[c]["x1fm"]),
             "w_gate": inputs["L0_ffn_w_gate"], "w_up": inputs["L0_ffn_w_up"],
             "w_down": inputs["L0_ffn_w_down"], "s_gate": inputs["L0_ffn_s_gate"],
             "s_up": inputs["L0_ffn_s_up"], "s_down": inputs["L0_ffn_s_down"]}
        m.update(c2)
        maps.append(m)
    r2 = run_bass_kernel_spmd(nc2, maps, core_ids=list(range(ncores)))

    nc3 = _get("l3", build_launch3)
    maps = []
    for c in range(ncores):
        m = {"x2fm": np.ascontiguousarray(r2.results[c]["x2fm"]),
             "Wq": inputs["L1_attn_Wq"], "bq": inputs["L1_attn_bq"],
             "Wk": inputs["L1_attn_Wk"], "bk": inputs["L1_attn_bk"],
             "Wv": inputs["L1_attn_Wv"], "bv": inputs["L1_attn_bv"]}
        m.update(c3)
        maps.append(m)
    r3 = run_bass_kernel_spmd(nc3, maps, core_ids=list(range(ncores)))

    nc4 = _get("l4", build_launch4)
    maps = []
    for c in range(ncores):
        m = {"x3fm": np.ascontiguousarray(r3.results[c]["x3fm"]),
             "Wc": inputs["L1_ffn_Wc"],
             "temp": np.asarray(inputs["L1_ffn_temperature"], dtype=np.float32).reshape(1),
             "genes": inputs["L1_ffn_genes"], "Wd": inputs["L1_ffn_Wd"],
             "Wu": inputs["L1_ffn_Wu"], "norm_w": inputs["L1_ffn_norm_w"],
             "scale": np.asarray(inputs["L1_ffn_scale"], dtype=np.float32).reshape(1)}
        m.update(c4)
        maps.append(m)
    r4 = run_bass_kernel_spmd(nc4, maps, core_ids=list(range(ncores)))

    return np.stack([r4.results[c]["out"] for c in range(ncores)], axis=0)


# version 06 (final)
# speedup vs baseline: 1.1436x; 1.0889x over previous
import numpy as np
from contextlib import ExitStack
import concourse.bass as bass
import concourse.mybir as mybir
from concourse.bass_utils import run_bass_kernel_spmd
import ml_dtypes

F32 = mybir.dt.float32
BF16 = mybir.dt.bfloat16
I8 = mybir.dt.int8
AF = mybir.ActivationFunctionType
ALU = mybir.AluOpType
AX = mybir.AxisListType
BF = ml_dtypes.bfloat16

D = 2048
S = 4096
NT = D // 128
ASCALE = float(1.0 / np.sqrt(128.0))

ENGINES = ("tensor", "vector", "scalar", "gpsimd", "sync")
DMA_QUEUES = ("sync", "gpsimd", "scalar", "vector")


class Step:
    __slots__ = ("engine", "fn", "deps", "sem", "val", "is_dma", "n_dma", "tag")

    def __init__(self, engine, fn, deps, is_dma=False, n_dma=1, tag=""):
        self.engine = engine
        self.fn = fn
        self.deps = deps
        self.is_dma = is_dma
        self.n_dma = n_dma
        self.sem = None
        self.val = None
        self.tag = tag


class Buf:
    __slots__ = ("name", "last_w", "readers")

    def __init__(self, name):
        self.name = name
        self.last_w = None
        self.readers = []


class Prog:
    def __init__(self, nc: bass.Bass):
        self.nc = nc
        self.steps = []
        self.es = ExitStack()
        self._scopes = []
        self._sems = {}
        self._counts = {}
        self._barrier = None
        self._nbuf = 0

    def push_scope(self):
        self._scopes.append(ExitStack())

    def pop_scope(self):
        self._scopes.pop().close()

    def _stack(self):
        return self._scopes[-1] if self._scopes else self.es

    def sbuf(self, name, shape, dtype):
        return self._stack().enter_context(self.nc.sbuf_tensor(name, shape, dtype))

    def psum(self, name, shape, dtype=F32):
        return self._stack().enter_context(self.nc.psum_tensor(name, shape, dtype))

    def buf(self, name=None):
        self._nbuf += 1
        return Buf(name or f"b{self._nbuf}")

    SERIAL = False

    def _mkdeps(self, reads, writes, deps):
        dd = set(d for d in deps if d is not None)
        if Prog.SERIAL and self.steps:
            dd.add(self.steps[-1])
        if self._barrier is not None:
            dd.add(self._barrier)
        for b in reads:
            if b.last_w is not None:
                dd.add(b.last_w)
        for b in writes:
            if b.last_w is not None:
                dd.add(b.last_w)
            dd.update(b.readers)
        return dd

    def _post(self, s, reads, writes):
        for b in reads:
            b.readers.append(s)
        for b in writes:
            b.last_w = s
            b.readers = []
        return s

    def step(self, engine, fn, reads=(), writes=(), deps=(), tag=""):
        s = Step(engine, fn, tuple(self._mkdeps(reads, writes, deps)), tag=tag)
        self.steps.append(s)
        return self._post(s, reads, writes)

    def dma(self, queue, fn, reads=(), writes=(), deps=(), n_dma=1, tag=""):
        s = Step(queue, fn, tuple(self._mkdeps(reads, writes, deps)), is_dma=True, n_dma=n_dma, tag=tag)
        self.steps.append(s)
        return self._post(s, reads, writes)

    def barrier(self, tag="bar"):
        lasts = {}
        for s in self.steps:
            lasts[(s.is_dma, s.engine)] = s
        j = Step("sync", lambda e: e.nop(), tuple(set(lasts.values())), tag=tag)
        self.steps.append(j)
        self._barrier = j
        return j

    def emit(self):
        nc = self.nc
        for e in ENGINES:
            self._sems[("c", e)] = self.es.enter_context(nc.semaphore(f"sem_{e}"))
            self._counts[("c", e)] = 0
        for q in DMA_QUEUES:
            self._sems[("d", q)] = self.es.enter_context(nc.semaphore(f"dsem_{q}"))
            self._counts[("d", q)] = 0
        for s in self.steps:
            key = ("d", s.engine) if s.is_dma else ("c", s.engine)
            s.sem = self._sems[key]
            inc = 16 * s.n_dma if s.is_dma else 1
            self._counts[key] += inc
            s.val = self._counts[key]
        block = self.es.enter_context(nc.Block())
        by_engine = {e: [] for e in ENGINES}
        for s in self.steps:
            by_engine[s.engine].append(s)

        def make_body(steps):
            def body(eng):
                done = {}
                for s in steps:
                    waits = {}
                    if s.is_dma:
                        prev = s.val - 16 * s.n_dma
                        k = id(s.sem)
                        if prev > 0 and done.get(k, -1) < prev:
                            waits[k] = (s.sem, prev)
                    for d in s.deps:
                        k = id(d.sem)
                        if done.get(k, -1) >= d.val:
                            continue
                        if k not in waits or waits[k][1] < d.val:
                            waits[k] = (d.sem, d.val)
                    for (sem, val) in waits.values():
                        eng.wait_ge(sem, val)
                        done[id(sem)] = val
                    if s.is_dma:
                        s.fn(eng, s.sem)
                    else:
                        inst = s.fn(eng)
                        assert inst is not None, f"step {s.tag} returned no instruction"
                        inst.then_inc(s.sem, 1)
            return body

        for e in ENGINES:
            if by_engine[e]:
                getattr(block, e)(make_body(by_engine[e]))



def make_consts():
    c = {}
    c["IDF"] = np.eye(128, dtype=np.float32)
    c["ONESCOL"] = np.ones((128, 1), dtype=np.float32).astype(BF)
    c["ONESROW"] = np.ones((1, 128), dtype=np.float32)
    tri = np.arange(128)
    c["TRI_LO"] = (tri[None, :] <= tri[:, None]).astype(np.float32).astype(BF)
    c["TRI_HI"] = (tri[None, :] >= tri[:, None]).astype(np.float32).astype(BF)
    i32 = np.arange(128)
    blk = i32 // 32
    same = blk[:, None] == blk[None, :]
    c["GLA_L"] = (same & (i32[:, None] <= i32[None, :])).astype(np.float32)
    c["GLA_R"] = (same & (i32[:, None] > i32[None, :])).astype(np.float32)
    t32 = np.arange(32)
    c["GLA_PMASK"] = (t32[:, None] <= np.tile(t32, 4)[None, :]).astype(np.float32)
    return c


CONSTS = make_consts()


def declare_consts(nc, p, names):
    out = {}
    loads = []
    for n in names:
        arr = CONSTS[n]
        dt = {np.dtype(np.float32): F32, np.dtype(BF): BF16}[arr.dtype]
        ap = nc.dram_tensor("c_" + n, list(arr.shape), dt, kind="ExternalInput").ap()
        sb = p.sbuf("cs_" + n, list(arr.shape), dt)
        out[n] = sb
        loads.append((sb, ap))

    def fn(e, s):
        for sb, ap in loads:
            e.dma_start(out=sb[:], in_=ap).then_inc(s, 16)
    out["_step"] = p.dma("sync", fn, n_dma=len(loads), tag="consts")
    return out


def const_inputs(names):
    return {"c_" + n: np.ascontiguousarray(CONSTS[n]) for n in names}


def fm_view(hbm_ap):
    return hbm_ap.rearrange("(j p) t -> p j t", p=128)



def emit_in_transpose(p, nc, x_in, xfm, cst):
    p.push_scope()
    xt = [p.sbuf(f"it_x{i}", [128, D], F32) for i in range(2)]
    ot = [p.sbuf(f"it_o{i}", [128, D], F32) for i in range(2)]
    pst = [p.psum(f"it_ps{i}", [128, 512]) for i in range(2)]
    xt_b = [p.buf() for _ in range(2)]
    ot_b = [p.buf() for _ in range(2)]
    ps_b = [p.buf() for _ in range(2)]
    for i in range(S // 128):
        bi = i % 2
        p.dma("sync", (lambda xsb, src: lambda e, s: e.dma_start(out=xsb[:], in_=src).then_inc(s, 16))(
            xt[bi], x_in[i * 128:(i + 1) * 128, :]), writes=[xt_b[bi]], deps=[cst["_step"]], tag=f"it_ld{i}")
        for q in range(4):
            bq = q % 2
            def trf(e, xsb=xt[bi], ps=pst[bq], q=q, idf=cst["IDF"]):
                for j in range(4):
                    et = 4 * q + j
                    inst = e.transpose(ps[:, 128 * j:128 * (j + 1)], xsb[:, 128 * et:128 * et + 128], idf[:])
                return inst
            p.step("tensor", trf, reads=[xt_b[bi]], writes=[ps_b[bq]], tag=f"it_tr{i}_{q}")
            if q % 2 == 0:
                evf = (lambda ps, osb, q: lambda e: e.copy(osb[:, 512 * q:512 * (q + 1)], ps[:]))(pst[bq], ot[bi], q)
                p.step("scalar", evf, reads=[ps_b[bq]], writes=[ot_b[bi]], tag=f"it_ev{i}_{q}")
            else:
                evf = (lambda ps, osb, q: lambda e: e.tensor_copy(osb[:, 512 * q:512 * (q + 1)], ps[:]))(pst[bq], ot[bi], q)
                p.step("vector", evf, reads=[ps_b[bq]], writes=[ot_b[bi]], tag=f"it_ev{i}_{q}")
        p.dma("sync", (lambda osb, i: lambda e, s: e.dma_start(
            out=fm_view(xfm)[:, :, 128 * i:128 * (i + 1)],
            in_=osb[:].rearrange("p (j t) -> p j t", t=128)).then_inc(s, 16))(ot[bi], i),
            reads=[ot_b[bi]], tag=f"it_st{i}")
    p.pop_scope()
    return p.barrier("it_end")



def emit_ln(p, nc, xfm, xn_dst, cst, tag, nw_col=None, nw_col_buf=None):
    CH = 512
    nch = S // CH
    p.push_scope()
    xc = [p.sbuf(f"{tag}_x{i}", [128, NT * CH], F32) for i in range(2)]
    xbB = [p.sbuf(f"{tag}_xb{i}", [128, NT * CH], BF16) for i in range(2)]
    xqB = [p.sbuf(f"{tag}_xq{i}", [128, NT * CH], BF16) for i in range(2)]
    xnc = [p.sbuf(f"{tag}_xn{i}", [128, NT * CH], BF16) for i in range(2)]
    tmp = [p.sbuf(f"{tag}_tmp{i}", [128, CH], F32) for i in range(4)]
    rowB = [p.sbuf(f"{tag}_rows{i}", [1, 6 * CH], F32) for i in range(2)]
    b1sb = [p.sbuf(f"{tag}_b1{i}", [128, CH], F32) for i in range(2)]
    b2sb = [p.sbuf(f"{tag}_b2{i}", [128, CH], F32) for i in range(2)]
    ps_stB = [p.psum(f"{tag}_ps{i}", [1, 2 * CH]) for i in range(2)]
    ps_bc = [p.psum(f"{tag}_pbc{i}", [128, CH]) for i in range(2)]
    B = dict(xc=[p.buf(), p.buf()], xb=[p.buf(), p.buf()], xq=[p.buf(), p.buf()],
             xn=[p.buf(), p.buf()], tmp=[p.buf() for _ in range(4)],
             rows=[p.buf(), p.buf()], ps=[p.buf(), p.buf()],
             pbc=[p.buf(), p.buf()], b1=[p.buf(), p.buf()], b2=[p.buf(), p.buf()])
    for c in range(nch):
        bi = c % 2
        xb = xbB[bi]; xq = xqB[bi]; rowbuf = rowB[bi]; ps_st = ps_stB[bi]
        r_m = rowbuf[:, 0:CH]; r_sq = rowbuf[:, CH:2 * CH]; r_var = rowbuf[:, 2 * CH:3 * CH]
        r_rs = rowbuf[:, 3 * CH:4 * CH]; r_mrs = rowbuf[:, 4 * CH:5 * CH]; r_t = rowbuf[:, 5 * CH:6 * CH]
        Brows = B["rows"][bi]; Bps = B["ps"][bi]; Bxb = B["xb"][bi]; Bxq = B["xq"][bi]
        p.dma("sync", (lambda dst, c: lambda e, s: e.dma_start(
            out=dst[:].rearrange("p (j t) -> p j t", t=CH),
            in_=fm_view(xfm)[:, :, CH * c:CH * (c + 1)]).then_inc(s, 16))(xc[bi], c),
            writes=[B["xc"][bi]], tag=f"{tag}_ld{c}")
        p.step("vector", (lambda src, xb=xb: lambda e: e.tensor_copy(xb[:], src[:]))(xc[bi]),
               reads=[B["xc"][bi]], writes=[Bxb], tag=f"{tag}_cast{c}")
        p.step("scalar", (lambda xb=xb, xq=xq: lambda e: e.activation(xq[:], xb[:], AF.Square))(),
               reads=[Bxb], writes=[Bxq], tag=f"{tag}_sq{c}")

        def stats(e, xb=xb, xq=xq, ps_st=ps_st):
            for j in range(NT):
                e.matmul(ps_st[:, 0:CH], cst["ONESCOL"][:], xb[:, CH * j:CH * (j + 1)],
                         start=(j == 0), stop=(j == NT - 1))
            for j in range(NT):
                inst = e.matmul(ps_st[:, CH:2 * CH], cst["ONESCOL"][:], xq[:, CH * j:CH * (j + 1)],
                                start=(j == 0), stop=(j == NT - 1))
            return inst
        p.step("tensor", stats, reads=[Bxb, Bxq], writes=[Bps], tag=f"{tag}_stat{c}")
        p.step("vector", (lambda r_m=r_m, ps_st=ps_st: lambda e: e.tensor_scalar_mul(r_m, ps_st[:, 0:CH], 1.0 / D))(),
               reads=[Bps], writes=[Brows])
        p.step("vector", (lambda r_sq=r_sq, ps_st=ps_st: lambda e: e.tensor_scalar_mul(r_sq, ps_st[:, CH:2 * CH], 1.0 / D))(),
               reads=[Bps], writes=[Brows])
        p.step("vector", (lambda r_t=r_t, r_m=r_m: lambda e: e.tensor_mul(r_t, r_m, r_m))(),
               reads=[Brows], writes=[Brows])
        p.step("vector", (lambda r_var=r_var, r_sq=r_sq, r_t=r_t: lambda e: e.tensor_sub(r_var, r_sq, r_t))(),
               reads=[Brows], writes=[Brows])
        p.step("vector", (lambda r_t=r_t, r_var=r_var: lambda e: e.tensor_scalar_add(r_t, r_var, 1e-5))(),
               reads=[Brows], writes=[Brows])
        p.step("scalar", (lambda r_t=r_t: lambda e: e.activation(r_t, r_t, AF.Ln))(),
               reads=[Brows], writes=[Brows])
        if nw_col is None:
            p.step("scalar", (lambda r_rs=r_rs, r_t=r_t: lambda e: e.activation(r_rs, r_t, AF.Exp, scale=-0.5))(),
                   reads=[Brows], writes=[Brows])
        else:
            p.step("scalar", (lambda r_mrs=r_mrs, r_t=r_t: lambda e: e.activation(r_mrs, r_t, AF.Exp, scale=-1.0))(),
                   reads=[Brows], writes=[Brows])
            p.step("vector", (lambda r_mrs=r_mrs, r_var=r_var: lambda e: e.tensor_mul(r_mrs, r_var, r_mrs))(),
                   reads=[Brows], writes=[Brows])
            p.step("vector", (lambda r_mrs=r_mrs: lambda e: e.tensor_scalar_add(r_mrs, r_mrs, 1.1920929e-07))(),
                   reads=[Brows], writes=[Brows])
            p.step("scalar", (lambda r_mrs=r_mrs: lambda e: e.activation(r_mrs, r_mrs, AF.Ln))(),
                   reads=[Brows], writes=[Brows])
            p.step("vector", (lambda r_t=r_t, r_mrs=r_mrs: lambda e: e.tensor_add(r_t, r_t, r_mrs))(),
                   reads=[Brows], writes=[Brows])
            p.step("scalar", (lambda r_rs=r_rs, r_t=r_t: lambda e: e.activation(r_rs, r_t, AF.Exp, scale=-0.5))(),
                   reads=[Brows], writes=[Brows])
        p.step("vector", (lambda r_mrs=r_mrs, r_m=r_m, r_rs=r_rs: lambda e: e.tensor_mul(r_mrs, r_m, r_rs))(),
               reads=[Brows], writes=[Brows])
        bcA = p.step("tensor", (lambda bi, r_rs=r_rs: lambda e: e.matmul(
            ps_bc[bi][:], cst["ONESROW"][0:1, :], r_rs, start=True, stop=True))(bi),
            reads=[Brows], writes=[B["pbc"][bi]], tag=f"{tag}_bcA{c}")
        cpA = p.step("vector", (lambda bi: lambda e: e.tensor_copy(b1sb[bi][:], ps_bc[bi][:]))(bi),
                     reads=[B["pbc"][bi]], writes=[B["b1"][bi]])
        bcB = p.step("tensor", (lambda bi, r_mrs=r_mrs: lambda e: e.matmul(
            ps_bc[bi][:], cst["ONESROW"][0:1, :], r_mrs, start=True, stop=True))(bi),
            reads=[Brows], writes=[B["pbc"][bi]], deps=[cpA], tag=f"{tag}_bcB{c}")
        cpB = p.step("vector", (lambda bi: lambda e: e.tensor_copy(b2sb[bi][:], ps_bc[bi][:]))(bi),
                     reads=[B["pbc"][bi]], writes=[B["b2"][bi]])
        for j in range(NT):
            tb = j % 4
            p.step("gpsimd", (lambda bi, j, tb: lambda e: e.tensor_mul(
                tmp[tb][:], xc[bi][:, CH * j:CH * (j + 1)], b1sb[bi][:]))(bi, j, tb),
                reads=[B["xc"][bi], B["b1"][bi]], writes=[B["tmp"][tb]], tag=f"{tag}_t1_{c}_{j}")
            if nw_col is None:
                p.step("vector", (lambda bi, j, tb: lambda e: e.tensor_sub(
                    xnc[bi][:, CH * j:CH * (j + 1)], tmp[tb][:], b2sb[bi][:]))(bi, j, tb),
                    reads=[B["tmp"][tb], B["b2"][bi]], writes=[B["xn"][bi]], tag=f"{tag}_t2_{c}_{j}")
            else:
                p.step("vector", (lambda bi, j, tb: lambda e: e.tensor_sub(
                    tmp[tb][:], tmp[tb][:], b2sb[bi][:]))(bi, j, tb),
                    reads=[B["tmp"][tb], B["b2"][bi]], writes=[B["tmp"][tb]])
                p.step("vector", (lambda bi, j, tb: lambda e: e.tensor_scalar_mul(
                    xnc[bi][:, CH * j:CH * (j + 1)], tmp[tb][:], nw_col[:, j:j + 1]))(bi, j, tb),
                    reads=[B["tmp"][tb], nw_col_buf], writes=[B["xn"][bi]], tag=f"{tag}_t3_{c}_{j}")
        p.dma("sync", (lambda src, c: lambda e, s: e.dma_start(
            out=fm_view(xn_dst)[:, :, CH * c:CH * (c + 1)],
            in_=src[:].rearrange("p (j t) -> p j t", t=CH)).then_inc(s, 16))(xnc[bi], c),
            reads=[B["xn"][bi]], tag=f"{tag}_st{c}")
    p.pop_scope()
    return p.barrier(f"{tag}_end")



def band_sched(chunk0):
    out = []
    if chunk0:
        for k in range(4):
            out.append((k, 128 * k, 512, k, None))
    else:
        for k in range(8):
            qlo = max(0, 128 * (k - 4))
            qhi = min(512, 128 * (k + 1))
            lo = (k - 4) if k >= 4 else None
            hi = k if k < 4 else None
            out.append((k, qlo, qhi, lo, hi))
    return out


def emit_l0_attn(p, nc, xn_hbm, wqkv_b, attnT_hbm, cst):
    CH = 512
    nch = S // CH
    p.push_scope()
    wq = [p.sbuf(f"a_wq{i}", [128, NT * CH], BF16) for i in range(2)]
    wk = [p.sbuf(f"a_wk{i}", [128, NT * CH], BF16) for i in range(2)]
    wv = [p.sbuf(f"a_wv{i}", [128, NT * CH], BF16) for i in range(2)]
    wq_b = [p.buf(), p.buf()]
    wk_b = [p.buf(), p.buf()]
    wv_b = [p.buf(), p.buf()]
    xnc = [p.sbuf(f"a_xn{i}", [128, NT * CH], BF16) for i in range(2)]
    xnc_b = [p.buf(), p.buf()]
    qT = p.sbuf("a_qT", [128, 4 * CH], BF16)
    kT = [p.sbuf(f"a_kT{i}", [128, 4 * CH], BF16) for i in range(2)]
    vv = [p.sbuf(f"a_v{i}", [128, 4 * CH], BF16) for i in range(2)]
    qT_b = p.buf()
    kT_b = [p.buf(), p.buf()]
    vv_b = [p.buf(), p.buf()]
    pT2 = [p.sbuf(f"a_pT{i}", [128, 8 * CH], BF16) for i in range(2)]
    pT2_b = [p.buf(), p.buf()]
    rz2 = [p.sbuf(f"a_rz{i}", [1, CH], F32) for i in range(2)]
    rzb2 = [p.sbuf(f"a_rzb{i}", [128, CH], F32) for i in range(2)]
    rz2_b = [p.buf(), p.buf()]
    rzb2_b = [p.buf(), p.buf()]
    att2 = [p.sbuf(f"a_att{i}", [128, CH], BF16) for i in range(2)]
    att2_b = [p.buf(), p.buf()]
    ps_q = p.psum("a_psq", [128, 4 * CH])
    ps_s = p.psum("a_pss", [128, 2 * CH])
    ps_q_b = [p.buf() for _ in range(4)]
    ps_s_b = [p.buf(), p.buf()]
    ps_pv = p.psum("a_pspv", [128, CH])
    ps_pv_b = p.buf()
    ps_z = p.psum("a_psz", [1, CH])
    ps_z_b = p.buf()
    ps_bc = ps_pv
    ps_bc_b = ps_pv_b

    for g in range(4):
        gw = g % 2
        def wload(e, s, g=g, gw=gw):
            for et in range(NT):
                e.dma_start(out=wq[gw][:, CH * et:CH * (et + 1)],
                            in_=wqkv_b[512 * g:512 * (g + 1), 128 * et:128 * (et + 1)], transpose=True).then_inc(s, 16)
                e.dma_start(out=wk[gw][:, CH * et:CH * (et + 1)],
                            in_=wqkv_b[2048 + 512 * g:2048 + 512 * (g + 1), 128 * et:128 * (et + 1)], transpose=True).then_inc(s, 16)
                e.dma_start(out=wv[gw][:, CH * et:CH * (et + 1)],
                            in_=wqkv_b[4096 + 512 * g:4096 + 512 * (g + 1), 128 * et:128 * (et + 1)], transpose=True).then_inc(s, 16)
        p.dma("sync", wload, writes=[wq_b[gw], wk_b[gw], wv_b[gw]], n_dma=3 * NT, tag=f"a_wl{g}")
        for c in range(nch):
            bi = c % 2
            cur, prv = c % 2, 1 - c % 2
            p.dma("gpsimd", (lambda dst, c: lambda e, s: e.dma_start(
                out=dst[:].rearrange("p (j t) -> p j t", t=CH),
                in_=fm_view(xn_hbm)[:, :, CH * c:CH * (c + 1)]).then_inc(s, 16))(xnc[bi], c),
                writes=[xnc_b[bi]], tag=f"a_ldx{g}_{c}")

            for ft in range(4):
                def qmmf(e, ft=ft, bi=bi, gw=gw):
                    for et in range(NT):
                        inst = e.matmul(ps_q[:, CH * ft:CH * (ft + 1)],
                                        wq[gw][:, CH * et + 128 * ft:CH * et + 128 * (ft + 1)],
                                        xnc[bi][:, CH * et:CH * (et + 1)],
                                        start=(et == 0), stop=(et == NT - 1))
                    return inst
                p.step("tensor", qmmf, reads=[wq_b[gw], xnc_b[bi]], writes=[ps_q_b[ft]], tag=f"a_qmm{g}_{c}_{ft}")
                p.step("scalar", (lambda ft: lambda e: e.mul(qT[:, CH * ft:CH * (ft + 1)], ps_q[:, CH * ft:CH * (ft + 1)], ASCALE))(ft),
                       reads=[ps_q_b[ft]], writes=[qT_b], tag=f"a_qev{g}_{c}_{ft}")
            for ft in range(4):
                def kmmf(e, ft=ft, bi=bi, gw=gw):
                    for et in range(NT):
                        inst = e.matmul(ps_q[:, CH * ft:CH * (ft + 1)],
                                        wk[gw][:, CH * et + 128 * ft:CH * et + 128 * (ft + 1)],
                                        xnc[bi][:, CH * et:CH * (et + 1)],
                                        start=(et == 0), stop=(et == NT - 1))
                    return inst
                p.step("tensor", kmmf, reads=[wk_b[gw], xnc_b[bi]], writes=[ps_q_b[ft]], tag=f"a_kmm{g}_{c}_{ft}")
                p.step("scalar", (lambda ft, cur=cur: lambda e: e.copy(kT[cur][:, CH * ft:CH * (ft + 1)], ps_q[:, CH * ft:CH * (ft + 1)]))(ft),
                       reads=[ps_q_b[ft]], writes=[kT_b[cur]], tag=f"a_kev{g}_{c}_{ft}")
            for tt in range(4):
                def vmmf(e, tt=tt, bi=bi, gw=gw):
                    for et in range(NT):
                        inst = e.matmul(ps_q[:, CH * tt:CH * (tt + 1)],
                                        xnc[bi][:, CH * et + 128 * tt:CH * et + 128 * (tt + 1)],
                                        wv[gw][:, CH * et:CH * (et + 1)],
                                        start=(et == 0), stop=(et == NT - 1))
                    return inst
                p.step("tensor", vmmf, reads=[wv_b[gw], xnc_b[bi]], writes=[ps_q_b[tt]], tag=f"a_vmm{g}_{c}_{tt}")
                p.step("scalar", (lambda tt, cur=cur: lambda e: e.copy(vv[cur][:, CH * tt:CH * (tt + 1)], ps_q[:, CH * tt:CH * (tt + 1)]))(tt),
                       reads=[ps_q_b[tt]], writes=[vv_b[cur]], tag=f"a_vev{g}_{c}_{tt}")
            sched = band_sched(c == 0)
            kread = [kT_b[cur]] if c == 0 else [kT_b[cur], kT_b[prv]]
            vread = [vv_b[cur]] if c == 0 else [vv_b[cur], vv_b[prv]]
            for h in range(4):
                hp = h % 2
                pT = pT2[hp]; pT_b = pT2_b[hp]
                rz = rz2[hp]; rz_b = rz2_b[hp]
                rzb = rzb2[hp]; rzb_b = rzb2_b[hp]
                att = att2[hp]; att_b = att2_b[hp]
                qs = qT[:, CH * h:CH * (h + 1)]
                for it, (k, qlo, qhi, lo, hi) in enumerate(sched):
                    pb = it % 2
                    if c == 0:
                        ksrc, ktile = kT[cur], k
                    else:
                        ksrc, ktile = (kT[prv], k) if k < 4 else (kT[cur], k - 4)
                    stat = ksrc[:, CH * h + 128 * ktile:CH * h + 128 * (ktile + 1)]
                    p.step("tensor", (lambda stat, qs, pb, qlo, qhi: lambda e: e.matmul(
                        ps_s[:, CH * pb + qlo:CH * pb + qhi], stat, qs[:, qlo:qhi], start=True, stop=True))(stat, qs, pb, qlo, qhi),
                        reads=kread + [qT_b], writes=[ps_s_b[pb]], tag=f"a_smm{g}_{c}_{h}_{it}")
                    p.step("scalar", (lambda pb, it, qlo, qhi: lambda e: e.activation(
                        pT[:, CH * it + qlo:CH * it + qhi], ps_s[:, CH * pb + qlo:CH * pb + qhi], AF.Exp))(pb, it, qlo, qhi),
                        reads=[ps_s_b[pb]], writes=[pT_b], tag=f"a_sev{g}_{c}_{h}_{it}")
                    if lo is not None:
                        p.step("vector", (lambda it, lo: lambda e: e.tensor_mul(
                            pT[:, CH * it + 128 * lo:CH * it + 128 * (lo + 1)],
                            pT[:, CH * it + 128 * lo:CH * it + 128 * (lo + 1)], cst["TRI_HI"][:]))(it, lo),
                            reads=[pT_b], writes=[pT_b], tag=f"a_mlo{g}_{c}_{h}_{it}")
                    if hi is not None:
                        p.step("vector", (lambda it, hi: lambda e: e.tensor_mul(
                            pT[:, CH * it + 128 * hi:CH * it + 128 * (hi + 1)],
                            pT[:, CH * it + 128 * hi:CH * it + 128 * (hi + 1)], cst["TRI_LO"][:]))(it, hi),
                            reads=[pT_b], writes=[pT_b], tag=f"a_mhi{g}_{c}_{h}_{it}")

                def zmm(e, sched=sched):
                    for it, (k, qlo, qhi, lo, hi) in enumerate(sched):
                        inst = e.matmul(ps_z[0:1, qlo:qhi], cst["ONESCOL"][:],
                                        pT[:, CH * it + qlo:CH * it + qhi],
                                        start=(it == 0), stop=(it == len(sched) - 1))
                    return inst
                p.step("tensor", zmm, reads=[pT_b], writes=[ps_z_b], tag=f"a_zmm{g}_{c}_{h}")
                p.step("vector", lambda e: e.reciprocal(rz[:], ps_z[:]), reads=[ps_z_b], writes=[rz_b])
                p.step("tensor", lambda e: e.matmul(ps_bc[:], cst["ONESROW"][0:1, :], rz[:], start=True, stop=True),
                       reads=[rz_b], writes=[ps_bc_b])
                p.step("vector", lambda e: e.tensor_copy(rzb[:], ps_bc[:]), reads=[ps_bc_b], writes=[rzb_b])

                def pvmm(e, sched=sched, h=h, cur=cur, prv=prv, c=c):
                    for it, (k, qlo, qhi, lo, hi) in enumerate(sched):
                        if c == 0:
                            vsrc, vtile = vv[cur], k
                        else:
                            vsrc, vtile = (vv[prv], k) if k < 4 else (vv[cur], k - 4)
                        stat = vsrc[:, CH * vtile + 128 * h:CH * vtile + 128 * (h + 1)]
                        inst = e.matmul(ps_pv[:, qlo:qhi], stat,
                                        pT[:, CH * it + qlo:CH * it + qhi],
                                        start=(it == 0), stop=(it == len(sched) - 1))
                    return inst
                p.step("tensor", pvmm, reads=[pT_b] + vread, writes=[ps_pv_b], tag=f"a_pv{g}_{c}_{h}")
                p.step("vector", lambda e: e.tensor_mul(att[:], ps_pv[:], rzb[:]),
                       reads=[ps_pv_b, rzb_b], writes=[att_b], tag=f"a_nrm{g}_{c}_{h}")
                hg = 4 * g + h
                p.dma("sync", (lambda hg, c: lambda e, s: e.dma_start(
                    out=attnT_hbm[128 * hg:128 * (hg + 1), CH * c:CH * (c + 1)], in_=att[:]).then_inc(s, 16))(hg, c),
                    reads=[att_b], tag=f"a_stA{g}_{c}_{h}")
    p.pop_scope()
    return p.barrier("a_end")



def emit_fm_proj_residual(p, nc, src_hbm, w_b, resid_hbm, out_hbm, cst, tag):
    CH = 512
    nch = S // CH
    p.push_scope()
    wo = p.sbuf(f"{tag}_wo", [128, NT * D], BF16)
    wo_b = p.buf()
    atc = [p.sbuf(f"{tag}_at{i}", [128, NT * CH], BF16) for i in range(2)]
    atc_b = [p.buf(), p.buf()]
    xc = [p.sbuf(f"{tag}_x{i}", [128, NT * CH], F32) for i in range(2)]
    xc_b = [p.buf(), p.buf()]
    oc = [p.sbuf(f"{tag}_o{i}", [128, CH], F32) for i in range(4)]
    oc_b = [p.buf() for _ in range(4)]
    ps = p.psum(f"{tag}_ps", [128, 2 * CH])
    ps_b = [p.buf(), p.buf()]

    def wload(e, s):
        for et in range(NT):
            e.dma_start(out=wo[:, D * et:D * (et + 1)],
                        in_=w_b[:, 128 * et:128 * (et + 1)], transpose=True).then_inc(s, 16)
    p.dma("sync", wload, writes=[wo_b], n_dma=NT, tag=f"{tag}_wl")
    for c in range(nch):
        bi = c % 2
        p.dma("gpsimd", (lambda dst, c: lambda e, s: e.dma_start(
            out=dst[:].rearrange("p (j t) -> p j t", t=CH),
            in_=fm_view(src_hbm)[:, :, CH * c:CH * (c + 1)]).then_inc(s, 16))(atc[bi], c),
            writes=[atc_b[bi]], tag=f"{tag}_lda{c}")
        p.dma("sync", (lambda dst, c: lambda e, s: e.dma_start(
            out=dst[:].rearrange("p (j t) -> p j t", t=CH),
            in_=fm_view(resid_hbm)[:, :, CH * c:CH * (c + 1)]).then_inc(s, 16))(xc[bi], c),
            writes=[xc_b[bi]], tag=f"{tag}_ldx{c}")
        for ft in range(NT):
            pb = ft % 2
            ob = ft % 4

            def omm(e, bi=bi, ft=ft, pb=pb):
                for et in range(NT):
                    inst = e.matmul(ps[:, CH * pb:CH * (pb + 1)],
                                    wo[:, D * et + 128 * ft:D * et + 128 * (ft + 1)],
                                    atc[bi][:, CH * et:CH * (et + 1)],
                                    start=(et == 0), stop=(et == NT - 1))
                return inst
            p.step("tensor", omm, reads=[wo_b, atc_b[bi]], writes=[ps_b[pb]], tag=f"{tag}_mm{c}_{ft}")
            p.step("vector", (lambda bi, ft, pb, ob: lambda e: e.tensor_add(
                oc[ob][:], ps[:, CH * pb:CH * (pb + 1)], xc[bi][:, CH * ft:CH * (ft + 1)]))(bi, ft, pb, ob),
                reads=[ps_b[pb], xc_b[bi]], writes=[oc_b[ob]], tag=f"{tag}_ev{c}_{ft}")
            p.dma("sync", (lambda ft, c, ob: lambda e, s: e.dma_start(
                out=out_hbm[128 * ft:128 * (ft + 1), CH * c:CH * (c + 1)], in_=oc[ob][:]).then_inc(s, 16))(ft, c, ob),
                reads=[oc_b[ob]], tag=f"{tag}_st{c}_{ft}")
    p.pop_scope()
    return p.barrier(f"{tag}_end")



L1_CONSTS = ["IDF", "ONESCOL", "ONESROW", "TRI_LO", "TRI_HI"]


def build_launch1(dbg=False):
    nc = bass.Bass("TRN2", target_bir_lowering=False, debug=False)
    x_in = nc.dram_tensor("x", [S, D], F32, kind="ExternalInput").ap()
    wqkv = nc.dram_tensor("w_qkv", [3 * D, D], F32, kind="ExternalInput").ap()
    wout = nc.dram_tensor("w_out", [D, D], F32, kind="ExternalInput").ap()
    xfm_out = nc.dram_tensor("xfm", [D, S], F32, kind="ExternalOutput").ap()
    x1fm = nc.dram_tensor("x1fm", [D, S], F32, kind="ExternalOutput").ap()
    knd = "ExternalOutput" if dbg else "Internal"
    xn_hbm = nc.dram_tensor("xn_s", [D, S], BF16, kind=knd).ap()
    wqkv_b = nc.dram_tensor("wqkv_b", [3 * D, D], BF16, kind="Internal").ap()
    wout_b = nc.dram_tensor("wout_b", [D, D], BF16, kind="Internal").ap()
    attnT = nc.dram_tensor("attnT_s", [D, S], BF16, kind=knd).ap()

    p = Prog(nc)
    cst = declare_consts(nc, p, L1_CONSTS)
    p.dma("gpsimd", lambda e, s: e.dma_start(out=wqkv_b, in_=wqkv).then_inc(s, 16), n_dma=1, tag="wc1")
    p.dma("gpsimd", lambda e, s: e.dma_start(out=wout_b, in_=wout).then_inc(s, 16), n_dma=1, tag="wc2")
    emit_in_transpose(p, nc, x_in, xfm_out, cst)
    emit_ln(p, nc, xfm_out, xn_hbm, cst, "ln0a")
    emit_l0_attn(p, nc, xn_hbm, wqkv_b, attnT, cst)
    emit_fm_proj_residual(p, nc, attnT, wout_b, xfm_out, x1fm, cst, "o")
    p.emit()
    return nc




def emit_scale_cols(p, nc, vec_ap, n, tag, cst, pst, pst_b, col):
    tmp = p.sbuf(f"{tag}_r", [max(n, 1), 128], F32)
    tb, cb = p.buf(), p.buf()
    p.dma("sync", lambda e, s: e.dma_start(out=tmp[:], in_=vec_ap.rearrange("(i p) -> i p", p=128)).then_inc(s, 16),
          writes=[tb], tag=f"{tag}_ld")
    p.step("tensor", lambda e: e.transpose(pst[:, 0:n], tmp[:], cst["IDF"][0:max(n, 1), 0:max(n, 1)]),
           reads=[tb], writes=[pst_b], tag=f"{tag}_tr")
    p.step("vector", lambda e: e.tensor_copy(col[:], pst[:, 0:n]), reads=[pst_b], writes=[cb], tag=f"{tag}_cp")
    return cb


def emit_l0_ffn(p, nc, x1fm, xn_hbm, wgb16, wub16, wdb16, sg_ap, su_ap, sd_ap, fuse_hbm, x2fm, cst, gate_dbg=None):
    INTER = 8192
    NO = INTER // 128
    p.push_scope()
    sg_col = p.sbuf("sg_c", [128, 64], F32)
    su_col = p.sbuf("su_c", [128, 64], F32)
    sd_col = p.sbuf("sd_c", [128, 16], F32)
    p.push_scope()
    pst_sc = p.psum("sc_ps", [128, 128])
    pst_sc_b = p.buf()
    sg_b = emit_scale_cols(p, nc, sg_ap, 64, "sg", cst, pst_sc, pst_sc_b, sg_col)
    su_b = emit_scale_cols(p, nc, su_ap, 64, "su", cst, pst_sc, pst_sc_b, su_col)
    sd_b = emit_scale_cols(p, nc, sd_ap, 16, "sd", cst, pst_sc, pst_sc_b, sd_col)
    p.pop_scope()
    p.barrier("sc_end")

    p.push_scope()
    xn = p.sbuf("f_xn", [128, NT * S], BF16)
    xn_b = p.buf()
    p.dma("sync", lambda e, s: e.dma_start(
        out=xn[:].rearrange("p (j t) -> p j t", t=S), in_=fm_view(xn_hbm)).then_inc(s, 16),
        writes=[xn_b], tag="f_ldxn")
    wg = [p.sbuf("f_wg0", [128, NT * 512], BF16)] * 2
    wu = [p.sbuf("f_wu0", [128, NT * 512], BF16)] * 2
    _wgb = p.buf(); _wub = p.buf()
    wg_b = [_wgb, _wgb]
    wu_b = [_wub, _wub]
    gsb = p.sbuf("f_gate", [128, S], F32)
    gsb_b = p.buf()
    fsb = [p.sbuf(f"f_fuse{i}", [128, S], BF16) for i in range(2)]
    fsb_b = [p.buf(), p.buf()]
    ps = [p.psum(f"f_ps{i}", [128, 2048]) for i in range(2)]
    ps_b = [p.buf(), p.buf()]
    for og in range(16):
        gi = og % 2
        def wload(e, s, og=og, gi=gi):
            e.dma_start(out=wg[gi][:].rearrange("p (j o) -> p j o", o=512),
                        in_=wgb16.rearrange("(j p) o -> p j o", p=128)[:, :, 512 * og:512 * (og + 1)]).then_inc(s, 16)
            e.dma_start(out=wu[gi][:].rearrange("p (j o) -> p j o", o=512),
                        in_=wub16.rearrange("(j p) o -> p j o", p=128)[:, :, 512 * og:512 * (og + 1)]).then_inc(s, 16)
        p.dma("gpsimd", wload, writes=[wg_b[gi], wu_b[gi]], n_dma=2, tag=f"f_wl{og}")
        for ot in range(4):
            o = 4 * og + ot
            for half in range(2):
                pb = half
                def gmm(e, gi=gi, ot=ot, half=half, pb=pb):
                    for h in range(NT):
                        for q in range(4):
                            inst = e.matmul(ps[pb][:, 512 * q:512 * (q + 1)],
                                            wg[gi][:, 512 * h + 128 * ot:512 * h + 128 * (ot + 1)],
                                            xn[:, S * h + 2048 * half + 512 * q:S * h + 2048 * half + 512 * (q + 1)],
                                            start=(h == 0), stop=(h == NT - 1))
                    return inst
                p.step("tensor", gmm, reads=[wg_b[gi], xn_b], writes=[ps_b[pb]], tag=f"f_gmm{o}_{half}")
                p.step("scalar", (lambda o, half, pb: lambda e: e.activation(
                    gsb[:, 2048 * half:2048 * (half + 1)], ps[pb][:], AF.Silu, scale=sg_col[:, o:o + 1]))(o, half, pb),
                    reads=[ps_b[pb], sg_b], writes=[gsb_b], tag=f"f_gev{o}_{half}")
            for half in range(2):
                pb = half
                def umm(e, gi=gi, ot=ot, half=half, pb=pb):
                    for h in range(NT):
                        for q in range(4):
                            inst = e.matmul(ps[pb][:, 512 * q:512 * (q + 1)],
                                            wu[gi][:, 512 * h + 128 * ot:512 * h + 128 * (ot + 1)],
                                            xn[:, S * h + 2048 * half + 512 * q:S * h + 2048 * half + 512 * (q + 1)],
                                            start=(h == 0), stop=(h == NT - 1))
                    return inst
                p.step("tensor", umm, reads=[wu_b[gi], xn_b], writes=[ps_b[pb]], tag=f"f_umm{o}_{half}")
                p.step("vector", (lambda o, half, pb, fi: lambda e: e.scalar_tensor_tensor(
                    fsb[fi][:, 2048 * half:2048 * (half + 1)], ps[pb][:], su_col[:, o:o + 1],
                    gsb[:, 2048 * half:2048 * (half + 1)], ALU.mult, ALU.mult))(o, half, pb, o % 2),
                    reads=[ps_b[pb], su_b, gsb_b], writes=[fsb_b[o % 2]], tag=f"f_uev{o}_{half}")
            if gate_dbg is not None:
                p.dma("scalar", (lambda o: lambda e, s: e.dma_start(
                    out=gate_dbg[128 * o:128 * (o + 1), :], in_=gsb[:]).then_inc(s, 16))(o),
                    reads=[gsb_b], tag=f"f_gdb{o}")
            p.dma("sync", (lambda o, fi: lambda e, s: e.dma_start(
                out=fuse_hbm[128 * o:128 * (o + 1), :], in_=fsb[fi][:]).then_inc(s, 16))(o, o % 2),
                reads=[fsb_b[o % 2]], tag=f"f_fst{o}")
    p.pop_scope()
    p.barrier("f_gu_end")

    p.push_scope()
    TC = 512
    ntc = S // TC
    wd = p.sbuf("f_wd", [128, NO * 1024], BF16)
    wd_b = p.buf()
    fc = p.sbuf("f_fc0", [128, NO * TC], BF16)
    fc_b = p.buf()
    xr = [p.sbuf(f"f_xr{i}", [128, TC], F32) for i in range(2)]
    xr_b = [p.buf(), p.buf()]
    ob = [p.sbuf(f"f_ob{i}", [128, TC], F32) for i in range(2)]
    ob_b = [p.buf() for _ in range(2)]
    psd = [p.psum(f"f_psd{i}", [128, 512]) for i in range(2)]
    psd_b = [p.buf(), p.buf()]
    for h2h in range(2):
        def wdload(e, s, h2h=h2h):
            e.dma_start(out=wd[:].rearrange("p (j o) -> p j o", o=1024),
                        in_=wdb16.rearrange("(j p) o -> p j o", p=128)[:, :, 1024 * h2h:1024 * (h2h + 1)]).then_inc(s, 16)
        p.dma("gpsimd", wdload, writes=[wd_b], n_dma=1, tag=f"f_wdl{h2h}")
        for tc in range(ntc):
            p.dma("sync", (lambda tc: lambda e, s: e.dma_start(
                out=fc[:].rearrange("p (j t) -> p j t", t=TC),
                in_=fuse_hbm.rearrange("(j p) t -> p j t", p=128)[:, :, TC * tc:TC * (tc + 1)]).then_inc(s, 16))(tc),
                writes=[fc_b], tag=f"f_fld{h2h}_{tc}")
            for h2t in range(8):
                pb = h2t % 2
                h2 = 8 * h2h + h2t
                p.dma("scalar", (lambda h2, tc, pb: lambda e, s: e.dma_start(
                    out=xr[pb][:], in_=x1fm[128 * h2:128 * (h2 + 1), TC * tc:TC * (tc + 1)]).then_inc(s, 16))(h2, tc, pb),
                    writes=[xr_b[pb]], tag=f"f_xld{h2h}_{tc}_{h2t}")

                def dmm(e, h2t=h2t, pb=pb):
                    for ot in range(NO):
                        inst = e.matmul(psd[pb][:],
                                        wd[:, 1024 * ot + 128 * h2t:1024 * ot + 128 * (h2t + 1)],
                                        fc[:, TC * ot:TC * (ot + 1)],
                                        start=(ot == 0), stop=(ot == NO - 1))
                    return inst
                p.step("tensor", dmm, reads=[wd_b, fc_b], writes=[psd_b[pb]], tag=f"f_dmm{h2h}_{tc}_{h2t}")
                p.step("vector", (lambda h2, pb: lambda e: e.scalar_tensor_tensor(
                    ob[pb][:], psd[pb][:], sd_col[:, h2:h2 + 1],
                    xr[pb][:], ALU.mult, ALU.add))(h2, pb),
                    reads=[psd_b[pb], sd_b, xr_b[pb]], writes=[ob_b[pb]], tag=f"f_dev{h2h}_{tc}_{h2t}")
                p.dma("sync", (lambda h2, tc, pb: lambda e, s: e.dma_start(
                    out=x2fm[128 * h2:128 * (h2 + 1), TC * tc:TC * (tc + 1)], in_=ob[pb][:]).then_inc(s, 16))(h2, tc, pb),
                    reads=[ob_b[pb]], tag=f"f_dst{h2h}_{tc}_{h2t}")
    p.pop_scope()
    p.pop_scope()
    return p.barrier("f_end")


L2_CONSTS = ["IDF", "ONESCOL", "ONESROW"]


def build_launch2(dbg=False):
    nc = bass.Bass("TRN2", target_bir_lowering=False, debug=False)
    x1fm = nc.dram_tensor("x1fm", [D, S], F32, kind="ExternalInput").ap()
    wg8 = nc.dram_tensor("w_gate", [D, 8192], I8, kind="ExternalInput").ap()
    wu8 = nc.dram_tensor("w_up", [D, 8192], I8, kind="ExternalInput").ap()
    wd8 = nc.dram_tensor("w_down", [8192, D], I8, kind="ExternalInput").ap()
    sg = nc.dram_tensor("s_gate", [8192], F32, kind="ExternalInput").ap()
    su = nc.dram_tensor("s_up", [8192], F32, kind="ExternalInput").ap()
    sd = nc.dram_tensor("s_down", [D], F32, kind="ExternalInput").ap()
    x2fm = nc.dram_tensor("x2fm", [D, S], F32, kind="ExternalOutput").ap()
    knd = "ExternalOutput" if dbg else "Internal"
    xn_hbm = nc.dram_tensor("xn_s", [D, S], BF16, kind=knd).ap()
    fuse_hbm = nc.dram_tensor("fuse_s", [8192, S], BF16, kind=knd).ap()
    wgb16 = nc.dram_tensor("wgb16", [D, 8192], BF16, kind="Internal").ap()
    wub16 = nc.dram_tensor("wub16", [D, 8192], BF16, kind="Internal").ap()
    wdb16 = nc.dram_tensor("wdb16", [8192, D], BF16, kind="Internal").ap()

    p = Prog(nc)
    cst = declare_consts(nc, p, L2_CONSTS)
    def ffnpc(e, s):
        e.dma_start(out=wgb16, in_=wg8).then_inc(s, 16)
        e.dma_start(out=wub16, in_=wu8).then_inc(s, 16)
        e.dma_start(out=wdb16, in_=wd8).then_inc(s, 16)
    p.dma("gpsimd", ffnpc, n_dma=3, tag="f_precast")
    gate_dbg = nc.dram_tensor("gate_s", [8192, S], F32, kind="ExternalOutput").ap() if dbg else None
    emit_ln(p, nc, x1fm, xn_hbm, cst, "ln0f")
    emit_l0_ffn(p, nc, x1fm, xn_hbm, wgb16, wub16, wdb16, sg, su, sd, fuse_hbm, x2fm, cst, gate_dbg)
    p.emit()
    return nc




def emit_l1_attn(p, nc, x2fm, xn_hbm, wqb, wkb, wvb, bq_ap, bk_ap, bv_ap, x3fm, cst, dbg_ap=None):
    PC = 512
    npc = S // PC
    p.push_scope()
    bqs_col = p.sbuf("g_bqs", [128, NT], F32)
    bkr = p.sbuf("g_bkr", [1, 1024], F32)
    bkr_b = p.buf()
    bkb = p.sbuf("g_bkb", [128, 1024], F32)
    bkb_b = p.buf()
    p.push_scope()
    pst_sc = p.psum("g_scps", [128, 128])
    pst_sc_b = p.buf()
    bqs_b = emit_scale_cols(p, nc, bq_ap, NT, "gbq", cst, pst_sc, pst_sc_b, bqs_col)
    sc = p.step("vector", lambda e: e.tensor_scalar_mul(bqs_col[:], bqs_col[:], ASCALE),
                reads=[bqs_b], writes=[bqs_b], tag="g_bqs_s")
    p.dma("sync", lambda e, s: [e.dma_start(out=bkr[0:1, 0:512], in_=bk_ap[None, :]).then_inc(s, 16),
                                e.dma_start(out=bkr[0:1, 512:1024], in_=bv_ap[None, :]).then_inc(s, 16)][-1],
          writes=[bkr_b], n_dma=2, tag="g_bkr")
    for i in range(8):
        p.step("tensor", (lambda i: lambda e: e.matmul(pst_sc[:], cst["ONESROW"][0:1, :], bkr[0:1, 128 * i:128 * (i + 1)],
                                                       start=True, stop=True))(i),
               reads=[bkr_b], writes=[pst_sc_b], tag=f"g_bkb_mm{i}")
        p.step("vector", (lambda i: lambda e: e.tensor_copy(bkb[:, 128 * i:128 * (i + 1)], pst_sc[:]))(i),
               reads=[pst_sc_b], writes=[bkb_b], tag=f"g_bkb_cp{i}")
    p.pop_scope()
    p.barrier("g_pre")

    wqT = p.sbuf("g_wqT", [128, NT * D], BF16)
    wkT = p.sbuf("g_wkT", [128, NT * 512], BF16)
    wvT = p.sbuf("g_wvT", [128, NT * 512], BF16)
    wqT_b, wkT_b, wvT_b = p.buf(), p.buf(), p.buf()

    def wload(e, s):
        for et in range(NT):
            e.dma_start(out=wqT[:, D * et:D * (et + 1)],
                        in_=wqb[:, 128 * et:128 * (et + 1)], transpose=True).then_inc(s, 16)
            e.dma_start(out=wkT[:, 512 * et:512 * (et + 1)],
                        in_=wkb[:, 128 * et:128 * (et + 1)], transpose=True).then_inc(s, 16)
            e.dma_start(out=wvT[:, 512 * et:512 * (et + 1)],
                        in_=wvb[:, 128 * et:128 * (et + 1)], transpose=True).then_inc(s, 16)
    p.dma("sync", wload, writes=[wqT_b, wkT_b, wvT_b], n_dma=3 * NT, tag="g_wl")

    Sst = p.sbuf("g_S", [128, 4 * 128], F32)
    Sbf = p.sbuf("g_Sbf", [128, 4 * 128], BF16)
    S_b = [p.buf() for _ in range(4)]
    Sbf_b = [p.buf() for _ in range(4)]
    p.step("vector", lambda e: e.memset(Sst[:], 0.0), writes=S_b, tag="g_S0")
    p.step("gpsimd", lambda e: e.memset(Sbf[:], 0.0), writes=Sbf_b, tag="g_Sbf0")

    xnc = p.sbuf("g_xn", [128, NT * PC], BF16)
    xnc_b = p.buf()
    qT = p.sbuf("g_qT", [128, NT * PC], BF16)
    qT_b = p.buf()
    kpre = p.sbuf("g_kpre", [128, 512], F32)
    vF = p.sbuf("g_vF", [128, 512], BF16)
    vH = p.sbuf("g_vH", [64, 512], BF16)
    vQ1 = p.sbuf("g_vQ1", [32, 512], BF16)
    vQ3 = p.sbuf("g_vQ3", [32, 512], BF16)
    knF = p.sbuf("g_knF", [128, 512], BF16)
    knH = p.sbuf("g_knH", [64, 512], BF16)
    knQ1 = p.sbuf("g_knQ1", [32, 512], BF16)
    knQ3 = p.sbuf("g_knQ3", [32, 512], BF16)
    Ee = p.sbuf("g_E", [128, 512], F32)
    Rr = p.sbuf("g_R", [128, 512], F32)
    kf = p.sbuf("g_kf", [128, 512], F32)
    uu = p.sbuf("g_u", [128, 512], F32)
    gg = p.sbuf("g_g", [128, 512], F32)
    er = p.sbuf("g_er", [128, 512], F32)
    cT_sb = p.sbuf("g_cT", [128, 512], F32)
    cs = p.sbuf("g_cs", [128, 512], F32)
    eqc = p.sbuf("g_eqc", [128, 512], BF16)
    emc = p.sbuf("g_emc", [128, 512], F32)
    khatT = p.sbuf("g_khatT", [128, 512], BF16)
    qst = p.sbuf("g_qst", [128, 512], BF16)
    ect = p.sbuf("g_ect", [128, 16], F32)
    ecm = p.sbuf("g_ecm", [128, 16], F32)
    pT_sb = p.sbuf("g_pT", [32, 128], BF16)
    x2c = [p.sbuf(f"g_x2c{i}", [128, NT * 128], F32) for i in range(2)]
    x3c = [p.sbuf(f"g_x3c{i}", [128, NT * 128], F32) for i in range(2)]
    kpre_b = p.buf(); vAB_b = p.buf(); knAB_b = p.buf()
    E_b = p.buf(); R_b = p.buf(); kf_b = p.buf(); u_b = p.buf(); g_b = p.buf(); er_b = p.buf()
    cT_b = p.buf(); cs_b = p.buf(); eqc_b = p.buf(); emc_b = p.buf(); khatT_b = p.buf()
    qst_b = p.buf(); ect_b = p.buf(); ecm_b = p.buf(); pT_b = p.buf()
    x2c_b = [p.buf(), p.buf()]; x3c_b = [p.buf(), p.buf()]

    ps_q = p.psum("g_psq", [128, 1024])
    ps_q_b = [p.buf(), p.buf()]
    ps_c = p.psum("g_psc", [128, 512])
    ps_c_b = p.buf()
    ps_r = p.psum("g_psr", [128, 512])
    ps_r_b = p.buf()
    ps_cT = p.psum("g_pscT", [128, 512])
    ps_cT_b = p.buf()
    ps_kT = p.psum("g_pskT", [128, 512])
    ps_kT_b = p.buf()
    ps_o = p.psum("g_pso", [128, 512])
    ps_o_b = p.buf()
    ps_su = p.psum("g_pssu", [128, 128])
    ps_su_b = p.buf()

    for pc in range(npc):
        p.dma("gpsimd", (lambda pc: lambda e, s: e.dma_start(
            out=xnc[:].rearrange("p (j t) -> p j t", t=PC),
            in_=fm_view(xn_hbm)[:, :, PC * pc:PC * (pc + 1)]).then_inc(s, 16))(pc),
            writes=[xnc_b], tag=f"g_ldx{pc}")
        for ft in range(NT):
            pb = ft % 2
            def qmm(e, ft=ft, pb=pb):
                for et in range(NT):
                    inst = e.matmul(ps_q[:, 512 * pb:512 * (pb + 1)],
                                    wqT[:, D * et + 128 * ft:D * et + 128 * (ft + 1)],
                                    xnc[:, PC * et:PC * (et + 1)],
                                    start=(et == 0), stop=(et == NT - 1))
                return inst
            p.step("tensor", qmm, reads=[wqT_b, xnc_b], writes=[ps_q_b[pb]], tag=f"g_qmm{pc}_{ft}")
            p.step("scalar", (lambda ft, pb: lambda e: e.activation(
                qT[:, PC * ft:PC * (ft + 1)], ps_q[:, 512 * pb:512 * (pb + 1)], AF.Relu,
                scale=ASCALE, bias=bqs_col[:, ft:ft + 1]))(ft, pb),
                reads=[ps_q_b[pb], bqs_b], writes=[qT_b], tag=f"g_qev{pc}_{ft}")
        for gb in range(PC // 128):
            t0 = 128 * gb
            def kvmm(e, gb=gb, t0=t0):
                for et in range(NT):
                    e.matmul(ps_q[:, 0:512],
                             xnc[:, PC * et + t0:PC * et + t0 + 128],
                             wkT[:, 512 * et:512 * (et + 1)],
                             start=(et == 0), stop=(et == NT - 1))
                for et in range(NT):
                    inst = e.matmul(ps_q[:, 512:1024],
                                    xnc[:, PC * et + t0:PC * et + t0 + 128],
                                    wvT[:, 512 * et:512 * (et + 1)],
                                    start=(et == 0), stop=(et == NT - 1))
                return inst
            p.step("tensor", kvmm, reads=[wkT_b, wvT_b, xnc_b], writes=[ps_q_b[0], ps_q_b[1]], tag=f"g_kv{pc}_{gb}")
            p.step("vector", lambda e: e.tensor_add(kpre[:], ps_q[:, 0:512], bkb[:, 0:512]),
                   reads=[ps_q_b[0], bkb_b], writes=[kpre_b], tag=f"g_kev{pc}_{gb}")
            p.step("vector", lambda e: e.tensor_add(vF[:], ps_q[:, 512:1024], bkb[:, 512:1024]),
                   reads=[ps_q_b[1], bkb_b], writes=[vAB_b], tag=f"g_vev{pc}_{gb}")
            p.step("gpsimd", lambda e: e.tensor_copy(vQ1[:], vF[32:64, :]), reads=[vAB_b], writes=[vAB_b])
            p.step("gpsimd", lambda e: e.tensor_copy(vH[:], vF[64:128, :]), reads=[vAB_b], writes=[vAB_b])
            p.step("gpsimd", lambda e: e.tensor_copy(vQ3[:], vH[32:64, :]), reads=[vAB_b], writes=[vAB_b])
            p.step("scalar", lambda e: e.activation(Ee[:], kpre[:], AF.Exp, scale=-1.0),
                   reads=[kpre_b], writes=[E_b], tag=f"g_E{pc}_{gb}")
            p.step("vector", lambda e: e.tensor_scalar_add(Rr[:], Ee[:], 1.0), reads=[E_b], writes=[R_b])
            p.step("vector", lambda e: e.reciprocal(Rr[:], Rr[:]), reads=[R_b], writes=[R_b])
            p.step("vector", lambda e: e.tensor_scalar_min(kf[:], Rr[:], 0.95), reads=[R_b], writes=[kf_b])
            p.step("gpsimd", lambda e: e.tensor_mul(uu[:], Ee[:], Rr[:]), reads=[E_b, R_b], writes=[u_b])
            p.step("vector", lambda e: e.tensor_scalar_max(uu[:], uu[:], 0.05), reads=[u_b], writes=[u_b])
            p.step("scalar", lambda e: e.activation(gg[:], uu[:], AF.Ln), reads=[u_b], writes=[g_b], tag=f"g_g{pc}_{gb}")
            p.step("tensor", lambda e: e.matmul(ps_r[:], cst["GLA_R"][:], gg[:], start=True, stop=True),
                   reads=[g_b], writes=[ps_r_b], tag=f"g_mr{pc}_{gb}")
            def ctmm(e):
                for g4 in range(4):
                    inst = e.matmul(ps_cT[:, 128 * g4:128 * (g4 + 1)],
                                    gg[:, 128 * g4:128 * (g4 + 1)], cst["GLA_L"][:],
                                    start=True, stop=True)
                return inst
            p.step("tensor", ctmm, reads=[g_b], writes=[ps_cT_b], tag=f"g_mcT{pc}_{gb}")
            def kftmm(e):
                for g4 in range(4):
                    inst = e.transpose(ps_kT[:, 128 * g4:128 * (g4 + 1)],
                                       kf[:, 128 * g4:128 * (g4 + 1)], cst["IDF"][:])
                return inst
            p.step("tensor", kftmm, reads=[kf_b], writes=[ps_kT_b], tag=f"g_mkT{pc}_{gb}")
            p.step("scalar", lambda e: e.activation(er[:], ps_r[:], AF.Exp), reads=[ps_r_b], writes=[er_b])
            p.step("vector", lambda e: e.tensor_mul(knF[:], kf[:], er[:]),
                   reads=[kf_b, er_b], writes=[knAB_b])
            p.step("gpsimd", lambda e: e.tensor_copy(knQ1[:], knF[32:64, :]), reads=[knAB_b], writes=[knAB_b])
            p.step("gpsimd", lambda e: e.tensor_copy(knH[:], knF[64:128, :]), reads=[knAB_b], writes=[knAB_b])
            p.step("gpsimd", lambda e: e.tensor_copy(knQ3[:], knH[32:64, :]), reads=[knAB_b], writes=[knAB_b])
            p.step("vector", lambda e: e.tensor_copy(cT_sb[:], ps_cT[:]), reads=[ps_cT_b], writes=[cT_b])
            def csf(e):
                for g4 in range(4):
                    for sb_ in range(4):
                        o = 128 * g4 + 32 * sb_
                        inst = e.tensor_scalar(cs[:, o:o + 32], cT_sb[:, o:o + 32],
                                               cT_sb[:, o + 15:o + 16], None, ALU.subtract)
                return inst
            p.step("vector", csf, reads=[cT_b], writes=[cs_b], tag=f"g_cs{pc}_{gb}")
            p.step("scalar", lambda e: e.activation(eqc[:], cs[:], AF.Exp), reads=[cs_b], writes=[eqc_b])
            p.step("scalar", lambda e: e.activation(emc[:], cs[:], AF.Exp, scale=-1.0), reads=[cs_b], writes=[emc_b])
            p.step("vector", lambda e: e.tensor_mul(khatT[:], ps_kT[:], emc[:]),
                   reads=[ps_kT_b, emc_b], writes=[khatT_b])
            def ectf(e):
                cr = cT_sb[:].rearrange("p (g b t) -> p g b t", g=4, t=32)
                er_ = ect[:].rearrange("p (g b) -> p g b", g=4)
                for g4 in range(4):
                    inst = e.activation(er_[:, g4, :], cr[:, g4, :, 31:32].rearrange("p b t -> p (b t)"), AF.Exp)
                return inst
            p.step("scalar", ectf, reads=[cT_b], writes=[ect_b], tag=f"g_ect{pc}_{gb}")
            def ecmf(e):
                cr = cT_sb[:].rearrange("p (g b t) -> p g b t", g=4, t=32)
                er_ = ecm[:].rearrange("p (g b) -> p g b", g=4)
                for g4 in range(4):
                    inst = e.activation(er_[:, g4, :], cr[:, g4, :, 15:16].rearrange("p b t -> p (b t)"), AF.Exp)
                return inst
            p.step("scalar", ecmf, reads=[cT_b], writes=[ecm_b], tag=f"g_ecm{pc}_{gb}")
            gbt = PC * pc + t0
            xb2 = (pc * 4 + gb) % 2
            p.dma("sync", (lambda gbt, xb2: lambda e, s: e.dma_start(
                out=x2c[xb2][:].rearrange("p (j t) -> p j t", t=128),
                in_=fm_view(x2fm)[:, :, gbt:gbt + 128]).then_inc(s, 16))(gbt, xb2),
                writes=[x2c_b[xb2]], tag=f"g_x2l{pc}_{gb}")
            for g4 in range(4):
                def qstf(e, g4=g4, t0=t0):
                    for s_ in range(4):
                        for r_ in range(4):
                            inst = e.tensor_mul(qst[:, 128 * s_ + 32 * r_:128 * s_ + 32 * (r_ + 1)],
                                                qT[:, PC * (4 * g4 + r_) + t0 + 32 * s_:PC * (4 * g4 + r_) + t0 + 32 * (s_ + 1)],
                                                eqc[:, 128 * g4 + 32 * s_:128 * g4 + 32 * (s_ + 1)])
                    return inst
                p.step("vector", qstf, reads=[qT_b, eqc_b], writes=[qst_b], tag=f"g_qst{pc}_{gb}_{g4}")
                halves = [(knF, vF), (knQ1, vQ1), (knH, vH), (knQ3, vQ3)]
                for s_ in range(4):
                    kn_h, v_h = halves[s_]
                    ro = 0
                    p.step("scalar", (lambda g4, s_: lambda e: e.mul(
                        Sbf[:, 128 * g4:128 * (g4 + 1)], Sst[:, 128 * g4:128 * (g4 + 1)],
                        ecm[:, 4 * g4 + s_:4 * g4 + s_ + 1]))(g4, s_),
                        reads=[S_b[g4], ecm_b], writes=[Sbf_b[g4]], tag=f"g_sbr{pc}_{gb}_{g4}_{s_}")
                    p.step("tensor", (lambda g4, s_: lambda e: e.matmul(
                        ps_c[0:32, 0:128], khatT[:, 128 * g4 + 32 * s_:128 * g4 + 32 * (s_ + 1)],
                        qst[:, 128 * s_:128 * (s_ + 1)], start=True, stop=True))(g4, s_),
                        reads=[khatT_b, qst_b], writes=[ps_c_b], tag=f"g_pmm{pc}_{gb}_{g4}_{s_}")
                    p.step("vector", lambda e: e.tensor_mul(pT_sb[:], ps_c[0:32, 0:128], cst["GLA_PMASK"][:]),
                           reads=[ps_c_b], writes=[pT_b], tag=f"g_pev{pc}_{gb}_{g4}_{s_}")
                    def omm(e, g4=g4, s_=s_, kn_h=kn_h, v_h=v_h, ro=ro):
                        e.matmul(ps_o[:, 128 * s_:128 * (s_ + 1)], Sbf[:, 128 * g4:128 * (g4 + 1)],
                                 qst[:, 128 * s_:128 * (s_ + 1)], start=True, stop=False)
                        return e.matmul(ps_o[:, 128 * s_:128 * (s_ + 1)], v_h[ro:ro + 32, 128 * g4:128 * (g4 + 1)],
                                        pT_sb[:], start=False, stop=True)
                    ost = p.step("tensor", omm, reads=[Sbf_b[g4], qst_b, vAB_b, pT_b], writes=[ps_o_b],
                           tag=f"g_omm{pc}_{gb}_{g4}_{s_}")
                    if dbg_ap is not None and pc == 7 and gb == 3 and g4 == 3 and s_ == 0:
                        dbg_sb = p.sbuf("g_dbg", [128, 512], F32)
                        dbg_b = p.buf()
                        p.step("vector", lambda e: e.tensor_copy(dbg_sb[:, 0:128], Sbf[:, 384:512]),
                               reads=[Sbf_b[3]], writes=[dbg_b], deps=[ost], tag="g_dbg1")
                        p.step("vector", lambda e: e.tensor_copy(dbg_sb[:, 128:256], ps_o[:, 0:128]),
                               reads=[ps_o_b], writes=[dbg_b], deps=[ost], tag="g_dbg2")
                        p.step("vector", lambda e: e.tensor_copy(dbg_sb[:, 256:384], Sst[:, 384:512]),
                               reads=[S_b[3]], writes=[dbg_b], deps=[ost], tag="g_dbg2b")
                        p.step("vector", lambda e: e.tensor_copy(dbg_sb[:, 384:512], ecm[:].rearrange("p f -> p f").to_broadcast([128, 128]) if False else eqc[:, 384:512]),
                               reads=[eqc_b], writes=[dbg_b], deps=[ost], tag="g_dbg2c")
                        p.dma("sync", lambda e, s: e.dma_start(out=dbg_ap, in_=dbg_sb[:]).then_inc(s, 16),
                              reads=[dbg_b], tag="g_dbg3")
                    p.step("tensor", (lambda g4, s_, kn_h=kn_h, v_h=v_h, ro=ro: lambda e: e.matmul(
                        ps_su[:], kn_h[ro:ro + 32, 128 * g4:128 * (g4 + 1)],
                        v_h[ro:ro + 32, 128 * g4:128 * (g4 + 1)], start=True, stop=True))(g4, s_),
                        reads=[knAB_b, vAB_b], writes=[ps_su_b], tag=f"g_smm{pc}_{gb}_{g4}_{s_}")
                    p.step("vector", (lambda g4, s_: lambda e: e.scalar_tensor_tensor(
                        Sst[:, 128 * g4:128 * (g4 + 1)], Sst[:, 128 * g4:128 * (g4 + 1)],
                        ect[:, 4 * g4 + s_:4 * g4 + s_ + 1], ps_su[:], ALU.mult, ALU.add))(g4, s_),
                        reads=[ps_su_b, ect_b, S_b[g4]], writes=[S_b[g4]], tag=f"g_sup{pc}_{gb}_{g4}_{s_}")
                po = ps_o[:].rearrange("p (s2 r t) -> p s2 r t", r=4, t=32)
                for r_ in range(4):
                    h = 4 * g4 + r_
                    p.step("vector", (lambda h, r_, xb2: lambda e: e.tensor_add(
                        x3c[xb2][:, 128 * h:128 * h + 128].rearrange("p (s2 t) -> p s2 t", t=32),
                        po[:, :, r_, :],
                        x2c[xb2][:, 128 * h:128 * (h + 1)].rearrange("p (s2 t) -> p s2 t", t=32)))(h, r_, xb2),
                        reads=[ps_o_b, x2c_b[xb2]], writes=[x3c_b[xb2]], tag=f"g_oev{pc}_{gb}_{g4}_{r_}")
            p.dma("sync", (lambda gbt, xb2: lambda e, s: e.dma_start(
                out=fm_view(x3fm)[:, :, gbt:gbt + 128],
                in_=x3c[xb2][:].rearrange("p (j t) -> p j t", t=128)).then_inc(s, 16))(gbt, xb2),
                reads=[x3c_b[xb2]], tag=f"g_x3s{pc}_{gb}")
    p.pop_scope()
    return p.barrier("g_end")


L3_CONSTS = ["IDF", "ONESCOL", "ONESROW", "GLA_L", "GLA_R", "GLA_PMASK"]


def build_launch3(dbg=False):
    nc = bass.Bass("TRN2", target_bir_lowering=False, debug=False)
    x2fm = nc.dram_tensor("x2fm", [D, S], F32, kind="ExternalInput").ap()
    Wq = nc.dram_tensor("Wq", [D, D], F32, kind="ExternalInput").ap()
    bq = nc.dram_tensor("bq", [D], F32, kind="ExternalInput").ap()
    Wk = nc.dram_tensor("Wk", [512, D], F32, kind="ExternalInput").ap()
    bk = nc.dram_tensor("bk", [512], F32, kind="ExternalInput").ap()
    Wv = nc.dram_tensor("Wv", [512, D], F32, kind="ExternalInput").ap()
    bv = nc.dram_tensor("bv", [512], F32, kind="ExternalInput").ap()
    x3fm = nc.dram_tensor("x3fm", [D, S], F32, kind="ExternalOutput").ap()
    knd = "ExternalOutput" if dbg else "Internal"
    xn_hbm = nc.dram_tensor("xn_s", [D, S], BF16, kind=knd).ap()
    wqb = nc.dram_tensor("wqb", [D, D], BF16, kind="Internal").ap()
    wkb = nc.dram_tensor("wkb", [512, D], BF16, kind="Internal").ap()
    wvb = nc.dram_tensor("wvb", [512, D], BF16, kind="Internal").ap()

    p = Prog(nc)
    cst = declare_consts(nc, p, L3_CONSTS)
    p.dma("gpsimd", lambda e, s: [e.dma_start(out=wqb, in_=Wq).then_inc(s, 16),
                                  e.dma_start(out=wkb, in_=Wk).then_inc(s, 16),
                                  e.dma_start(out=wvb, in_=Wv).then_inc(s, 16)][-1], n_dma=3, tag="g_wc")
    dbg_ap = nc.dram_tensor("dbg", [128, 512], F32, kind="ExternalOutput").ap() if dbg else None
    emit_ln(p, nc, x2fm, xn_hbm, cst, "ln1a")
    emit_l1_attn(p, nc, x2fm, xn_hbm, wqb, wkb, wvb, bq, bk, bv, x3fm, cst, dbg_ap)
    p.emit()
    return nc




def emit_l1_ffn(p, nc, x3fm, xn_hbm, wcb, genes_ap, wdb, wub, temp_ap, scale_ap,
                w_hbm, out_tm, cst):
    G = 4096
    p.push_scope()
    srow = p.sbuf("m_srow", [1, 2], F32)
    srow_b = p.buf()
    p.dma("sync", lambda e, s: [e.dma_start(out=srow[0:1, 0:1], in_=temp_ap[None, :]).then_inc(s, 16),
                                e.dma_start(out=srow[0:1, 1:2], in_=scale_ap[None, :]).then_inc(s, 16)][-1],
          writes=[srow_b], n_dma=2, tag="m_sld")
    p.step("vector", lambda e: e.tensor_scalar_max(srow[0:1, 0:1], srow[0:1, 0:1], 0.1),
           reads=[srow_b], writes=[srow_b])
    p.step("vector", lambda e: e.reciprocal(srow[0:1, 0:1], srow[0:1, 0:1]),
           reads=[srow_b], writes=[srow_b])
    sccol = p.sbuf("m_sccol", [128, 2], F32)
    sccol_b = p.buf()
    p.push_scope()
    ps_tmp = p.psum("m_pst", [128, 128])
    ps_tmp_b = p.buf()
    p.step("tensor", lambda e: e.matmul(ps_tmp[:, 0:2], cst["ONESROW"][0:1, :], srow[0:1, 0:2], start=True, stop=True),
           reads=[srow_b], writes=[ps_tmp_b], tag="m_scbc")
    p.step("vector", lambda e: e.tensor_copy(sccol[:], ps_tmp[:, 0:2]), reads=[ps_tmp_b], writes=[sccol_b])
    p.pop_scope()
    p.barrier("m_pre")

    p.push_scope()
    wcT = p.sbuf("m_wcT", [128, NT * G], BF16)
    wcT_b = p.buf()

    def wcl(e, s):
        for et in range(NT):
            e.dma_start(out=wcT[:, G * et:G * (et + 1)],
                        in_=wcb[:, 128 * et:128 * (et + 1)], transpose=True).then_inc(s, 16)
    p.dma("sync", wcl, writes=[wcT_b], n_dma=NT, tag="m_wcl")
    xnt = [p.sbuf(f"m_xnt{i}", [128, NT * 128], BF16) for i in range(2)]
    xnt_b = [p.buf(), p.buf()]
    lg2 = [p.sbuf(f"m_lg{i}", [128, G], BF16) for i in range(2)]
    lg2_b = [p.buf(), p.buf()]
    sa = p.sbuf("m_sa", [128, G], BF16)
    sb_ = p.sbuf("m_sb", [128, G], BF16)
    m8 = p.sbuf("m_m8", [128, 40], BF16)
    m8f = p.sbuf("m_m8f", [128, 2], F32)
    ge = p.sbuf("m_ge", [128, G], BF16)
    ee = p.sbuf("m_ee", [128, G], BF16)
    wb = p.sbuf("m_wb", [128, G], BF16)
    sa_b, sb_b, m8_b, ge_b, ee_b, wb_b = [p.buf() for _ in range(6)]
    ps_l = [p.psum(f"m_psl{i}", [128, 2048]) for i in range(2)]
    ps_l_b = [p.buf(), p.buf()]
    for tt in range(S // 128):
        bi = tt % 2
        p.dma("gpsimd", (lambda bi, tt: lambda e, s: e.dma_start(
            out=xnt[bi][:].rearrange("p (j t) -> p j t", t=128),
            in_=fm_view(xn_hbm)[:, :, 128 * tt:128 * (tt + 1)]).then_inc(s, 16))(bi, tt),
            writes=[xnt_b[bi]], tag=f"m_xl{tt}")
        lg = lg2[bi]
        lg_b = lg2_b[bi]
        for gh in range(2):
            def lmm(e, bi=bi, gh=gh):
                for et in range(NT):
                    for q in range(4):
                        inst = e.matmul(ps_l[gh][:, 512 * q:512 * (q + 1)],
                                        xnt[bi][:, 128 * et:128 * (et + 1)],
                                        wcT[:, G * et + 2048 * gh + 512 * q:G * et + 2048 * gh + 512 * (q + 1)],
                                        start=(et == 0), stop=(et == NT - 1))
                return inst
            p.step("tensor", lmm, reads=[xnt_b[bi], wcT_b], writes=[ps_l_b[gh]], tag=f"m_lmm{tt}_{gh}")
            p.step("vector", (lambda gh, lg=lg: lambda e: e.tensor_scalar_mul(
                lg[:, 2048 * gh:2048 * (gh + 1)], ps_l[gh][:], sccol[:, 0:1]))(gh),
                reads=[ps_l_b[gh], sccol_b], writes=[lg_b], tag=f"m_lev{tt}_{gh}")
        p.step("vector", (lambda lg: lambda e: e.max(out=m8[:, 0:8], in_=lg[:]))(lg), reads=[lg_b], writes=[m8_b])
        p.step("vector", (lambda lg: lambda e: e.match_replace(out=sa[:], in_to_replace=m8[:, 0:8], in_values=lg[:], imm_value=-1e30))(lg),
               reads=[lg_b, m8_b], writes=[sa_b])
        p.step("vector", lambda e: e.max(out=m8[:, 8:16], in_=sa[:]), reads=[sa_b], writes=[m8_b])
        p.step("vector", lambda e: e.match_replace(out=sb_[:], in_to_replace=m8[:, 8:16], in_values=sa[:], imm_value=-1e30),
               reads=[sa_b, m8_b], writes=[sb_b])
        p.step("vector", lambda e: e.max(out=m8[:, 16:24], in_=sb_[:]), reads=[sb_b], writes=[m8_b])
        p.step("vector", lambda e: e.match_replace(out=sa[:], in_to_replace=m8[:, 16:24], in_values=sb_[:], imm_value=-1e30),
               reads=[sb_b, m8_b], writes=[sa_b])
        p.step("vector", lambda e: e.max(out=m8[:, 24:32], in_=sa[:]), reads=[sa_b], writes=[m8_b])
        p.step("vector", lambda e: e.tensor_copy(m8f[:, 0:1], m8[:, 31:32]), reads=[m8_b], writes=[m8_b])
        p.step("vector", lambda e: e.tensor_scalar_mul(m8f[:, 1:2], m8[:, 0:1], -1.0), reads=[m8_b], writes=[m8_b])
        p.step("vector", (lambda lg: lambda e: e.tensor_scalar(ge[:], lg[:], m8f[:, 0:1], None, ALU.is_ge))(lg),
               reads=[lg_b, m8_b], writes=[ge_b], tag=f"m_ge{tt}")
        p.step("scalar", (lambda lg: lambda e: e.activation(ee[:], lg[:], AF.Exp, bias=m8f[:, 1:2]))(lg),
               reads=[lg_b, m8_b], writes=[ee_b], tag=f"m_exp{tt}")
        p.step("vector", lambda e: e.tensor_mul(wb[:], ee[:], ge[:]), reads=[ee_b, ge_b], writes=[wb_b], tag=f"m_w{tt}")
        p.dma("sync", (lambda tt: lambda e, s: e.dma_start(
            out=w_hbm[128 * tt:128 * (tt + 1), :], in_=wb[:]).then_inc(s, 16))(tt),
            reads=[wb_b], tag=f"m_ws{tt}")
    p.pop_scope()
    p.barrier("m_A_end")

    p.push_scope()
    TC = 512
    ntc = S // TC
    gen = p.sbuf("m_gen", [128, 32 * 256], BF16)
    gen_b = p.buf()

    def genl(e, s):
        e.dma_start(out=gen[:].rearrange("p (j c) -> p j c", c=256),
                    in_=genes_ap.rearrange("(j p) c -> p j c", p=128)).then_inc(s, 16)
    p.dma("gpsimd", genl, writes=[gen_b], n_dma=1, tag="m_genl")
    wdT = p.sbuf("m_wdT", [128, 2 * D], BF16)
    wuT = p.sbuf("m_wuT", [128, 2 * D], BF16)
    wdT_b, wuT_b = p.buf(), p.buf()

    def wdl(e, s):
        for ct in range(2):
            e.dma_start(out=wdT[:, D * ct:D * (ct + 1)],
                        in_=wdb[:, 128 * ct:128 * (ct + 1)], transpose=True).then_inc(s, 16)
            e.dma_start(out=wuT[:, D * ct:D * (ct + 1)],
                        in_=wub[:, 128 * ct:128 * (ct + 1)], transpose=True).then_inc(s, 16)
    p.dma("sync", wdl, writes=[wdT_b, wuT_b], n_dma=4, tag="m_wdl")
    wT = [p.sbuf(f"m_wT{i}", [128, 32 * TC], BF16) for i in range(2)]
    wT_b = [p.buf(), p.buf()]
    xnc = p.sbuf("m_xnc", [128, NT * TC], BF16)
    xnc_b = p.buf()
    x3c = p.sbuf("m_x3c", [128, NT * TC], F32)
    x3c_b = p.buf()
    rz = p.sbuf("m_rz", [1, TC], F32)
    rz_b = p.buf()
    rzb = p.sbuf("m_rzb", [128, TC], F32)
    rzb_b = p.buf()
    exT = p.sbuf("m_exT", [128, 2 * TC], BF16)
    exT_b = p.buf()
    dn = p.sbuf("m_dn", [128, TC], BF16)
    up = p.sbuf("m_up", [128, TC], BF16)
    dn_b, up_b = p.buf(), p.buf()
    t1 = p.sbuf("m_t1", [128, TC], BF16)
    t1_b = p.buf()
    ofm = p.sbuf("m_ofm", [128, NT * TC], F32)
    ofm_b = p.buf()
    otr = [p.sbuf(f"m_otr{i}", [128, D], F32) for i in range(2)]
    otr_b = [p.buf(), p.buf()]
    ps_z = p.psum("m_psz", [1, TC])
    ps_z_b = p.buf()
    ps_b1 = p.psum("m_psb1", [128, TC])
    ps_b1_b = p.buf()
    ps_b2 = p.psum("m_psb2", [128, TC])
    ps_b2_b = p.buf()
    ps_b3 = p.psum("m_psb3", [128, TC])
    ps_b3_b = p.buf()
    ps_tr = p.psum("m_pstr", [128, D])
    ps_tr_b = p.buf()
    for tc in range(ntc):
        wbi = tc % 2
        def wTl(e, s, tc=tc, wbi=wbi):
            for gt in range(32):
                e.dma_start(out=wT[wbi][:, TC * gt:TC * (gt + 1)],
                            in_=w_hbm[TC * tc:TC * (tc + 1), 128 * gt:128 * (gt + 1)], transpose=True).then_inc(s, 16)
        p.dma("sync", wTl, writes=[wT_b[wbi]], n_dma=32, tag=f"m_wTl{tc}")
        p.dma("gpsimd", (lambda tc: lambda e, s: e.dma_start(
            out=xnc[:].rearrange("p (j t) -> p j t", t=TC),
            in_=fm_view(xn_hbm)[:, :, TC * tc:TC * (tc + 1)]).then_inc(s, 16))(tc),
            writes=[xnc_b], tag=f"m_xnl{tc}")
        p.dma("gpsimd", (lambda tc: lambda e, s: e.dma_start(
            out=x3c[:].rearrange("p (j t) -> p j t", t=TC),
            in_=fm_view(x3fm)[:, :, TC * tc:TC * (tc + 1)]).then_inc(s, 16))(tc),
            writes=[x3c_b], tag=f"m_x3l{tc}")

        def zmm(e, wbi=wbi):
            for gt in range(32):
                inst = e.matmul(ps_z[0:1, :], cst["ONESCOL"][:], wT[wbi][:, TC * gt:TC * (gt + 1)],
                                start=(gt == 0), stop=(gt == 31))
            return inst
        p.step("tensor", zmm, reads=[wT_b[wbi]], writes=[ps_z_b], tag=f"m_zmm{tc}")
        p.step("vector", lambda e: e.reciprocal(rz[:], ps_z[:]), reads=[ps_z_b], writes=[rz_b])
        def zbc(e):
            for q in range(TC // 512):
                inst = e.matmul(ps_b1[:, 512 * q:512 * (q + 1)], cst["ONESROW"][0:1, :],
                                rz[0:1, 512 * q:512 * (q + 1)], start=True, stop=True)
            return inst
        p.step("tensor", zbc, reads=[rz_b], writes=[ps_b1_b], tag=f"m_zbc{tc}")
        p.step("vector", lambda e: e.tensor_copy(rzb[:], ps_b1[:]), reads=[ps_b1_b], writes=[rzb_b])
        for ct in range(2):
            psx, psx_b = (ps_b2, ps_b2_b) if ct == 0 else (ps_b3, ps_b3_b)
            def emm(e, ct=ct, psx=psx, wbi=wbi):
                for gt in range(32):
                    inst = e.matmul(psx[:], gen[:, 256 * gt + 128 * ct:256 * gt + 128 * (ct + 1)],
                                    wT[wbi][:, TC * gt:TC * (gt + 1)],
                                    start=(gt == 0), stop=(gt == 31))
                return inst
            p.step("tensor", emm, reads=[gen_b, wT_b[wbi]], writes=[psx_b], tag=f"m_emm{tc}_{ct}")
            p.step("vector", (lambda ct, psx: lambda e: e.tensor_mul(
                exT[:, TC * ct:TC * (ct + 1)], psx[:], rzb[:]))(ct, psx),
                reads=[psx_b, rzb_b], writes=[exT_b], tag=f"m_eev{tc}_{ct}")
        for ft in range(NT):
            def dmm(e, ft=ft):
                for ct in range(2):
                    inst = e.matmul(ps_b2[:], wdT[:, D * ct + 128 * ft:D * ct + 128 * (ft + 1)],
                                    exT[:, TC * ct:TC * (ct + 1)], start=(ct == 0), stop=(ct == 1))
                return inst
            p.step("tensor", dmm, reads=[wdT_b, exT_b], writes=[ps_b2_b], tag=f"m_dmm{tc}_{ft}")
            p.step("scalar", lambda e: e.activation(dn[:], ps_b2[:], AF.Tanh), reads=[ps_b2_b], writes=[dn_b])
            def umm(e, ft=ft):
                for ct in range(2):
                    inst = e.matmul(ps_b3[:], wuT[:, D * ct + 128 * ft:D * ct + 128 * (ft + 1)],
                                    exT[:, TC * ct:TC * (ct + 1)], start=(ct == 0), stop=(ct == 1))
                return inst
            p.step("tensor", umm, reads=[wuT_b, exT_b], writes=[ps_b3_b], tag=f"m_umm{tc}_{ft}")
            p.step("scalar", lambda e: e.activation(up[:], ps_b3[:], AF.Copy, scale=sccol[:, 1:2]),
                   reads=[ps_b3_b, sccol_b], writes=[up_b])
            p.step("vector", lambda e: e.tensor_mul(t1[:], dn[:], up[:]), reads=[dn_b, up_b], writes=[t1_b])
            p.step("gpsimd", (lambda ft: lambda e: e.tensor_mul(t1[:], t1[:], xnc[:, TC * ft:TC * (ft + 1)]))(ft),
                   reads=[t1_b, xnc_b], writes=[t1_b])
            p.step("vector", (lambda ft: lambda e: e.tensor_add(
                ofm[:, TC * ft:TC * (ft + 1)], t1[:], x3c[:, TC * ft:TC * (ft + 1)]))(ft),
                reads=[t1_b, x3c_b], writes=[ofm_b], tag=f"m_oev{tc}_{ft}")
        for tts in range(TC // 128):
            ob = tts % 2
            def trf(e, tts=tts):
                for ft in range(NT):
                    inst = e.transpose(ps_tr[:, 128 * ft:128 * (ft + 1)],
                                       ofm[:, TC * ft + 128 * tts:TC * ft + 128 * (tts + 1)], cst["IDF"][:])
                return inst
            p.step("tensor", trf, reads=[ofm_b], writes=[ps_tr_b], tag=f"m_tr{tc}_{tts}")
            p.step("vector", (lambda ob: lambda e: e.tensor_copy(otr[ob][:], ps_tr[:]))(ob),
                   reads=[ps_tr_b], writes=[otr_b[ob]], tag=f"m_trv{tc}_{tts}")
            p.dma("sync", (lambda tc, tts, ob: lambda e, s: e.dma_start(
                out=out_tm[TC * tc + 128 * tts:TC * tc + 128 * (tts + 1), :], in_=otr[ob][:]).then_inc(s, 16))(tc, tts, ob),
                reads=[otr_b[ob]], tag=f"m_os{tc}_{tts}")
    p.pop_scope()
    p.pop_scope()
    return p.barrier("m_end")


L4_CONSTS = ["IDF", "ONESCOL", "ONESROW"]


def build_launch4(dbg=False):
    nc = bass.Bass("TRN2", target_bir_lowering=False, debug=False)
    x3fm = nc.dram_tensor("x3fm", [D, S], F32, kind="ExternalInput").ap()
    Wc = nc.dram_tensor("Wc", [4096, D], F32, kind="ExternalInput").ap()
    temp = nc.dram_tensor("temp", [1], F32, kind="ExternalInput").ap()
    genes = nc.dram_tensor("genes", [4096, 256], F32, kind="ExternalInput").ap()
    Wd = nc.dram_tensor("Wd", [D, 256], F32, kind="ExternalInput").ap()
    Wu = nc.dram_tensor("Wu", [D, 256], F32, kind="ExternalInput").ap()
    nw = nc.dram_tensor("norm_w", [D], F32, kind="ExternalInput").ap()
    scale = nc.dram_tensor("scale", [1], F32, kind="ExternalInput").ap()
    out_tm = nc.dram_tensor("out", [S, D], F32, kind="ExternalOutput").ap()
    knd = "ExternalOutput" if dbg else "Internal"
    xn_hbm = nc.dram_tensor("xn_s", [D, S], BF16, kind=knd).ap()
    w_hbm = nc.dram_tensor("w_s", [S, 4096], BF16, kind=knd).ap()
    wcb = nc.dram_tensor("wcb", [4096, D], BF16, kind="Internal").ap()
    wdb = nc.dram_tensor("wdb", [D, 256], BF16, kind="Internal").ap()
    wub = nc.dram_tensor("wub", [D, 256], BF16, kind="Internal").ap()

    p = Prog(nc)
    cst = declare_consts(nc, p, L4_CONSTS)
    p.dma("gpsimd", lambda e, s: [e.dma_start(out=wcb, in_=Wc).then_inc(s, 16),
                                  e.dma_start(out=wdb, in_=Wd).then_inc(s, 16),
                                  e.dma_start(out=wub, in_=Wu).then_inc(s, 16)][-1], n_dma=3, tag="m_wc")
    nw_col = p.sbuf("m_nw", [128, NT], F32)
    p.push_scope()
    pst_sc = p.psum("m_scps", [128, 128])
    pst_sc_b = p.buf()
    nw_b = emit_scale_cols(p, nc, nw, NT, "mnw", cst, pst_sc, pst_sc_b, nw_col)
    p.pop_scope()
    p.barrier("m_nw_end")
    emit_ln(p, nc, x3fm, xn_hbm, cst, "ln1f", nw_col=nw_col, nw_col_buf=nw_b)
    emit_l1_ffn(p, nc, x3fm, xn_hbm, wcb, genes, wdb, wub, temp, scale, w_hbm, out_tm, cst)
    p.emit()
    return nc




ALL_CONSTS = ["IDF", "ONESCOL", "ONESROW", "TRI_LO", "TRI_HI", "GLA_L", "GLA_R", "GLA_PMASK"]


def build_fused():
    nc = bass.Bass("TRN2", target_bir_lowering=False, debug=False)
    x_in = nc.dram_tensor("x", [S, D], F32, kind="ExternalInput").ap()
    wqkv = nc.dram_tensor("w_qkv", [3 * D, D], F32, kind="ExternalInput").ap()
    wout = nc.dram_tensor("w_out", [D, D], F32, kind="ExternalInput").ap()
    wg8 = nc.dram_tensor("w_gate", [D, 8192], I8, kind="ExternalInput").ap()
    wu8 = nc.dram_tensor("w_up", [D, 8192], I8, kind="ExternalInput").ap()
    wd8 = nc.dram_tensor("w_down", [8192, D], I8, kind="ExternalInput").ap()
    sg = nc.dram_tensor("s_gate", [8192], F32, kind="ExternalInput").ap()
    su = nc.dram_tensor("s_up", [8192], F32, kind="ExternalInput").ap()
    sd = nc.dram_tensor("s_down", [D], F32, kind="ExternalInput").ap()
    Wq = nc.dram_tensor("Wq", [D, D], F32, kind="ExternalInput").ap()
    bq = nc.dram_tensor("bq", [D], F32, kind="ExternalInput").ap()
    Wk = nc.dram_tensor("Wk", [512, D], F32, kind="ExternalInput").ap()
    bk = nc.dram_tensor("bk", [512], F32, kind="ExternalInput").ap()
    Wv = nc.dram_tensor("Wv", [512, D], F32, kind="ExternalInput").ap()
    bv = nc.dram_tensor("bv", [512], F32, kind="ExternalInput").ap()
    Wc = nc.dram_tensor("Wc", [4096, D], F32, kind="ExternalInput").ap()
    temp = nc.dram_tensor("temp", [1], F32, kind="ExternalInput").ap()
    genes = nc.dram_tensor("genes", [4096, 256], F32, kind="ExternalInput").ap()
    Wd = nc.dram_tensor("Wd", [D, 256], F32, kind="ExternalInput").ap()
    Wu = nc.dram_tensor("Wu", [D, 256], F32, kind="ExternalInput").ap()
    nw = nc.dram_tensor("norm_w", [D], F32, kind="ExternalInput").ap()
    scale = nc.dram_tensor("scale", [1], F32, kind="ExternalInput").ap()
    out_tm = nc.dram_tensor("out", [S, D], F32, kind="ExternalOutput").ap()

    xfm = nc.dram_tensor("xfm_s", [D, S], F32, kind="Internal").ap()
    x1fm = nc.dram_tensor("x1fm_s", [D, S], F32, kind="Internal").ap()
    x2fm = nc.dram_tensor("x2fm_s", [D, S], F32, kind="Internal").ap()
    x3fm = nc.dram_tensor("x3fm_s", [D, S], F32, kind="Internal").ap()
    xn_hbm = nc.dram_tensor("xn_s", [D, S], BF16, kind="Internal").ap()
    attnT = nc.dram_tensor("attnT_s", [D, S], BF16, kind="Internal").ap()
    fuse_hbm = nc.dram_tensor("fuse_s", [8192, S], BF16, kind="Internal").ap()
    w_hbm = nc.dram_tensor("w_s", [S, 4096], BF16, kind="Internal").ap()
    wqkv_b = nc.dram_tensor("wqkv_b", [3 * D, D], BF16, kind="Internal").ap()
    wout_b = nc.dram_tensor("wout_b", [D, D], BF16, kind="Internal").ap()
    wqb = nc.dram_tensor("wqb", [D, D], BF16, kind="Internal").ap()
    wkb = nc.dram_tensor("wkb", [512, D], BF16, kind="Internal").ap()
    wvb = nc.dram_tensor("wvb", [512, D], BF16, kind="Internal").ap()
    wcb = nc.dram_tensor("wcb", [4096, D], BF16, kind="Internal").ap()
    wdb = nc.dram_tensor("wdb", [D, 256], BF16, kind="Internal").ap()
    wub = nc.dram_tensor("wub", [D, 256], BF16, kind="Internal").ap()

    p = Prog(nc)
    cst = declare_consts(nc, p, ALL_CONSTS)
    wgb16 = nc.dram_tensor("wgb16", [D, 8192], BF16, kind="Internal").ap()
    wub16 = nc.dram_tensor("wub16", [D, 8192], BF16, kind="Internal").ap()
    wdb16 = nc.dram_tensor("wdb16", [8192, D], BF16, kind="Internal").ap()

    def wcasts(e, s):
        e.dma_start(out=wqkv_b, in_=wqkv).then_inc(s, 16)
        e.dma_start(out=wout_b, in_=wout).then_inc(s, 16)
        e.dma_start(out=wqb, in_=Wq).then_inc(s, 16)
        e.dma_start(out=wkb, in_=Wk).then_inc(s, 16)
        e.dma_start(out=wvb, in_=Wv).then_inc(s, 16)
        e.dma_start(out=wcb, in_=Wc).then_inc(s, 16)
        e.dma_start(out=wdb, in_=Wd).then_inc(s, 16)
        e.dma_start(out=wub, in_=Wu).then_inc(s, 16)
        e.dma_start(out=wgb16, in_=wg8).then_inc(s, 16)
        e.dma_start(out=wub16, in_=wu8).then_inc(s, 16)
        e.dma_start(out=wdb16, in_=wd8).then_inc(s, 16)
    p.dma("gpsimd", wcasts, n_dma=11, tag="wcasts")
    emit_in_transpose(p, nc, x_in, xfm, cst)
    emit_ln(p, nc, xfm, xn_hbm, cst, "ln0a")
    emit_l0_attn(p, nc, xn_hbm, wqkv_b, attnT, cst)
    emit_fm_proj_residual(p, nc, attnT, wout_b, xfm, x1fm, cst, "o")
    emit_ln(p, nc, x1fm, xn_hbm, cst, "ln0f")
    emit_l0_ffn(p, nc, x1fm, xn_hbm, wgb16, wub16, wdb16, sg, su, sd, fuse_hbm, x2fm, cst)
    emit_ln(p, nc, x2fm, xn_hbm, cst, "ln1a")
    emit_l1_attn(p, nc, x2fm, xn_hbm, wqb, wkb, wvb, bq, bk, bv, x3fm, cst)
    nw_col = p.sbuf("m_nw", [128, NT], F32)
    p.push_scope()
    pst_sc = p.psum("m_scps", [128, 128])
    pst_sc_b = p.buf()
    nw_b = emit_scale_cols(p, nc, nw, NT, "mnw", cst, pst_sc, pst_sc_b, nw_col)
    p.pop_scope()
    p.barrier("m_nw_end")
    emit_ln(p, nc, x3fm, xn_hbm, cst, "ln1f", nw_col=nw_col, nw_col_buf=nw_b)
    emit_l1_ffn(p, nc, x3fm, xn_hbm, wcb, genes, wdb, wub, temp, scale, w_hbm, out_tm, cst)
    p.emit()
    return nc


def kernel_fused(**inputs):
    x = inputs["L0_attn_x"]
    ncores = 8
    cc = const_inputs(ALL_CONSTS)
    ncf = _get("fused", build_fused)
    maps = []
    for c in range(ncores):
        m = {"x": np.ascontiguousarray(x[c]),
             "w_qkv": inputs["L0_attn_w_qkv"], "w_out": inputs["L0_attn_w_out"],
             "w_gate": inputs["L0_ffn_w_gate"], "w_up": inputs["L0_ffn_w_up"],
             "w_down": inputs["L0_ffn_w_down"], "s_gate": inputs["L0_ffn_s_gate"],
             "s_up": inputs["L0_ffn_s_up"], "s_down": inputs["L0_ffn_s_down"],
             "Wq": inputs["L1_attn_Wq"], "bq": inputs["L1_attn_bq"],
             "Wk": inputs["L1_attn_Wk"], "bk": inputs["L1_attn_bk"],
             "Wv": inputs["L1_attn_Wv"], "bv": inputs["L1_attn_bv"],
             "Wc": inputs["L1_ffn_Wc"],
             "temp": np.asarray(inputs["L1_ffn_temperature"], dtype=np.float32).reshape(1),
             "genes": inputs["L1_ffn_genes"], "Wd": inputs["L1_ffn_Wd"],
             "Wu": inputs["L1_ffn_Wu"], "norm_w": inputs["L1_ffn_norm_w"],
             "scale": np.asarray(inputs["L1_ffn_scale"], dtype=np.float32).reshape(1)}
        m.update(cc)
        maps.append(m)
    r = run_bass_kernel_spmd(ncf, maps, core_ids=list(range(ncores)))
    return np.stack([r.results[c]["out"] for c in range(ncores)], axis=0)


_BUILD_CACHE = {}


def _get(name, builder):
    if name not in _BUILD_CACHE:
        _BUILD_CACHE[name] = builder()
    return _BUILD_CACHE[name]


def kernel(**inputs):
    return kernel_fused(**inputs)


def kernel_multilaunch(**inputs):
    x = inputs["L0_attn_x"]
    ncores = 8
    c1 = const_inputs(L1_CONSTS)
    c2 = const_inputs(L2_CONSTS)
    c3 = const_inputs(L3_CONSTS)
    c4 = const_inputs(L4_CONSTS)

    nc1 = _get("l1", build_launch1)
    maps = []
    for c in range(ncores):
        m = {"x": np.ascontiguousarray(x[c]), "w_qkv": inputs["L0_attn_w_qkv"],
             "w_out": inputs["L0_attn_w_out"]}
        m.update(c1)
        maps.append(m)
    r1 = run_bass_kernel_spmd(nc1, maps, core_ids=list(range(ncores)))

    nc2 = _get("l2", build_launch2)
    maps = []
    for c in range(ncores):
        m = {"x1fm": np.ascontiguousarray(r1.results[c]["x1fm"]),
             "w_gate": inputs["L0_ffn_w_gate"], "w_up": inputs["L0_ffn_w_up"],
             "w_down": inputs["L0_ffn_w_down"], "s_gate": inputs["L0_ffn_s_gate"],
             "s_up": inputs["L0_ffn_s_up"], "s_down": inputs["L0_ffn_s_down"]}
        m.update(c2)
        maps.append(m)
    r2 = run_bass_kernel_spmd(nc2, maps, core_ids=list(range(ncores)))

    nc3 = _get("l3", build_launch3)
    maps = []
    for c in range(ncores):
        m = {"x2fm": np.ascontiguousarray(r2.results[c]["x2fm"]),
             "Wq": inputs["L1_attn_Wq"], "bq": inputs["L1_attn_bq"],
             "Wk": inputs["L1_attn_Wk"], "bk": inputs["L1_attn_bk"],
             "Wv": inputs["L1_attn_Wv"], "bv": inputs["L1_attn_bv"]}
        m.update(c3)
        maps.append(m)
    r3 = run_bass_kernel_spmd(nc3, maps, core_ids=list(range(ncores)))

    nc4 = _get("l4", build_launch4)
    maps = []
    for c in range(ncores):
        m = {"x3fm": np.ascontiguousarray(r3.results[c]["x3fm"]),
             "Wc": inputs["L1_ffn_Wc"],
             "temp": np.asarray(inputs["L1_ffn_temperature"], dtype=np.float32).reshape(1),
             "genes": inputs["L1_ffn_genes"], "Wd": inputs["L1_ffn_Wd"],
             "Wu": inputs["L1_ffn_Wu"], "norm_w": inputs["L1_ffn_norm_w"],
             "scale": np.asarray(inputs["L1_ffn_scale"], dtype=np.float32).reshape(1)}
        m.update(c4)
        maps.append(m)
    r4 = run_bass_kernel_spmd(nc4, maps, core_ids=list(range(ncores)))

    return np.stack([r4.results[c]["out"] for c in range(ncores)], axis=0)
